# version 06
# baseline (speedup 1.0000x reference)
from contextlib import ExitStack
from dataclasses import dataclass

import numpy as np
import ml_dtypes

import concourse.bass as bass
import concourse.mybir as mybir
from concourse.bass_utils import run_bass_kernel_spmd
from concourse.alu_op_type import AluOpType as OP
F32 = mybir.dt.float32
BF16 = mybir.dt.bfloat16
AFT = mybir.ActivationFunctionType
AX = mybir.AxisListType

NCORES = 8
P = 128
EPS = 1e-5


@dataclass
class Cfg:
    D: int = 2048
    T: int = 8192
    H: int = 16
    HD: int = 128
    INTER: int = 8192
    QT: int = 256

    @property
    def TLOC(self): return self.T // NCORES
    @property
    def DCH(self): return self.D // P
    @property
    def ICH(self): return self.INTER // P
    @property
    def GT(self): return self.T // 8
    @property
    def NTG(self): return self.GT // P
    @property
    def NTO(self): return self.TLOC // P
    @property
    def NKC(self): return self.T // P
    @property
    def NQT(self): return self.TLOC // self.QT
    @property
    def COGS(self): return [(o, min(512, self.D - o)) for o in range(0, self.D, 512)]
    @property
    def TLB(self): return [(o, min(512, self.TLOC - o))
                           for o in range(0, self.TLOC, 512)]


FULL = Cfg()
MINI = Cfg(D=256, T=2048, H=2, HD=128, INTER=1024, QT=256)

SCL = float(np.float32(128.0 ** -0.25))
TWO_PI = float(np.float64(2 * np.pi))
INV_2PI = float(np.float32(1.0 / (2 * np.pi)))
_c1 = np.float32(6.28125)
_c2 = np.float32(np.float64(TWO_PI) - np.float64(_c1))
_c3 = np.float32(np.float64(TWO_PI) - np.float64(_c1) - np.float64(_c2))
C1, C2, C3 = float(_c1), float(_c2), float(_c3)
PI = float(np.float32(np.pi))
PI_2 = float(np.float32(np.pi / 2))


def ref_freqs(hd):
    lin = np.linspace(np.float32(0.0), np.float32(1.0), hd // 2, dtype=np.float32)
    f = np.float32(40.0) ** lin
    f = f * np.float32(200.0) / np.float32(1000.0)
    return (np.float32(200.0) * f / np.float32(1000.0)).astype(np.float32)


class Sem:
    def __init__(self, h):
        self.h, self.n = h, 0

    def inc(self, k=1):
        self.n += k
        return self.n


class Arena:
    def __init__(self, ap, total):
        self.ap, self.total = ap, total
        self.free = [(0, total)]
        self.live = {}

    def alloc(self, name, units):
        units = (units + 7) & ~7
        best = None
        for i, (off, sz) in enumerate(self.free):
            if sz >= units and (best is None or sz < self.free[best][1]):
                best = i
        if best is None:
            raise RuntimeError(f"arena OOM {name} needs {units} free={self.free} "
                               f"live={ {k: v[1] for k, v in self.live.items()} }")
        off, sz = self.free[best]
        self.free[best] = (off + units, sz - units)
        if self.free[best][1] == 0:
            self.free.pop(best)
        self.live[name] = (off, units)
        return self

    def alloc_high(self, name, units):
        units = (units + 7) & ~7
        for i in range(len(self.free) - 1, -1, -1):
            off, sz = self.free[i]
            if sz >= units:
                self.free[i] = (off, sz - units)
                if self.free[i][1] == 0:
                    self.free.pop(i)
                self.live[name] = (off + sz - units, units)
                return self
        raise RuntimeError(f"arena OOM(high) {name} needs {units}")

    def release(self, *names):
        for name in names:
            off, sz = self.live.pop(name)
            self.free.append((off, sz))
        self.free.sort()
        m = []
        for o, s in self.free:
            if m and m[-1][0] + m[-1][1] == o:
                m[-1][1] += s
            else:
                m.append([o, s])
        self.free = [tuple(x) for x in m]

    def f(self, name, cols=None):
        off, sz = self.live[name]
        return self.ap[:, off:off + (sz if cols is None else cols)]

    def b(self, name, cols=None):
        off, sz = self.live[name]
        v = self.ap[:, off:off + sz].bitcast(BF16)
        return v if cols is None else v[:, 0:cols]


def build(cfg: Cfg, dbg: bool = False):
    D, T, H, HD, INTER, QT = cfg.D, cfg.T, cfg.H, cfg.HD, cfg.INTER, cfg.QT
    DCH, ICH, GT, NTG, NTO = cfg.DCH, cfg.ICH, cfg.GT, cfg.NTG, cfg.NTO
    NKC, NQT, TLOC = cfg.NKC, cfg.NQT, cfg.TLOC
    COGS, TLB = cfg.COGS, cfg.TLB
    NCG, NTB = len(COGS), len(TLB)
    NGRP = T // GT

    nc = bass.Bass("TRN2", target_bir_lowering=False)

    x_d = nc.dram_tensor("x_full", [T, D], F32, kind="ExternalInput").ap()
    xow_d = nc.dram_tensor("x_own", [TLOC, D], F32, kind="ExternalInput").ap()
    wq_d = nc.dram_tensor("Wq", [D, D], F32, kind="ExternalInput").ap()
    wkv_d = nc.dram_tensor("Wkv", [D, 2 * D], F32, kind="ExternalInput").ap()
    wo_d = nc.dram_tensor("Wo", [D, D], F32, kind="ExternalInput").ap()
    wgu_d = nc.dram_tensor("Wgu", [2 * INTER, D], F32, kind="ExternalInput").ap()
    wdn_d = nc.dram_tensor("Wdn", [D, INTER], F32, kind="ExternalInput").ap()
    V = {}
    for nm, n in [("ln_q_g", D), ("ln_q_b", D), ("ln_kv_g", D), ("ln_kv_b", D),
                  ("ln_h_g", HD), ("ln_h_b", HD), ("bq", D), ("bkv", 2 * D),
                  ("bo", D)]:
        V[nm] = nc.dram_tensor(nm, [P, n], F32, kind="ExternalInput").ap()
    ident_d = nc.dram_tensor("ident_c", [P, P], BF16, kind="ExternalInput").ap()
    ones_d = nc.dram_tensor("ones_c", [P, 1], BF16, kind="ExternalInput").ap()
    onesf_d = nc.dram_tensor("onesf_c", [1, P], F32, kind="ExternalInput").ap()
    fdup_d = nc.dram_tensor("fdup_c", [P, HD], F32, kind="ExternalInput").ap()
    tcols_d = nc.dram_tensor("tcols_c", [P, T // P], F32, kind="ExternalInput").ap()
    tco_d = nc.dram_tensor("tcown_c", [P, NTO], F32, kind="ExternalInput").ap()
    kq_d = nc.dram_tensor("kq_c", [P, (T // P) * HD], F32,
                          kind="ExternalInput").ap()
    kq2_d = nc.dram_tensor("kq2_c", [P, (T // P) * HD], F32,
                           kind="ExternalInput").ap()
    kqo_d = nc.dram_tensor("kqo_c", [P, NTO * HD], F32,
                           kind="ExternalInput").ap()
    kqo2_d = nc.dram_tensor("kqo2_c", [P, NTO * HD], F32,
                            kind="ExternalInput").ap()
    y_d = nc.dram_tensor("y", [TLOC, D], F32, kind="ExternalOutput").ap()

    wgu_bf = nc.dram_tensor("wgu_bf", [2 * INTER, D], BF16).ap()
    wdn_bf = nc.dram_tensor("wdn_bf", [D, INTER], BF16).ap()
    wguT = nc.dram_tensor("wguT", [2 * ICH * D, P], BF16).ap()
    wdnT = nc.dram_tensor("wdnT", [ICH * P, D], BF16).ap()
    st_kn = nc.dram_tensor("st_kn", [NGRP * D, GT], BF16).ap()
    st_v = nc.dram_tensor("st_v", [H * NKC * P, HD], BF16).ap()
    qn_sp = nc.dram_tensor("qn_sp", [D, TLOC], BF16).ap()
    xn_sp = nc.dram_tensor("xn_sp", [D, TLOC], BF16).ap()
    ffn_sp = nc.dram_tensor("ffn_sp", [TLOC, D], F32).ap()

    ctx = ExitStack()
    AR_U = 49280
    arena_t = ctx.enter_context(nc.sbuf_tensor("arena", [P, AR_U], F32))
    A = Arena(arena_t[:], AR_U)
    small_t = ctx.enter_context(nc.sbuf_tensor("small", [P, 1150], F32))
    SM = small_t[:]
    psum_t = ctx.enter_context(nc.psum_tensor("psum", [P, 4096], F32))
    PS, PSB = psum_t[:], psum_t[:].bitcast(BF16)
    ident_t = ctx.enter_context(nc.sbuf_tensor("identS", [P, P], BF16))
    IDENT = ident_t[:]
    ones_t = ctx.enter_context(nc.sbuf_tensor("onesS", [P, 1], BF16))
    ONES = ones_t[:]
    onesf_t = ctx.enter_context(nc.sbuf_tensor("onesF", [1, P], F32))
    ONESF = onesf_t[:]
    trt_t = ctx.enter_context(nc.sbuf_tensor("trtS", [P, 2048], BF16,
                                             is_transpose_dest=True))
    TRT = trt_t[:]

    def bank(b, w=512, o=0):
        return PS[:, b * 512 + o: b * 512 + o + w]

    def bankb(b, w=1024, o=0):
        return PSB[:, b * 1024 + o: b * 1024 + o + w]

    sems = {n: Sem(ctx.enter_context(nc.semaphore(n))) for n in
            ("sx", "stw", "srd", "swgp", "swff", "ste", "sact", "sdve", "sgp",
             "swcast", "swwq", "swkv", "swwo", "sptw", "skq")}
    sx, stw, srd, swgp, swff = (sems[n] for n in ("sx", "stw", "srd", "swgp",
                                                  "swff"))
    s_te, s_act, s_dve, s_gp = (sems[n] for n in ("ste", "sact", "sdve", "sgp"))
    swcast, swwq, swkv, swwo, sptw, skq = (sems[n] for n in
                                            ("swcast", "swwq", "swkv", "swwo",
                                             "sptw", "skq"))

    Q = {k: [] for k in ("te", "act", "dve", "sp", "gp")}

    def emit(q, meth, *args, waits=(), inc=None, inc16=None, **kw):
        r = None
        incs = []
        if inc is not None:
            r = inc.inc()
            incs.append((inc.h, 1))
        if inc16 is not None:
            r = inc16.inc(16)
            incs.append((inc16.h, 16))
        ws = [(s.h, v) for (s, v) in waits if s is not None and v and v > 0]
        Q[q].append((meth, args, kw, ws, incs))
        return r

    def drain(q):
        emit(q, "drain")

    def dma(q, sem, dst, src, waits=()):
        return emit(q, "dma_start", waits=waits, inc16=sem, out=dst, in_=src)

    def dma_tr(q, sem, dst, src, waits=()):
        return emit(q, "dma_start_transpose", dst, src, waits=waits, inc16=sem)

    bg = [None] * 8

    def bgw(b):
        return [] if bg[b] is None else [bg[b]]

    _smo = [0]

    def sm(n):
        o = _smo[0]
        _smo[0] += max(n, 1)
        assert _smo[0] <= 1150, _smo[0]
        return SM[:, o:o + n]

    FDUP = sm(HD)
    TCOLS = sm(T // P)
    TCO = sm(NTO)
    ST = sm(16)
    HL = sm(6 * H)
    MSV = sm(NTG)
    NMSV = sm(NTG)
    QMAX = sm(H)
    KMAX = sm(H)
    RED1 = sm(2)
    BBFr = sm(8)
    NEGB = sm(1)
    NEGPI = sm(1)
    LROW = sm(QT)
    ANG = sm(HD)
    WRK = sm(HD)
    WRK2 = sm(HD)

    BBF = BBFr.bitcast(BF16)

    def hl(i):
        return HL[:, i * H:(i + 1) * H]

    dma("sp", sx, IDENT, ident_d[:, :])
    dma("sp", sx, ONES, ones_d[:, :])
    dma("sp", sx, ONESF, onesf_d[:, :])
    dma("sp", sx, FDUP, fdup_d[:, :])
    dma("sp", sx, TCOLS, tcols_d[:, :])
    dma("sp", sx, TCO, tco_d[:, :])
    emit("dve", "memset", NEGPI, -PI)
    drain("dve")

    A.alloc("vrow", 2 * D)
    vrow = A.f("vrow")
    bc_specs = [("gq_b", V["ln_q_g"][:, :], D, None),
                ("gkv_b", V["ln_kv_g"][:, :], D, None),
                ("bq2_b", V["ln_q_b"][:, :], D, None),
                ("bkv2_b", V["ln_kv_b"][:, :], D, None),
                ("bqs_b", V["bq"][:, :], D, SCL),
                ("bkvk_s", V["bkv"][:, 0:D], D, SCL),
                ("bkvv_b", V["bkv"][:, D:2 * D], D, None),
                ("lnhg_e", V["ln_h_g"][:, :], HD, SCL),
                ("lnhb_e", V["ln_h_b"][:, :], HD, SCL),
                ("bo_b", V["bo"][:, :], D, None)]
    last_bc = 0
    for nm, srcv, n, scale in bc_specs:
        A.alloc(nm, (n + 1) // 2)
        vv = dma("gp", swgp, vrow[:, 0:n], srcv, waits=[(s_dve, last_bc)])
        if scale is None:
            last_bc = emit("dve", "tensor_copy", A.b(nm, n), vrow[:, 0:n],
                           waits=[(swgp, vv)], inc=s_dve)
        else:
            last_bc = emit("dve", "tensor_scalar_mul", A.b(nm, n), vrow[:, 0:n],
                           float(scale), waits=[(swgp, vv)], inc=s_dve)
        drain("dve")
    bc_done = last_bc
    A.release("vrow")

    ch = max(P, 2 * INTER // 8)
    for i in range(0, 2 * INTER, ch):
        dma("gp", swcast, wgu_bf[i:i + ch, :], wgu_d[i:i + ch, :])
    ch2 = max(P, D // 4)
    for i in range(0, D, ch2):
        dma("gp", swcast, wdn_bf[i:i + ch2, :], wdn_d[i:i + ch2, :])
    ffn_cast_done = swcast.n


    GU_ROWS = min(1024, 2 * INTER)
    DN_ROWS = min(1024, D)
    _ptr_tiles = []
    for j0 in range(0, 2 * INTER, GU_ROWS):
        for ki in range(DCH):
            _ptr_tiles.append(("gu", j0, ki))
    for h0 in range(0, D, DN_ROWS):
        for ic in range(ICH):
            _ptr_tiles.append(("dn", h0, ic))

    def ptr_slice(idx, total):
        n = len(_ptr_tiles)
        lo, hi = idx * n // total, (idx + 1) * n // total
        for ti in range(lo, hi):
            kind, a0, cc = _ptr_tiles[ti]
            buf_all = TRT
            rows = GU_ROWS if kind == "gu" else DN_ROWS
            buf = buf_all[:, (ti % 2) * 1024:(ti % 2) * 1024 + rows]
            w = [(swcast, ffn_cast_done)] if ti in (lo, lo + 1) else []
            w.append((sptw, sptw.n))
            if kind == "gu":
                src = wgu_bf[a0:a0 + rows, cc * P:(cc + 1) * P]
                tv = dma_tr("sp", swff, buf, src, waits=w)
                nblk = rows // P
                dst = bass.AP(wguT.tensor, ((a0 // P) * D + cc * P) * P,
                              [[P, P], [D * P, nblk], [1, P]])
                s3 = buf.rearrange("p (j c) -> p j c", j=nblk)
                dma("sp", sptw, dst, s3, waits=[(swff, tv)])
            else:
                src = wdn_bf[a0:a0 + rows, cc * P:(cc + 1) * P]
                tv = dma_tr("sp", swff, buf, src, waits=w)
                dma("sp", sptw, wdnT[cc * P:(cc + 1) * P, a0:a0 + rows], buf,
                    waits=[(swff, tv)])

    def rsqrt_ref(dst, src, t0, t1):
        av = emit("act", "activation", t0, src, AFT.Sqrt, inc=s_act)
        emit("dve", "reciprocal", dst, t0, waits=[(s_act, av)])
        drain("dve")
        emit("dve", "tensor_tensor", t0, dst, dst, OP.mult)
        drain("dve")
        emit("dve", "tensor_tensor", t0, t0, src, OP.mult)
        drain("dve")
        emit("dve", "tensor_scalar", t1, t0, -0.5, 1.5, OP.mult, OP.add)
        drain("dve")
        emit("dve", "tensor_tensor", dst, dst, t1, OP.mult)
        drain("dve")

    def ln_stats(xt, sq):
        emit("dve", "reduce_sum", ST[:, 0:1], xt, axis=AX.X)
        av = emit("act", "activation", sq, xt, AFT.Square, inc=s_act)
        emit("dve", "reduce_sum", ST[:, 1:2], sq, axis=AX.X, waits=[(s_act, av)])
        drain("dve")
        emit("dve", "tensor_scalar", ST[:, 2:3], ST[:, 0:1], 1.0 / D, None, OP.mult)
        emit("dve", "tensor_scalar", ST[:, 3:4], ST[:, 1:2], 1.0 / D, None, OP.mult)
        drain("dve")
        emit("dve", "tensor_tensor", ST[:, 11:12], ST[:, 2:3], ST[:, 2:3], OP.mult)
        drain("dve")
        emit("dve", "tensor_tensor", ST[:, 3:4], ST[:, 3:4], ST[:, 11:12],
             OP.subtract)
        drain("dve")
        emit("dve", "tensor_scalar", ST[:, 4:5], ST[:, 3:4], EPS, None, OP.add)
        drain("dve")
        dv = emit("dve", "nop", inc=s_dve)
        emit("act", "nop", waits=[(s_dve, dv)])
        rsqrt_ref(ST[:, 5:6], ST[:, 4:5], ST[:, 12:13], ST[:, 13:14])
        emit("dve", "reciprocal", ST[:, 12:13], ST[:, 4:5])
        drain("dve")
        emit("dve", "tensor_tensor", ST[:, 6:7], ST[:, 3:4], ST[:, 12:13], OP.mult)
        drain("dve")
        emit("dve", "tensor_scalar", ST[:, 7:8], ST[:, 6:7], float(D), None, OP.mult)
        emit("dve", "tensor_scalar", ST[:, 14:15], ST[:, 6:7], EPS, None, OP.add)
        drain("dve")
        dv = emit("dve", "nop", inc=s_dve)
        av = emit("act", "activation", ST[:, 8:9], ST[:, 7:8], AFT.Sqrt,
                  waits=[(s_dve, dv)], inc=s_act)
        emit("dve", "tensor_scalar", ST[:, 9:10], ST[:, 8:9], -1.0, None, OP.mult,
             waits=[(s_act, av)])
        drain("dve")
        rsqrt_ref(ST[:, 10:11], ST[:, 14:15], ST[:, 12:13], ST[:, 13:14])
        emit("dve", "tensor_tensor", ST[:, 15:16], ST[:, 5:6], ST[:, 10:11], OP.mult)
        drain("dve")

    kq_guard = [0]

    def rot_tables(kqsrc1, kqsrc2, tcol, rr, ri):
        kq = A.f("kq", 2 * HD)
        kv1 = dma("sp", skq, kq[:, 0:HD], kqsrc1,
                  waits=[(s_dve, kq_guard[0])])
        kv2 = dma("sp", skq, kq[:, HD:2 * HD], kqsrc2)
        emit("dve", "tensor_scalar", ANG, FDUP, tcol, None, OP.mult)
        drain("dve")

        def wrap(dst, shift, kk, kwait):
            emit("dve", "scalar_tensor_tensor", dst, kk, -C1, ANG, OP.mult,
                 OP.add, waits=[(skq, kv2)])
            drain("dve")
            if shift != 0.0:
                emit("dve", "tensor_scalar", dst, dst, shift, None, OP.add)
                drain("dve")
            emit("dve", "scalar_tensor_tensor", dst, kk, -C2, dst, OP.mult, OP.add)
            drain("dve")
            emit("dve", "scalar_tensor_tensor", dst, kk, -C3, dst, OP.mult, OP.add)
            drain("dve")
            emit("dve", "tensor_scalar", dst, dst, 0.0, TWO_PI, OP.max, OP.min)
            drain("dve")
            dv = emit("dve", "nop", inc=s_dve)
            return emit("act", "activation", dst, dst, AFT.Sin, bias=NEGPI,
                        scale=1.0, waits=[(s_dve, dv)], inc=s_act)

        a1 = wrap(ri, 0.0, kq[:, 0:HD], kv1)
        a2 = wrap(rr, PI_2, kq[:, HD:2 * HD], kv2)
        emit("dve", "tensor_scalar", rr, rr, ST[:, 9:10], None, OP.mult,
             waits=[(s_act, a2)])
        emit("dve", "tensor_scalar", ri, ri, ST[:, 9:10], None, OP.mult,
             waits=[(s_act, a1)])
        drain("dve")
        kq_guard[0] = emit("dve", "nop", inc=s_dve)

    def rot_headln(yt, sq, tmp, out_bf, rr, ri, acc, first):
        y4 = yt.rearrange("p (h j two) -> p h j two", h=H, two=2)
        t4 = tmp.rearrange("p (h j two) -> p h j two", h=H, two=2)
        s4 = sq.rearrange("p (h j two) -> p h j two", h=H, two=2)
        rr4 = rr.unsqueeze(1).broadcast_to([P, H, HD]).rearrange(
            "p h (j two) -> p h j two", two=2)
        ri4 = ri.unsqueeze(1).broadcast_to([P, H, HD]).rearrange(
            "p h (j two) -> p h j two", two=2)
        ye, yo = y4[:, :, :, 0:1], y4[:, :, :, 1:2]
        emit("dve", "tensor_tensor", t4[:, :, :, 0:1], ye, rr4[:, :, :, 0:1],
             OP.mult)
        emit("dve", "tensor_tensor", t4[:, :, :, 1:2], ye, ri4[:, :, :, 0:1],
             OP.mult)
        emit("dve", "tensor_tensor", s4[:, :, :, 0:1], yo, ri4[:, :, :, 1:2],
             OP.mult)
        emit("dve", "tensor_tensor", s4[:, :, :, 1:2], yo, rr4[:, :, :, 1:2],
             OP.mult)
        drain("dve")
        emit("dve", "tensor_tensor", t4[:, :, :, 0:1], t4[:, :, :, 0:1],
             s4[:, :, :, 0:1], OP.subtract)
        emit("dve", "tensor_tensor", t4[:, :, :, 1:2], t4[:, :, :, 1:2],
             s4[:, :, :, 1:2], OP.add)
        drain("dve")
        dv = emit("dve", "nop", inc=s_dve)
        av = emit("act", "activation", sq, tmp, AFT.Square, waits=[(s_dve, dv)],
                  inc=s_act)
        t3 = tmp.rearrange("p (h d) -> p h d", h=H)
        s3 = sq.rearrange("p (h d) -> p h d", h=H)
        emit("dve", "reduce_sum", hl(0).unsqueeze(2), t3, axis=AX.X)
        emit("dve", "reduce_sum", hl(1).unsqueeze(2), s3, axis=AX.X,
             waits=[(s_act, av)])
        drain("dve")
        emit("dve", "tensor_scalar", hl(2), hl(0), 1.0 / HD, None, OP.mult)
        emit("dve", "tensor_scalar", hl(5), hl(1), 1.0 / HD, None, OP.mult)
        drain("dve")
        emit("dve", "tensor_tensor", hl(1), hl(2), hl(2), OP.mult)
        drain("dve")
        emit("dve", "tensor_tensor", hl(5), hl(5), hl(1), OP.subtract)
        drain("dve")
        emit("dve", "tensor_scalar", hl(5), hl(5), EPS, None, OP.add)
        drain("dve")
        dv = emit("dve", "nop", inc=s_dve)
        emit("act", "nop", waits=[(s_dve, dv)])
        rsqrt_ref(hl(3), hl(5), hl(0), hl(1))
        emit("dve", "tensor_tensor", hl(4), hl(2), hl(3), OP.mult)
        drain("dve")
        emit("dve", "tensor_scalar", hl(4), hl(4), -1.0, None, OP.mult)
        drain("dve")
        dv = emit("dve", "nop", inc=s_dve)
        emit("act", "nop", waits=[(s_dve, dv)])
        for h in range(H):
            av = emit("act", "activation", sq[:, h * HD:(h + 1) * HD],
                      tmp[:, h * HD:(h + 1) * HD], AFT.Identity,
                      bias=hl(4)[:, h:h + 1], scale=hl(3)[:, h:h + 1],
                      inc=(s_act if h == H - 1 else None))
        ge = A.b("lnhg_e", HD).unsqueeze(1).broadcast_to([P, H, HD])
        be = A.b("lnhb_e", HD).unsqueeze(1).broadcast_to([P, H, HD])
        emit("dve", "tensor_tensor", t3, s3, ge, OP.mult, waits=[(s_act, av)])
        drain("dve")
        ob3 = out_bf.rearrange("p (h d) -> p h d", h=H)
        emit("dve", "tensor_tensor", ob3, t3, be, OP.add)
        drain("dve")
        dv = emit("dve", "nop", inc=s_dve)
        av = emit("act", "activation", sq, out_bf, AFT.Square,
                  waits=[(s_dve, dv)], inc=s_act)
        emit("dve", "reduce_sum", hl(5).unsqueeze(2), s3, axis=AX.X,
             waits=[(s_act, av)])
        drain("dve")
        if first:
            emit("dve", "tensor_copy", acc, hl(5))
        else:
            emit("dve", "tensor_tensor", acc, acc, hl(5), OP.max)
        drain("dve")
        return emit("dve", "nop", inc=s_dve)

    def tm_to_fm(src_bf, dst_fn, nch, te_waits=()):
        last = None
        for k0 in range(0, nch, 4):
            kn = min(4, nch - k0)
            pb = 4 + (k0 // 4) % 2
            tv = None
            for kk in range(kn):
                w = list(te_waits) if (k0 == 0 and kk == 0) else []
                if kk == 0:
                    w += bgw(pb)
                tv = emit("te", "transpose", bankb(pb, P, o=kk * P),
                          src_bf[:, (k0 + kk) * P:(k0 + kk + 1) * P], IDENT,
                          waits=w, inc=(s_te if kk == kn - 1 else None))
            for kk in range(kn):
                emit("dve", "tensor_copy", dst_fn(k0 + kk),
                     bankb(pb, P, o=kk * P), waits=[(s_te, tv)] if kk == 0 else [])
            drain("dve")
            last = emit("dve", "nop", inc=s_dve)
            bg[pb] = (s_dve, last)
        return last

    A.alloc("wbig", DCH * D // 2)
    WQ = A.b("wbig", DCH * D)
    srcw = wq_d[:, :].rearrange("(ki p) n -> p ki n", p=P)
    dstw = WQ.rearrange("p (ki n) -> p ki n", ki=DCH)
    wq_loaded = 0
    for kk in range(0, DCH, 4):
        kn = min(4, DCH - kk)
        wq_loaded = dma("gp", swwq, dstw[:, kk:kk + kn], srcw[:, kk:kk + kn],
                        waits=[(s_dve, bc_done)] if kk == 0 else [])

    for nm, u in [("xpipe", D), ("sq", D), ("xnn", D), ("xn_f", D),
                  ("tile_bf", D // 2), ("fm_bf", DCH * P // 2), ("qtm", D),
                  ("rrg", NTG * HD), ("rig", NTG * HD)]:
        A.alloc(nm, u)
    A.alloc_high("kq", 2 * HD)
    xp = A.f("xpipe", D)
    sq_s = A.f("sq", D)
    xnn_s = A.f("xnn", D)
    xnf_s = A.f("xn_f", D)
    tile_b = A.b("tile_bf", D)
    fm_b = A.b("fm_bf", DCH * P)
    fm3 = fm_b.rearrange("p (ki c) -> p ki c", ki=DCH)
    qtm = A.f("qtm", D)
    RRG = A.f("rrg", NTG * HD)
    RIG = A.f("rig", NTG * HD)

    x_guard = 0
    for t in range(NTO):
        xg_prev = x_guard if t > 0 else bc_done
        xv = dma("sp", sx, xp, xow_d[t * P:(t + 1) * P, :],
                 waits=[(s_dve, xg_prev)])
        emit("dve", "nop", waits=[(sx, xv)])
        emit("act", "nop", waits=[(sx, xv), (s_dve, xg_prev)])
        ln_stats(xp, sq_s)
        emit("dve", "tensor_scalar", xnf_s, xp, ST[:, 2:3], ST[:, 5:6],
             OP.subtract, OP.mult)
        drain("dve")
        x_guard = emit("dve", "nop", inc=s_dve)
        emit("dve", "tensor_scalar", xnn_s, xnf_s, ST[:, 10:11], None, OP.mult)
        drain("dve")
        emit("dve", "tensor_copy", tile_b, xnf_s)
        drain("dve")
        dv = emit("dve", "nop", inc=s_dve)
        fmk = tm_to_fm(tile_b, lambda k: fm_b[:, k * P:(k + 1) * P], DCH,
                       te_waits=[(s_dve, dv)])
        dstx = bass.AP(xn_sp.tensor, t * P,
                       [[TLOC, P], [P * TLOC, DCH], [1, P]])
        wv = dma("sp", stw, dstx, fm3, waits=[(s_dve, fmk)])
        emit("dve", "tensor_tensor", sq_s, xnn_s, A.b("gq_b", D), OP.mult)
        drain("dve")
        emit("dve", "tensor_tensor", tile_b, sq_s, A.b("bq2_b", D), OP.add,
             waits=[(stw, wv)])
        drain("dve")
        dv = emit("dve", "nop", inc=s_dve)
        tm_to_fm(tile_b, lambda k: fm_b[:, k * P:(k + 1) * P], DCH,
                 te_waits=[(s_dve, dv)])
        aq_done = emit("dve", "nop", inc=s_dve)
        emit("te", "nop", waits=[(swwq, wq_loaded), (s_dve, aq_done)])
        for ki in range(DCH):
            emit("te", "ldweights", fm_b[:, ki * P:(ki + 1) * P])
            for ci, (co, cw) in enumerate(COGS):
                w = bgw(ci) if ki == 0 else []
                tv = emit("te", "matmul", bank(ci, cw),
                          fm_b[:, ki * P:(ki + 1) * P],
                          WQ[:, ki * D + co: ki * D + co + cw],
                          start=(ki == 0), stop=(ki == DCH - 1), waits=w,
                          inc=(s_te if (ki == DCH - 1 and ci == NCG - 1)
                               else None))
        for ci, (co, cw) in enumerate(COGS):
            emit("dve", "scalar_tensor_tensor", qtm[:, co:co + cw], bank(ci, cw),
                 SCL, A.b("bqs_b", D)[:, co:co + cw], OP.mult, OP.add,
                 waits=[(s_te, tv)] if ci == 0 else [])
        drain("dve")
        dv = emit("dve", "nop", inc=s_dve)
        for ci in range(NCG):
            bg[ci] = (s_dve, dv)
        rot_tables(kqo_d[:, t * HD:(t + 1) * HD],
                   kqo2_d[:, t * HD:(t + 1) * HD],
                   TCO[:, t:t + 1], RRG[:, 0:HD], RIG[:, 0:HD])
        rot_headln(qtm, sq_s, xnn_s, tile_b, RRG[:, 0:HD], RIG[:, 0:HD],
                   QMAX, t == 0)
        dv = emit("dve", "nop", inc=s_dve)
        fmk = tm_to_fm(tile_b, lambda k: fm_b[:, k * P:(k + 1) * P], DCH,
                       te_waits=[(s_dve, dv)])
        dstq = bass.AP(qn_sp.tensor, t * P,
                       [[TLOC, P], [P * TLOC, DCH], [1, P]])
        qv = dma("sp", stw, dstq, fm3, waits=[(s_dve, fmk)])
        x_guard = emit("dve", "nop", waits=[(stw, qv)], inc=s_dve)

    A.release("wbig", "qtm", "gq_b", "bq2_b", "bqs_b")

    q1_te_done = s_te.n
    A.alloc("wkvh", DCH * D // 2)
    WKVH = A.b("wkvh", DCH * D)
    dstwh = WKVH.rearrange("p (ki n) -> p ki n", ki=DCH)
    A.alloc("akv_fm", DCH * GT // 2)
    akvb = A.b("akv_fm", DCH * GT)
    A.alloc("ktm2", D)
    ktm = A.f("ktm2", D)
    n_ptr = NGRP + 1
    ptr_slice(0, n_ptr)

    wkv_half_loaded = [0, 0]

    def load_wkv_half(half, wguard):
        src = wkv_d[:, half * D:(half + 1) * D].rearrange("(ki p) n -> p ki n",
                                                          p=P)
        v = 0
        for kk in range(DCH):
            w = [wguard] if (kk == 0 and wguard is not None) else []
            v = dma("gp", swkv, dstwh[:, kk:kk + 1], src[:, kk:kk + 1], waits=w)
        return v

    x_guard2 = 0
    sq_gate = x_guard
    for g in range(NGRP):
        for t in range(NTG):
            gt = g * NTG + t
            xg_prev = sq_gate
            xv = dma("sp", sx, xp, x_d[gt * P:(gt + 1) * P, :],
                     waits=[(s_dve, xg_prev)])
            emit("dve", "nop", waits=[(sx, xv)])
            emit("act", "nop", waits=[(sx, xv), (s_dve, xg_prev)])
            ln_stats(xp, sq_s)
            emit("dve", "tensor_scalar", xnn_s, xp, ST[:, 2:3], ST[:, 15:16],
                 OP.subtract, OP.mult)
            drain("dve")
            x_guard2 = emit("dve", "nop", inc=s_dve)
            emit("dve", "tensor_copy", MSV[:, t:t + 1], ST[:, 8:9])
            emit("dve", "tensor_copy", NMSV[:, t:t + 1], ST[:, 9:10])
            drain("dve")
            rot_tables(kq_d[:, gt * HD:(gt + 1) * HD],
                       kq2_d[:, gt * HD:(gt + 1) * HD],
                       TCOLS[:, gt:gt + 1], RRG[:, t * HD:(t + 1) * HD],
                       RIG[:, t * HD:(t + 1) * HD])
            emit("dve", "tensor_tensor", sq_s, xnn_s, A.b("gkv_b", D), OP.mult)
            drain("dve")
            emit("dve", "tensor_tensor", tile_b, sq_s, A.b("bkv2_b", D), OP.add)
            drain("dve")
            dv = emit("dve", "nop", inc=s_dve)
            sq_gate = tm_to_fm(tile_b,
                     lambda k, _t=t: akvb[:, k * GT + _t * P:k * GT + (_t + 1) * P],
                     DCH, te_waits=[(s_dve, dv)])
        akv_ready = emit("dve", "nop", inc=s_dve)
        kguard = (s_te, s_te.n) if g > 0 else (s_te, q1_te_done)
        wkv_half_loaded[0] = load_wkv_half(0, kguard)
        emit("te", "nop", waits=[(swkv, wkv_half_loaded[0]),
                                 (s_dve, akv_ready)])
        for t in range(NTG):
            for ki in range(DCH):
                emit("te", "ldweights",
                     akvb[:, ki * GT + t * P: ki * GT + (t + 1) * P])
                for ci, (co, cw) in enumerate(COGS):
                    w = bgw(ci) if ki == 0 else []
                    tv = emit("te", "matmul", bank(ci, cw),
                              akvb[:, ki * GT + t * P: ki * GT + (t + 1) * P],
                              WKVH[:, ki * D + co: ki * D + co + cw],
                              start=(ki == 0), stop=(ki == DCH - 1), waits=w,
                              inc=(s_te if (ki == DCH - 1 and ci == NCG - 1)
                                   else None))
            for ci, (co, cw) in enumerate(COGS):
                emit("dve", "scalar_tensor_tensor", ktm[:, co:co + cw],
                     bank(ci, cw), SCL, A.b("bkvk_s", D)[:, co:co + cw],
                     OP.mult, OP.add, waits=[(s_te, tv)] if ci == 0 else [])
            drain("dve")
            dv = emit("dve", "nop", inc=s_dve)
            for ci in range(NCG):
                bg[ci] = (s_dve, dv)
            emit("dve", "tensor_copy", ST[:, 9:10], NMSV[:, t:t + 1])
            drain("dve")
            rot_headln(ktm, sq_s, xnn_s, tile_b, RRG[:, t * HD:(t + 1) * HD],
                       RIG[:, t * HD:(t + 1) * HD], KMAX, g == 0 and t == 0)
            dv = emit("dve", "nop", inc=s_dve)
            fmk = tm_to_fm(tile_b, lambda k: fm_b[:, k * P:(k + 1) * P], DCH,
                           te_waits=[(s_dve, dv)])
            dstk = bass.AP(st_kn.tensor, g * D * GT + t * P,
                           [[GT, P], [P * GT, DCH], [1, P]])
            kv = dma("sp", stw, dstk, fm3, waits=[(s_dve, fmk)])
            emit("dve", "nop", waits=[(stw, kv)])
        wkv_half_loaded[1] = load_wkv_half(1, (s_te, s_te.n))
        emit("te", "nop", waits=[(swkv, wkv_half_loaded[1])])
        for t in range(NTG):
            gt = g * NTG + t
            for ki in range(DCH):
                emit("te", "ldweights",
                     akvb[:, ki * GT + t * P: ki * GT + (t + 1) * P])
                for ci, (co, cw) in enumerate(COGS):
                    w = bgw(ci) if ki == 0 else []
                    tv = emit("te", "matmul", bank(ci, cw),
                              akvb[:, ki * GT + t * P: ki * GT + (t + 1) * P],
                              WKVH[:, ki * D + co: ki * D + co + cw],
                              start=(ki == 0), stop=(ki == DCH - 1), waits=w,
                              inc=(s_te if (ki == DCH - 1 and ci == NCG - 1)
                                   else None))
            for ci, (co, cw) in enumerate(COGS):
                emit("dve", "scalar_tensor_tensor", tile_b[:, co:co + cw],
                     bank(ci, cw), 1.0, A.b("bkvv_b", D)[:, co:co + cw],
                     OP.mult, OP.add, waits=[(s_te, tv)] if ci == 0 else [])
            drain("dve")
            dv = emit("dve", "nop", inc=s_dve)
            for ci in range(NCG):
                bg[ci] = (s_dve, dv)
            dstv = bass.AP(st_v.tensor, gt * P * HD,
                           [[HD, P], [NKC * P * HD, H], [1, HD]])
            t3v = tile_b.rearrange("p (h c) -> p h c", h=H)
            vv = dma("sp", stw, dstv, t3v, waits=[(s_dve, dv)])
            x_guard2 = emit("dve", "nop", waits=[(stw, vv)], inc=s_dve)
            sq_gate = x_guard2
        ptr_slice(g + 1, n_ptr)

    A.release("wkvh", "akv_fm", "ktm2", "xpipe", "sq", "xnn", "xn_f",
              "tile_bf", "fm_bf", "rrg", "rig", "kq",
              "gkv_b", "bkv2_b", "bkvk_s", "bkvv_b", "lnhg_e", "lnhb_e")
    kv_all_stored = stw.n

    emit("dve", "reduce_max", RED1[:, 0:1], QMAX, axis=AX.X)
    emit("dve", "reduce_max", RED1[:, 1:2], KMAX, axis=AX.X)
    drain("dve")
    emit("dve", "tensor_copy", BBF[:, 0:2], RED1[:, 0:2])
    drain("dve")
    dv = emit("dve", "nop", inc=s_dve)
    tv = emit("te", "transpose", bankb(6, P)[0:2, 0:P], BBF[:, 0:2], IDENT,
              waits=[(s_dve, dv)] + bgw(6), inc=s_te)
    emit("dve", "reduce_max", BBF[0:2, 4:5], bankb(6, P)[0:2, 0:P], axis=AX.X,
         waits=[(s_te, tv)])
    drain("dve")
    dv = emit("dve", "nop", inc=s_dve)
    tv = emit("te", "transpose", bankb(6, P, o=512)[0:1, 0:P], BBF[:, 4:5],
              IDENT, waits=[(s_dve, dv)] + bgw(7), inc=s_te)
    emit("dve", "tensor_copy", BBF[0:1, 6:8], bankb(6, P, o=512)[0:1, 0:2],
         waits=[(s_te, tv)])
    drain("dve")
    emit("dve", "tensor_tensor", BBF[0:1, 8:9], BBF[0:1, 6:7], BBF[0:1, 7:8],
         OP.mult)
    drain("dve")
    dv = emit("dve", "nop", inc=s_dve)
    av = emit("act", "activation", RED1[0:1, 0:1], BBF[0:1, 8:9], AFT.Sqrt,
              waits=[(s_dve, dv)], inc=s_act)
    emit("dve", "tensor_scalar", BBF[0:1, 10:11], RED1[0:1, 0:1], -1.02, None,
         OP.mult, waits=[(s_act, av)])
    drain("dve")
    dv = emit("dve", "nop", inc=s_dve)
    tv = emit("te", "matmul", bank(6, 1), ONES[0:1, 0:1].broadcast_to([1, P]),
              BBF[0:1, 10:11], start=True, stop=True,
              waits=[(s_dve, dv)], inc=s_te)
    emit("dve", "tensor_copy", NEGB, bank(6, 1), waits=[(s_te, tv)])
    drain("dve")
    negb_done = emit("dve", "nop", inc=s_dve)
    bg[6] = (s_dve, negb_done)

    A.alloc("xn_fm", DCH * TLOC // 2)
    XNF = A.b("xn_fm", DCH * TLOC)
    srcx = bass.AP(xn_sp.tensor, 0, [[TLOC, P], [P * TLOC, DCH], [1, TLOC]])
    ph1_te_done = s_te.n
    xn_rl = dma("sp", srd, XNF.rearrange("p (ki n) -> p ki n", ki=DCH), srcx,
                waits=[(stw, stw.n), (s_te, ph1_te_done), (s_dve, s_dve.n),
                       (sptw, sptw.n)])
    A.alloc("m_fm", ICH * TLOC // 2)
    MFM = A.b("m_fm", ICH * TLOC)
    A.alloc("wgu_t", 2 * DCH * P)
    GBUF = A.b("wgu_t", 4 * DCH * P)
    A.alloc("silu_t", TLOC)
    SILU = A.f("silu_t", TLOC)

    gu_guard = [None, None]
    for j in range(ICH):
        lv = 0
        for s_i, blk in enumerate((j, ICH + j)):
            dbuf = GBUF[:, ((j % 2) * 2 + s_i) * DCH * P:
                        ((j % 2) * 2 + s_i + 1) * DCH * P]
            srcb = bass.AP(wguT.tensor, blk * D * P,
                           [[P, P], [P * P, DCH], [1, P]])
            w = []
            if s_i == 0:
                w = [(srd, srd.n)]
            if j < 2:
                w += [(sptw, sptw.n), (s_te, ph1_te_done)]
            elif s_i == 0 and gu_guard[j % 2] is not None:
                w += [gu_guard[j % 2]]
            lv = dma("sp", srd, dbuf.rearrange("p (ki c) -> p ki c", ki=DCH),
                     srcb, waits=w)
        emit("te", "nop", waits=[(srd, lv), (srd, xn_rl)])
        for s_i in range(2):
            dbuf = GBUF[:, ((j % 2) * 2 + s_i) * DCH * P:
                        ((j % 2) * 2 + s_i + 1) * DCH * P]
            for ki in range(DCH):
                emit("te", "ldweights", dbuf[:, ki * P:(ki + 1) * P])
                for bi, (to, tw) in enumerate(TLB):
                    bidx = s_i * 2 + bi
                    w = bgw(bidx) if ki == 0 else []
                    tv = emit("te", "matmul", bank(bidx, tw),
                              dbuf[:, ki * P:(ki + 1) * P],
                              XNF[:, ki * TLOC + to: ki * TLOC + to + tw],
                              start=(ki == 0), stop=(ki == DCH - 1), waits=w,
                              inc=(s_te if (s_i == 1 and ki == DCH - 1
                                            and bi == NTB - 1) else None))
        gu_guard[j % 2] = (s_te, tv)
        for bi, (to, tw) in enumerate(TLB):
            av = emit("act", "activation", SILU[:, to:to + tw], bank(bi, tw),
                      AFT.Silu, waits=[(s_te, tv)] if bi == 0 else [],
                      inc=(s_act if bi == NTB - 1 else None))
        for bi, (to, tw) in enumerate(TLB):
            emit("dve", "tensor_tensor",
                 MFM[:, j * TLOC + to: j * TLOC + to + tw],
                 SILU[:, to:to + tw], bank(2 + bi, tw), OP.mult,
                 waits=[(s_act, av)] if bi == 0 else [])
        drain("dve")
        mk = emit("dve", "nop", inc=s_dve)
        for bidx in range(2 * NTB + 2):
            if bidx < 8:
                bg[bidx] = (s_dve, mk)
    A.release("wgu_t", "silu_t", "xn_fm")
    up_te_done = s_te.n
    up_dve_done = s_dve.n

    A.alloc("wdn_t", 2 * D)
    WDN = A.b("wdn_t", 2 * D)
    A.alloc("ffn_t", D)
    FFT = A.f("ffn_t", D)
    dn_guard = [None, None]
    out_stash = 0
    for tp in range(0, NTO, 2):
        ntp = min(2, NTO - tp)
        for i in range(ICH):
            dbuf = WDN[:, (i % 2) * D:(i % 2) * D + D]
            w = [(srd, srd.n)]
            if dn_guard[i % 2] is not None:
                w.append(dn_guard[i % 2])
            if tp == 0 and i < 2:
                w += [(sptw, sptw.n), (s_te, up_te_done), (s_dve, up_dve_done)]
            lv = dma("sp", srd, dbuf, wdnT[i * P:(i + 1) * P, :], waits=w)
            emit("te", "nop", waits=[(srd, lv)])
            for tti in range(ntp):
                tt = tp + tti
                emit("te", "ldweights",
                     MFM[:, i * TLOC + tt * P: i * TLOC + (tt + 1) * P])
                for ci, (co, cw) in enumerate(COGS):
                    bidx = tti * NCG + ci
                    w2 = bgw(bidx) if i == 0 else []
                    tvd = emit("te", "matmul", bank(bidx, cw),
                               MFM[:, i * TLOC + tt * P: i * TLOC + (tt + 1) * P],
                               WDN[:, (i % 2) * D + co:(i % 2) * D + co + cw],
                               start=(i == 0), stop=(i == ICH - 1), waits=w2,
                               inc=(s_te if (tti == ntp - 1 and ci == NCG - 1)
                                    else None))
            dn_guard[i % 2] = (s_te, tvd)
        tv = dn_guard[(ICH - 1) % 2][1]
        for tti in range(ntp):
            tt = tp + tti
            for ci, (co, cw) in enumerate(COGS):
                bidx = tti * NCG + ci
                emit("dve", "scalar_tensor_tensor", FFT[:, co:co + cw],
                     bank(bidx, cw), 1.0, A.b("bo_b", D)[:, co:co + cw],
                     OP.mult, OP.add,
                     waits=([(s_te, tv), (stw, out_stash)]
                            if (tti == 0 and ci == 0) else []))
            drain("dve")
            dv = emit("dve", "nop", inc=s_dve)
            out_stash = dma("sp", stw, ffn_sp[tt * P:(tt + 1) * P, :], FFT,
                            waits=[(s_dve, dv)])
            mk = emit("dve", "nop", waits=[(stw, out_stash)], inc=s_dve)
            for ci in range(NCG):
                bg[tti * NCG + ci] = (s_dve, mk)
    A.release("wdn_t", "ffn_t", "m_fm")

    A.alloc("qn_fm", DCH * TLOC // 2)
    QNF = A.b("qn_fm", DCH * TLOC)
    srcq = bass.AP(qn_sp.tensor, 0, [[TLOC, P], [P * TLOC, DCH], [1, TLOC]])
    ph2_te_done = s_te.n
    qn_rl = dma("sp", srd, QNF.rearrange("p (ki n) -> p ki n", ki=DCH), srcq,
                waits=[(s_te, ph2_te_done), (s_dve, s_dve.n)])
    A.alloc("kn_h", NKC * P)
    KNB = A.b("kn_h", 2 * NKC * P)
    A.alloc("v_h", NKC * HD)
    VB = A.b("v_h", 2 * NKC * HD)
    A.alloc("p_buf", NKC * QT // 2)
    PBUF = A.b("p_buf", NKC * QT)
    A.alloc("attn_fm", DCH * TLOC // 2)
    ATTN = A.b("attn_fm", DCH * TLOC)
    A.alloc("lrec", QT)
    LREC = A.f("lrec", QT)

    kn_loads = [0] * H
    head_done = [None] * (H + 2)

    def load_head(h):
        par = h % 2
        w = [(stw, kv_all_stored), (s_te, ph2_te_done), (srd, srd.n)]
        if h >= 2 and head_done[h - 2] is not None:
            w = [head_done[h - 2], (srd, srd.n)]
        last = 0
        for g in range(NGRP):
            last = dma("sp", srd,
                       KNB[:, par * NKC * P + g * GT:par * NKC * P + (g + 1) * GT],
                       st_kn[g * D + h * P: g * D + (h + 1) * P, :],
                       waits=w if g == 0 else [])
        srcv = bass.AP(st_v.tensor, h * NKC * P * HD,
                       [[HD, P], [P * HD, NKC], [1, HD]])
        last = dma("sp", srd,
                   VB[:, par * NKC * HD:(par + 1) * NKC * HD]
                   .rearrange("p (kc c) -> p kc c", kc=NKC), srcv)
        kn_loads[h] = last

    load_head(0)
    if H > 1:
        load_head(1)
    emit("act", "nop", waits=[(s_dve, negb_done)])
    for h in range(H):
        par = h % 2
        kn_h = KNB[:, par * NKC * P:(par + 1) * NKC * P]
        v_h = VB[:, par * NKC * HD:(par + 1) * NKC * HD]
        emit("te", "nop", waits=[(srd, kn_loads[h]), (srd, qn_rl),
                                 (s_dve, negb_done)])
        for qi in range(NQT):
            q0 = qi * QT
            rhs_q = QNF[:, h * TLOC + q0: h * TLOC + q0 + QT]
            exp_marks = []
            for c0 in range(0, NKC, 2):
                b_i = (c0 // 2) % 2
                npair = min(2, NKC - c0)
                for cc in range(npair):
                    w = bgw(b_i) if cc == 0 else []
                    emit("te", "ldweights",
                         kn_h[:, (c0 + cc) * P:(c0 + cc + 1) * P])
                    tv = emit("te", "matmul", bank(b_i, QT, o=cc * QT),
                              kn_h[:, (c0 + cc) * P:(c0 + cc + 1) * P], rhs_q,
                              start=True, stop=True, waits=w,
                              inc=(s_te if cc == npair - 1 else None))
                av = emit("act", "activation",
                          PBUF[:, c0 * QT:(c0 + npair) * QT],
                          bank(b_i, npair * QT), AFT.Exp, bias=NEGB, scale=1.0,
                          waits=[(s_te, tv)], inc=s_act)
                exp_marks.append(av)
                bg[b_i] = (s_act, av)
            for c in range(NKC):
                w = [(s_act, exp_marks[c // 2])]
                if c == 0:
                    w += bgw(2)
                emit("te", "ldweights", v_h[:, c * HD:(c + 1) * HD], waits=w)
                av_tv = emit("te", "matmul", bank(2, QT),
                             v_h[:, c * HD:(c + 1) * HD],
                             PBUF[:, c * QT:(c + 1) * QT],
                             start=(c == 0), stop=(c == NKC - 1),
                             inc=(s_te if c == NKC - 1 else None))
            emit("te", "ldweights", ONES, waits=bgw(3))
            for c in range(NKC):
                l_tv = emit("te", "matmul", bank(3, QT)[0:1, :], ONES,
                            PBUF[:, c * QT:(c + 1) * QT],
                            start=(c == 0), stop=(c == NKC - 1),
                            inc=(s_te if c == NKC - 1 else None))
            emit("dve", "tensor_copy", LROW[0:1, :], bank(3, QT)[0:1, :],
                 waits=[(s_te, l_tv)])
            drain("dve")
            emit("dve", "reciprocal", LROW[0:1, :], LROW[0:1, :])
            drain("dve")
            dv = emit("dve", "nop", inc=s_dve)
            bg[3] = (s_dve, dv)
            tv = emit("te", "matmul", bank(6, QT), ONESF[0:1, :],
                      LROW[0:1, :], start=True, stop=True,
                      waits=[(s_dve, dv)] + bgw(6), inc=s_te)
            emit("dve", "tensor_copy", LREC, bank(6, QT), waits=[(s_te, tv)])
            drain("dve")
            lb = emit("dve", "nop", inc=s_dve)
            bg[6] = (s_dve, lb)
            emit("dve", "tensor_tensor",
                 ATTN[:, h * TLOC + q0: h * TLOC + q0 + QT], bank(2, QT), LREC,
                 OP.mult, waits=[(s_te, av_tv)])
            drain("dve")
            bg[2] = (s_dve, emit("dve", "nop", inc=s_dve))
        head_done[h] = (s_te, l_tv)
        if h + 2 < H:
            load_head(h + 2)
    attn_done = emit("dve", "nop", inc=s_dve)
    A.release("kn_h", "v_h", "p_buf", "qn_fm", "lrec")

    A.alloc("wo_r", DCH * D // 2)
    WOB = A.b("wo_r", DCH * D)
    srcw2 = wo_d[:, :].rearrange("(ki p) n -> p ki n", p=P)
    dstw2 = WOB.rearrange("p (ki n) -> p ki n", ki=DCH)
    wo_loaded = 0
    for kk in range(0, DCH, 4):
        kn = min(4, DCH - kk)
        wo_loaded = dma("gp", swwo, dstw2[:, kk:kk + kn], srcw2[:, kk:kk + kn],
                        waits=[(s_te, head_done[H - 1][1] if head_done[H - 1]
                                else 0), (s_dve, attn_done)] if kk == 0 else [])
    A.alloc("ffn_in", D)
    FIN = A.f("ffn_in", D)
    A.alloc("x_in", D)
    XIN = A.f("x_in", D)
    A.alloc("out_t", D)
    OUTT = A.f("out_t", D)
    emit("te", "nop", waits=[(swwo, wo_loaded), (s_dve, attn_done)])
    out_guard = 0
    for tt in range(NTO):
        fv = dma("sp", srd, FIN, ffn_sp[tt * P:(tt + 1) * P, :],
                 waits=[(s_dve, out_guard if tt > 0 else attn_done)])
        xv = dma("sp", srd, XIN, xow_d[tt * P:(tt + 1) * P, :])
        for ki in range(DCH):
            emit("te", "ldweights",
                 ATTN[:, ki * TLOC + tt * P: ki * TLOC + (tt + 1) * P])
            for ci, (co, cw) in enumerate(COGS):
                w = bgw(ci) if ki == 0 else []
                tv = emit("te", "matmul", bank(ci, cw),
                          ATTN[:, ki * TLOC + tt * P: ki * TLOC + (tt + 1) * P],
                          WOB[:, ki * D + co: ki * D + co + cw],
                          start=(ki == 0), stop=(ki == DCH - 1), waits=w,
                          inc=(s_te if (ki == DCH - 1 and ci == NCG - 1)
                               else None))
        for ci, (co, cw) in enumerate(COGS):
            w = [(s_te, tv), (srd, xv)] if ci == 0 else []
            emit("dve", "tensor_tensor", OUTT[:, co:co + cw], bank(ci, cw),
                 FIN[:, co:co + cw], OP.add, waits=w)
        drain("dve")
        emit("dve", "tensor_tensor", OUTT, OUTT, XIN, OP.add,
             waits=[(srd, xv)])
        drain("dve")
        dv = emit("dve", "nop", inc=s_dve)
        for ci in range(NCG):
            bg[ci] = (s_dve, dv)
        ov = dma("sp", stw, y_d[tt * P:(tt + 1) * P, :], OUTT,
                 waits=[(s_dve, dv)])
        out_guard = emit("dve", "nop", waits=[(stw, ov)], inc=s_dve)

    if dbg:
        dbgs = [("d_xn", xn_sp[:, :]), ("d_qn", qn_sp[:, :]),
                ("d_kn", st_kn[0:D, :]), ("d_v", st_v[0:NKC * P, :]),
                ("d_ffn", ffn_sp[:, :]), ("d_wgu", wguT[0:D, :]),
                ("d_wdn", wdnT[0:P, :])]
        for nm, srcd in dbgs:
            od = nc.dram_tensor(nm, list(srcd.shape),
                                srcd.dtype, kind="ExternalOutput").ap()
            dma("gp", swgp, od[:, :], srcd,
                waits=[(stw, stw.n), (sptw, sptw.n), (srd, srd.n)])
        emit("gp", "nop", waits=[(swgp, swgp.n)])
    emit("sp", "nop", waits=[(stw, stw.n)])
    emit("te", "nop", waits=[(stw, stw.n)])
    emit("gp", "nop", waits=[(stw, stw.n)])
    emit("act", "nop", waits=[(stw, stw.n)])

    def render(q):
        def body(eng):
            for meth, args, kw, waits, incs in Q[q]:
                for s, v in waits:
                    eng.wait_ge(s, v)
                if meth == "nop":
                    if incs:
                        ins = eng.nop()
                        for s, k in incs:
                            ins.then_inc(s, k)
                    continue
                ins = getattr(eng, meth)(*args, **kw)
                if ins is not None:
                    for s, k in incs:
                        ins.then_inc(s, k)
        return body

    with nc.Block() as blk:
        blk.tensor(render("te"))
        blk.scalar(render("act"))
        blk.vector(render("dve"))
        blk.sync(render("sp"))
        blk.gpsimd(render("gp"))
    ctx.close()
    return nc


def _consts(cfg: Cfg):
    f = ref_freqs(cfg.HD)
    fdup = np.repeat(f, 2)[None, :].repeat(P, 0).astype(np.float32)
    tcols = (np.arange(cfg.T, dtype=np.float32).reshape(cfg.T // P, P).T)
    return (np.ascontiguousarray(fdup), np.ascontiguousarray(tcols),
            np.eye(P).astype(ml_dtypes.bfloat16),
            np.ones((P, 1), ml_dtypes.bfloat16))


def _kq_tables(cfg: Cfg, t0, t1):
    f = ref_freqs(cfg.HD)
    fdup = np.repeat(f, 2).astype(np.float32)
    t_idx = np.arange(t0, t1, dtype=np.float32)
    ang32 = (t_idx[:, None] * fdup[None, :]).astype(np.float32)
    k1 = np.floor(ang32.astype(np.float64) / (2 * np.pi))
    k2 = np.floor((ang32.astype(np.float64) + np.float64(np.float32(np.pi / 2)))
                  / (2 * np.pi))
    nt = (t1 - t0) // P

    def lay(k):
        k = k.astype(np.float32).reshape(nt, P, cfg.HD)
        return np.ascontiguousarray(k.transpose(1, 0, 2).reshape(P, nt * cfg.HD))

    return lay(k1), lay(k2)


def _bc(v):
    return np.ascontiguousarray(
        np.broadcast_to(np.asarray(v, np.float32).reshape(1, -1), (P, len(np.asarray(v).reshape(-1)))))


def run(cfg: Cfg, inputs: dict, runner=None, dbg=False):
    nc = build(cfg, dbg=dbg)
    fdup, tcols, ident, ones = _consts(cfg)
    x = np.ascontiguousarray(np.asarray(inputs["L0_attn_x"], np.float32)
                             .reshape(cfg.T, cfg.D))
    base = {
        "x_full": x,
        "Wq": np.ascontiguousarray(np.asarray(inputs["L0_attn_Wq"], np.float32)),
        "Wkv": np.ascontiguousarray(np.asarray(inputs["L0_attn_Wkv"], np.float32)),
        "Wo": np.ascontiguousarray(np.asarray(inputs["L0_attn_Wo"], np.float32)),
        "Wgu": np.ascontiguousarray(np.asarray(inputs["L0_ffn_w_gate_up"],
                                               np.float32)),
        "Wdn": np.ascontiguousarray(np.asarray(inputs["L0_ffn_w_down"],
                                               np.float32)),
        "ln_q_g": _bc(inputs["L0_attn_ln_q_g"]),
        "ln_q_b": _bc(inputs["L0_attn_ln_q_b"]),
        "ln_kv_g": _bc(inputs["L0_attn_ln_kv_g"]),
        "ln_kv_b": _bc(inputs["L0_attn_ln_kv_b"]),
        "ln_h_g": _bc(inputs["L0_attn_ln_h_g"]),
        "ln_h_b": _bc(inputs["L0_attn_ln_h_b"]),
        "bq": _bc(inputs["L0_attn_bq"]),
        "bkv": _bc(inputs["L0_attn_bkv"]),
        "bo": _bc(inputs["L0_attn_bo"]),
        "ident_c": ident, "ones_c": ones, "fdup_c": fdup, "tcols_c": tcols,
        "onesf_c": np.ones((1, P), np.float32),
    }
    kq1, kq2 = _kq_tables(cfg, 0, cfg.T)
    base["kq_c"], base["kq2_c"] = kq1, kq2
    base = {k: np.ascontiguousarray(v) for k, v in base.items()}
    in_maps = []
    tl = cfg.TLOC
    for c in range(NCORES):
        m = dict(base)
        m["x_own"] = np.ascontiguousarray(x[c * tl:(c + 1) * tl, :])
        m["tcown_c"] = np.ascontiguousarray(
            np.arange(c * tl, (c + 1) * tl, dtype=np.float32)
            .reshape(tl // P, P).T)
        m["kqo_c"], m["kqo2_c"] = _kq_tables(cfg, c * tl, (c + 1) * tl)
        in_maps.append(m)
    if runner is None:
        res = run_bass_kernel_spmd(nc, in_maps, core_ids=list(range(NCORES)))
        outs = [res.results[c]["y"] for c in range(NCORES)]
        if dbg:
            y = np.concatenate([np.asarray(o, np.float32) for o in outs], axis=0)
            return y.reshape(1, cfg.T, cfg.D), res.results
    else:
        outs = runner(nc, in_maps)
    y = np.concatenate([np.asarray(o, np.float32) for o in outs], axis=0)
    return y.reshape(1, cfg.T, cfg.D)


def kernel(**inputs):
    return run(FULL, inputs)


# version 08
# speedup vs baseline: 1.0931x; 1.0931x over previous
from contextlib import ExitStack
from dataclasses import dataclass

import numpy as np
import ml_dtypes

import concourse.bass as bass
import concourse.mybir as mybir
from concourse.bass_utils import run_bass_kernel_spmd
from concourse.alu_op_type import AluOpType as OP
F32 = mybir.dt.float32
BF16 = mybir.dt.bfloat16
AFT = mybir.ActivationFunctionType
AX = mybir.AxisListType

NCORES = 8
P = 128
EPS = 1e-5


@dataclass
class Cfg:
    D: int = 2048
    T: int = 8192
    H: int = 16
    HD: int = 128
    INTER: int = 8192
    QT: int = 256

    @property
    def TLOC(self): return self.T // NCORES
    @property
    def DCH(self): return self.D // P
    @property
    def ICH(self): return self.INTER // P
    @property
    def GT(self): return self.T // 8
    @property
    def NTG(self): return self.GT // P
    @property
    def NTO(self): return self.TLOC // P
    @property
    def NKC(self): return self.T // P
    @property
    def NQT(self): return self.TLOC // self.QT
    @property
    def COGS(self): return [(o, min(512, self.D - o)) for o in range(0, self.D, 512)]
    @property
    def TLB(self): return [(o, min(512, self.TLOC - o))
                           for o in range(0, self.TLOC, 512)]


FULL = Cfg()
MINI = Cfg(D=256, T=2048, H=2, HD=128, INTER=1024, QT=256)

SCL = float(np.float32(128.0 ** -0.25))
TWO_PI = float(np.float64(2 * np.pi))
INV_2PI = float(np.float32(1.0 / (2 * np.pi)))
_c1 = np.float32(6.28125)
_c2 = np.float32(np.float64(TWO_PI) - np.float64(_c1))
_c3 = np.float32(np.float64(TWO_PI) - np.float64(_c1) - np.float64(_c2))
C1, C2, C3 = float(_c1), float(_c2), float(_c3)
PI = float(np.float32(np.pi))
PI_2 = float(np.float32(np.pi / 2))


def ref_freqs(hd):
    lin = np.linspace(np.float32(0.0), np.float32(1.0), hd // 2, dtype=np.float32)
    f = np.float32(40.0) ** lin
    f = f * np.float32(200.0) / np.float32(1000.0)
    return (np.float32(200.0) * f / np.float32(1000.0)).astype(np.float32)


class Sem:
    def __init__(self, h):
        self.h, self.n = h, 0

    def inc(self, k=1):
        self.n += k
        return self.n


class Arena:
    def __init__(self, ap, total):
        self.ap, self.total = ap, total
        self.free = [(0, total)]
        self.live = {}

    def alloc(self, name, units):
        units = (units + 7) & ~7
        best = None
        for i, (off, sz) in enumerate(self.free):
            if sz >= units and (best is None or sz < self.free[best][1]):
                best = i
        if best is None:
            raise RuntimeError(f"arena OOM {name} needs {units} free={self.free} "
                               f"live={ {k: v[1] for k, v in self.live.items()} }")
        off, sz = self.free[best]
        self.free[best] = (off + units, sz - units)
        if self.free[best][1] == 0:
            self.free.pop(best)
        self.live[name] = (off, units)
        return self

    def alloc_high(self, name, units):
        units = (units + 7) & ~7
        for i in range(len(self.free) - 1, -1, -1):
            off, sz = self.free[i]
            if sz >= units:
                self.free[i] = (off, sz - units)
                if self.free[i][1] == 0:
                    self.free.pop(i)
                self.live[name] = (off + sz - units, units)
                return self
        raise RuntimeError(f"arena OOM(high) {name} needs {units}")

    def release(self, *names):
        for name in names:
            off, sz = self.live.pop(name)
            self.free.append((off, sz))
        self.free.sort()
        m = []
        for o, s in self.free:
            if m and m[-1][0] + m[-1][1] == o:
                m[-1][1] += s
            else:
                m.append([o, s])
        self.free = [tuple(x) for x in m]

    def f(self, name, cols=None):
        off, sz = self.live[name]
        return self.ap[:, off:off + (sz if cols is None else cols)]

    def b(self, name, cols=None):
        off, sz = self.live[name]
        v = self.ap[:, off:off + sz].bitcast(BF16)
        return v if cols is None else v[:, 0:cols]


def build(cfg: Cfg, dbg: bool = False):
    D, T, H, HD, INTER, QT = cfg.D, cfg.T, cfg.H, cfg.HD, cfg.INTER, cfg.QT
    DCH, ICH, GT, NTG, NTO = cfg.DCH, cfg.ICH, cfg.GT, cfg.NTG, cfg.NTO
    NKC, NQT, TLOC = cfg.NKC, cfg.NQT, cfg.TLOC
    COGS, TLB = cfg.COGS, cfg.TLB
    NCG, NTB = len(COGS), len(TLB)
    NGRP = T // GT

    nc = bass.Bass("TRN2", target_bir_lowering=False)

    x_d = nc.dram_tensor("x_full", [T, D], F32, kind="ExternalInput").ap()
    xow_d = nc.dram_tensor("x_own", [TLOC, D], F32, kind="ExternalInput").ap()
    wq_d = nc.dram_tensor("Wq", [D, D], F32, kind="ExternalInput").ap()
    wkv_d = nc.dram_tensor("Wkv", [D, 2 * D], F32, kind="ExternalInput").ap()
    wo_d = nc.dram_tensor("Wo", [D, D], F32, kind="ExternalInput").ap()
    wgu_d = nc.dram_tensor("Wgu", [2 * INTER, D], F32, kind="ExternalInput").ap()
    wdn_d = nc.dram_tensor("Wdn", [D, INTER], F32, kind="ExternalInput").ap()
    V = {}
    for nm, n in [("ln_q_g", D), ("ln_q_b", D), ("ln_kv_g", D), ("ln_kv_b", D),
                  ("ln_h_g", HD), ("ln_h_b", HD), ("bq", D), ("bkv", 2 * D),
                  ("bo", D)]:
        V[nm] = nc.dram_tensor(nm, [P, n], F32, kind="ExternalInput").ap()
    ident_d = nc.dram_tensor("ident_c", [P, P], BF16, kind="ExternalInput").ap()
    ones_d = nc.dram_tensor("ones_c", [P, 1], BF16, kind="ExternalInput").ap()
    onesf_d = nc.dram_tensor("onesf_c", [1, P], F32, kind="ExternalInput").ap()
    fdup_d = nc.dram_tensor("fdup_c", [P, HD], F32, kind="ExternalInput").ap()
    tcols_d = nc.dram_tensor("tcols_c", [P, T // P], F32, kind="ExternalInput").ap()
    tco_d = nc.dram_tensor("tcown_c", [P, NTO], F32, kind="ExternalInput").ap()
    kq_d = nc.dram_tensor("kq_c", [P, (T // P) * HD], F32,
                          kind="ExternalInput").ap()
    kq2_d = nc.dram_tensor("kq2_c", [P, (T // P) * HD], F32,
                           kind="ExternalInput").ap()
    kqo_d = nc.dram_tensor("kqo_c", [P, NTO * HD], F32,
                           kind="ExternalInput").ap()
    kqo2_d = nc.dram_tensor("kqo2_c", [P, NTO * HD], F32,
                            kind="ExternalInput").ap()
    gqc_d = nc.dram_tensor("gqc_c", [P, DCH], F32, kind="ExternalInput").ap()
    bqc_d = nc.dram_tensor("bqc_c", [P, DCH], F32, kind="ExternalInput").ap()
    gkc_d = nc.dram_tensor("gkc_c", [P, DCH], F32, kind="ExternalInput").ap()
    bkc_d = nc.dram_tensor("bkc_c", [P, DCH], F32, kind="ExternalInput").ap()
    y_d = nc.dram_tensor("y", [TLOC, D], F32, kind="ExternalOutput").ap()

    wq_bf = nc.dram_tensor("wq_bf", [D, D], BF16).ap()
    wkv_bf = nc.dram_tensor("wkv_bf", [D, 2 * D], BF16).ap()
    wgu_bf = nc.dram_tensor("wgu_bf", [2 * INTER, D], BF16).ap()
    wdn_bf = nc.dram_tensor("wdn_bf", [D, INTER], BF16).ap()
    wguT = nc.dram_tensor("wguT", [2 * ICH * D, P], BF16).ap()
    wdnT = nc.dram_tensor("wdnT", [ICH * P, D], BF16).ap()
    st_kn = nc.dram_tensor("st_kn", [NGRP * D, GT], BF16).ap()
    st_v = nc.dram_tensor("st_v", [H * NKC * P, HD], BF16).ap()
    qn_sp = nc.dram_tensor("qn_sp", [D, TLOC], BF16).ap()
    xn_sp = nc.dram_tensor("xn_sp", [D, TLOC], BF16).ap()
    ffn_sp = nc.dram_tensor("ffn_sp", [TLOC, D], F32).ap()

    ctx = ExitStack()
    AR_U = 49280
    arena_t = ctx.enter_context(nc.sbuf_tensor("arena", [P, AR_U], F32))
    A = Arena(arena_t[:], AR_U)
    small_t = ctx.enter_context(nc.sbuf_tensor("small", [P, 1150], F32))
    SM = small_t[:]
    psum_t = ctx.enter_context(nc.psum_tensor("psum", [P, 4096], F32))
    PS, PSB = psum_t[:], psum_t[:].bitcast(BF16)
    ident_t = ctx.enter_context(nc.sbuf_tensor("identS", [P, P], BF16))
    IDENT = ident_t[:]
    ones_t = ctx.enter_context(nc.sbuf_tensor("onesS", [P, 1], BF16))
    ONES = ones_t[:]
    onesf_t = ctx.enter_context(nc.sbuf_tensor("onesF", [1, P], F32))
    ONESF = onesf_t[:]
    trt_t = ctx.enter_context(nc.sbuf_tensor("trtS", [P, 2048], BF16,
                                             is_transpose_dest=True))
    TRT = trt_t[:]

    def bank(b, w=512, o=0):
        return PS[:, b * 512 + o: b * 512 + o + w]

    def bankb(b, w=1024, o=0):
        return PSB[:, b * 1024 + o: b * 1024 + o + w]

    sems = {n: Sem(ctx.enter_context(nc.semaphore(n))) for n in
            ("sx", "stw", "srd", "swgp", "swff", "ste", "sact", "sdve", "sgp",
             "swcast", "swwq", "swkv", "swwo", "sptw", "skq", "swgp2")}
    sx, stw, srd, swgp, swff = (sems[n] for n in ("sx", "stw", "srd", "swgp",
                                                  "swff"))
    s_te, s_act, s_dve, s_gp = (sems[n] for n in ("ste", "sact", "sdve", "sgp"))
    swcast, swwq, swkv, swwo, sptw, skq, swgp2 = (sems[n] for n in
                                                  ("swcast", "swwq", "swkv",
                                                   "swwo", "sptw", "skq",
                                                   "swgp2"))

    Q = {k: [] for k in ("te", "act", "dve", "sp", "gp")}

    def emit(q, meth, *args, waits=(), inc=None, inc16=None, **kw):
        r = None
        incs = []
        if inc is not None:
            r = inc.inc()
            incs.append((inc.h, 1))
        if inc16 is not None:
            r = inc16.inc(16)
            incs.append((inc16.h, 16))
        ws = [(s.h, v) for (s, v) in waits if s is not None and v and v > 0]
        Q[q].append((meth, args, kw, ws, incs))
        return r

    def drain(q):
        emit(q, "drain")

    def dma(q, sem, dst, src, waits=()):
        return emit(q, "dma_start", waits=waits, inc16=sem, out=dst, in_=src)

    def dma_tr(q, sem, dst, src, waits=()):
        return emit(q, "dma_start_transpose", dst, src, waits=waits, inc16=sem)

    bg = [None] * 8

    def bgw(b):
        return [] if bg[b] is None else [bg[b]]

    _smo = [0]

    def sm(n):
        o = _smo[0]
        _smo[0] += max(n, 1)
        assert _smo[0] <= 1150, _smo[0]
        return SM[:, o:o + n]

    FDUP = sm(HD)
    TCOLS = sm(T // P)
    TCO = sm(NTO)
    ST = sm(16)
    HL = sm(6 * H)
    MSV = sm(NTG)
    NMSV = sm(NTG)
    QMAX = sm(H)
    KMAX = sm(H)
    RED1 = sm(2)
    BBFr = sm(8)
    NEGB = sm(1)
    NEGPI = sm(1)
    LROW = sm(QT)
    ANG = sm(HD)
    WRK = sm(HD)
    WRK2 = sm(HD)

    BBF = BBFr.bitcast(BF16)

    def hl(i):
        return HL[:, i * H:(i + 1) * H]

    dma("sp", sx, IDENT, ident_d[:, :])
    dma("sp", sx, ONES, ones_d[:, :])
    dma("sp", sx, ONESF, onesf_d[:, :])
    dma("sp", sx, FDUP, fdup_d[:, :])
    dma("sp", sx, TCOLS, tcols_d[:, :])
    dma("sp", sx, TCO, tco_d[:, :])
    emit("dve", "memset", NEGPI, -PI)
    drain("dve")

    A.alloc("vrow", 2 * D)
    vrow = A.f("vrow")
    bc_specs = [("bqs_b", V["bq"][:, :], D, SCL),
                ("bkvk_s", V["bkv"][:, 0:D], D, SCL),
                ("bkvv_b", V["bkv"][:, D:2 * D], D, None),
                ("lnhg_e", V["ln_h_g"][:, :], HD, SCL),
                ("lnhb_e", V["ln_h_b"][:, :], HD, SCL),
                ("bo_b", V["bo"][:, :], D, None)]
    last_bc = 0
    for nm, srcv, n, scale in bc_specs:
        A.alloc(nm, (n + 1) // 2)
        vv = dma("gp", swgp, vrow[:, 0:n], srcv, waits=[(s_dve, last_bc)])
        if scale is None:
            last_bc = emit("dve", "tensor_copy", A.b(nm, n), vrow[:, 0:n],
                           waits=[(swgp, vv)], inc=s_dve)
        else:
            last_bc = emit("dve", "tensor_scalar_mul", A.b(nm, n), vrow[:, 0:n],
                           float(scale), waits=[(swgp, vv)], inc=s_dve)
        drain("dve")
    bc_done = last_bc
    av = emit("act", "activation", vrow[:, 0:HD], A.b("lnhg_e", HD), AFT.Square,
              waits=[(s_dve, last_bc)], inc=s_act)
    emit("dve", "reduce_max", RED1[:, 0:1], vrow[:, 0:HD], axis=AX.X,
         waits=[(s_act, av)])
    drain("dve")
    dv = emit("dve", "nop", inc=s_dve)
    av = emit("act", "activation", vrow[:, 0:HD], A.b("lnhb_e", HD), AFT.Square,
              waits=[(s_dve, dv)], inc=s_act)
    emit("dve", "reduce_sum", RED1[:, 1:2], vrow[:, 0:HD], axis=AX.X,
         waits=[(s_act, av)])
    drain("dve")
    dv = emit("dve", "nop", inc=s_dve)
    av = emit("act", "activation", BBFr[:, 2:4], RED1[:, 0:2], AFT.Sqrt,
              waits=[(s_dve, dv)], inc=s_act)
    emit("dve", "tensor_scalar", BBFr[:, 0:1], BBFr[:, 2:3],
         float(np.sqrt(HD)), None, OP.mult, waits=[(s_act, av)])
    drain("dve")
    emit("dve", "tensor_tensor", BBFr[:, 0:1], BBFr[:, 0:1], BBFr[:, 3:4], OP.add)
    drain("dve")
    emit("dve", "tensor_tensor", BBFr[:, 1:2], BBFr[:, 0:1], BBFr[:, 0:1], OP.mult)
    drain("dve")
    emit("dve", "tensor_scalar", NEGB, BBFr[:, 1:2], -1.0001, None, OP.mult)
    drain("dve")
    negb_done = emit("dve", "nop", inc=s_dve)

    A.alloc("wprep", 2 * D)
    A.alloc("gcolp", 2 * DCH + 4)
    WPREP = A.b("wprep", 4 * D)
    GCOL = A.f("gcolp", 2 * DCH)
    BCOLB = A.f("gcolp")[:, 2 * DCH:2 * DCH + 2].bitcast(BF16)
    gl1 = dma("sp", sx, GCOL[:, 0:DCH], gqc_d[:, :])
    gl2 = dma("sp", sx, GCOL[:, DCH:2 * DCH], gkc_d[:, :])
    A.alloc("bcolf", DCH * 2)
    BCOLF = A.f("bcolf", 2 * DCH)
    bl1 = dma("sp", sx, BCOLF[:, 0:DCH], bqc_d[:, :])
    bl2 = dma("sp", sx, BCOLF[:, DCH:2 * DCH], bkc_d[:, :])
    emit("dve", "nop", waits=[(sx, bl2)])
    prep_specs = [("q", wq_d, wq_bf, D, 0), ("kv", wkv_d, wkv_bf, 2 * D, 1)]
    A.alloc("brow", 3 * D)
    BROW = A.f("brow", 3 * D)
    prep_done = {}
    for nm, wsrc, wdst, cols, gi in prep_specs:
        ncog = cols // 512 if cols >= 512 else 1
        cw0 = min(512, cols)
        pg = None
        for ki in range(DCH):
            buf = WPREP[:, (ki % 2) * 2 * D:(ki % 2) * 2 * D + cols]
            w = [(s_dve, bc_done)] if ki == 0 and nm == "q" else []
            w.append((sptw, sptw.n))
            psem = swgp if ki % 2 == 0 else swgp2
            cv = dma("gp", psem, buf, wsrc[ki * P:(ki + 1) * P, :], waits=w)
            bv = emit("dve", "tensor_copy", BCOLB[:, gi:gi + 1],
                      BCOLF[:, gi * DCH + ki:gi * DCH + ki + 1],
                      waits=[(psem, cv)], inc=s_dve)
            drain("dve")
            for cg in range(ncog):
                tvp = emit("te", "matmul",
                           bank(cg % 4 + (4 if cg >= 4 else 0), cw0)[0:1, :],
                           BCOLB[:, gi:gi + 1], buf[:, cg * cw0:(cg + 1) * cw0],
                           start=(ki == 0), stop=(ki == DCH - 1),
                           waits=[(s_dve, bv)] if cg == 0 else [],
                           inc=(s_te if cg == ncog - 1 else None))
            sv = emit("dve", "tensor_scalar_mul", buf, buf,
                      GCOL[:, gi * DCH + ki:gi * DCH + ki + 1],
                      waits=[(s_te, tvp)], inc=s_dve)
            drain("dve")
            dma("sp", sptw, wdst[ki * P:(ki + 1) * P, :], buf,
                waits=[(s_dve, sv)])
        off = 0 if nm == "q" else D
        for cg in range(ncog):
            emit("dve", "tensor_copy", BROW[0:1, off + cg * cw0: off + (cg + 1) * cw0],
                 bank(cg % 4 + (4 if cg >= 4 else 0), cw0)[0:1, :],
                 waits=[(s_te, tvp)] if cg == 0 else [])
        drain("dve")
        prep_done[nm] = emit("dve", "nop", inc=s_dve)
        for b_ in range(8):
            bg[b_] = (s_dve, prep_done[nm])
    addsp = [("bqs_b", 0, D, SCL), ("bkvk_s", D, D, SCL), ("bkvv_b", 2 * D, D, None)]
    for nm, o0, n, scale in addsp:
        for cg in range(max(1, n // 512)):
            cw0 = min(512, n)
            tv = emit("te", "matmul", bank(6, cw0), ONESF[0:1, :],
                      BROW[0:1, o0 + cg * cw0: o0 + (cg + 1) * cw0],
                      start=True, stop=True,
                      waits=([(s_dve, prep_done["kv"])] + bgw(6)), inc=s_te)
            emit("dve", "scalar_tensor_tensor",
                 A.b(nm, n)[:, cg * cw0:(cg + 1) * cw0], bank(6, cw0),
                 (scale if scale is not None else 1.0),
                 A.b(nm, n)[:, cg * cw0:(cg + 1) * cw0], OP.mult, OP.add,
                 waits=[(s_te, tv)])
            drain("dve")
            bg[6] = (s_dve, emit("dve", "nop", inc=s_dve))
    prep_all = emit("dve", "nop", inc=s_dve)
    A.release("vrow", "wprep", "gcolp", "bcolf", "brow")

    ch = max(P, 2 * INTER // 8)
    for i in range(0, 2 * INTER, ch):
        dma("gp", swcast, wgu_bf[i:i + ch, :], wgu_d[i:i + ch, :])
    ch2 = max(P, D // 4)
    for i in range(0, D, ch2):
        dma("gp", swcast, wdn_bf[i:i + ch2, :], wdn_d[i:i + ch2, :])
    ffn_cast_done = swcast.n


    GU_ROWS = min(1024, 2 * INTER)
    DN_ROWS = min(1024, D)
    _ptr_tiles = []
    for j0 in range(0, 2 * INTER, GU_ROWS):
        for ki in range(DCH):
            _ptr_tiles.append(("gu", j0, ki))
    for h0 in range(0, D, DN_ROWS):
        for ic in range(ICH):
            _ptr_tiles.append(("dn", h0, ic))

    def ptr_slice(idx, total):
        n = len(_ptr_tiles)
        lo, hi = idx * n // total, (idx + 1) * n // total
        for ti in range(lo, hi):
            kind, a0, cc = _ptr_tiles[ti]
            buf_all = TRT
            rows = GU_ROWS if kind == "gu" else DN_ROWS
            buf = buf_all[:, (ti % 2) * 1024:(ti % 2) * 1024 + rows]
            w = [(swcast, ffn_cast_done)] if ti in (lo, lo + 1) else []
            w.append((sptw, sptw.n))
            if kind == "gu":
                src = wgu_bf[a0:a0 + rows, cc * P:(cc + 1) * P]
                tv = dma_tr("sp", swff, buf, src, waits=w)
                nblk = rows // P
                dst = bass.AP(wguT.tensor, ((a0 // P) * D + cc * P) * P,
                              [[P, P], [D * P, nblk], [1, P]])
                s3 = buf.rearrange("p (j c) -> p j c", j=nblk)
                dma("sp", sptw, dst, s3, waits=[(swff, tv)])
            else:
                src = wdn_bf[a0:a0 + rows, cc * P:(cc + 1) * P]
                tv = dma_tr("sp", swff, buf, src, waits=w)
                dma("sp", sptw, wdnT[cc * P:(cc + 1) * P, a0:a0 + rows], buf,
                    waits=[(swff, tv)])

    def rsqrt_ref(dst, src, t0, t1):
        av = emit("act", "activation", t0, src, AFT.Sqrt, inc=s_act)
        emit("dve", "reciprocal", dst, t0, waits=[(s_act, av)])
        drain("dve")
        emit("dve", "tensor_tensor", t0, dst, dst, OP.mult)
        drain("dve")
        emit("dve", "tensor_tensor", t0, t0, src, OP.mult)
        drain("dve")
        emit("dve", "tensor_scalar", t1, t0, -0.5, 1.5, OP.mult, OP.add)
        drain("dve")
        emit("dve", "tensor_tensor", dst, dst, t1, OP.mult)
        drain("dve")

    def ln_stats(xt, sq):
        emit("dve", "reduce_sum", ST[:, 0:1], xt, axis=AX.X)
        av = emit("act", "activation", sq, xt, AFT.Square, inc=s_act)
        emit("dve", "reduce_sum", ST[:, 1:2], sq, axis=AX.X, waits=[(s_act, av)])
        drain("dve")
        emit("dve", "tensor_scalar", ST[:, 2:4], ST[:, 0:2], 1.0 / D, None,
             OP.mult)
        drain("dve")
        emit("dve", "tensor_tensor", ST[:, 15:16], ST[:, 2:3], ST[:, 2:3], OP.mult)
        drain("dve")
        emit("dve", "tensor_tensor", ST[:, 3:4], ST[:, 3:4], ST[:, 15:16],
             OP.subtract)
        drain("dve")
        emit("dve", "tensor_scalar", ST[:, 4:5], ST[:, 3:4], EPS, None, OP.add)
        drain("dve")
        emit("dve", "reciprocal", ST[:, 7:8], ST[:, 4:5])
        drain("dve")
        emit("dve", "tensor_tensor", ST[:, 15:16], ST[:, 3:4], ST[:, 7:8], OP.mult)
        drain("dve")
        emit("dve", "tensor_scalar", ST[:, 5:6], ST[:, 15:16], EPS, None, OP.add)
        emit("dve", "tensor_scalar", ST[:, 6:7], ST[:, 15:16], float(D), None,
             OP.mult)
        drain("dve")
        dv = emit("dve", "nop", inc=s_dve)
        av = emit("act", "activation", ST[:, 8:11], ST[:, 4:7], AFT.Sqrt,
                  waits=[(s_dve, dv)], inc=s_act)
        emit("dve", "tensor_scalar", ST[:, 11:12], ST[:, 10:11], -1.0, None,
             OP.mult, waits=[(s_act, av)])
        emit("dve", "reciprocal", ST[:, 12:14], ST[:, 8:10])
        drain("dve")
        emit("dve", "tensor_tensor", ST[:, 14:16], ST[:, 12:14], ST[:, 12:14],
             OP.mult)
        drain("dve")
        emit("dve", "tensor_tensor", ST[:, 14:16], ST[:, 14:16], ST[:, 4:6],
             OP.mult)
        drain("dve")
        emit("dve", "tensor_scalar", ST[:, 14:16], ST[:, 14:16], -0.5, 1.5,
             OP.mult, OP.add)
        drain("dve")
        emit("dve", "tensor_tensor", ST[:, 12:14], ST[:, 12:14], ST[:, 14:16],
             OP.mult)
        drain("dve")
        emit("dve", "tensor_tensor", ST[:, 14:15], ST[:, 12:13], ST[:, 13:14],
             OP.mult)
        drain("dve")

    kq_guard = [0]

    def rot_tables(kqsrc1, kqsrc2, tcol, rr, ri):
        kq = A.f("kq", 2 * HD)
        kv1 = dma("sp", skq, kq[:, 0:HD], kqsrc1,
                  waits=[(s_dve, kq_guard[0])])
        kv2 = dma("sp", skq, kq[:, HD:2 * HD], kqsrc2)
        emit("dve", "tensor_scalar", ANG, FDUP, tcol, None, OP.mult)
        drain("dve")

        def wrap(dst, shift, kk, kwait):
            emit("dve", "scalar_tensor_tensor", dst, kk, -C1, ANG, OP.mult,
                 OP.add, waits=[(skq, kv2)])
            drain("dve")
            if shift != 0.0:
                emit("dve", "tensor_scalar", dst, dst, shift, None, OP.add)
                drain("dve")
            emit("dve", "scalar_tensor_tensor", dst, kk, -C2, dst, OP.mult, OP.add)
            drain("dve")
            emit("dve", "scalar_tensor_tensor", dst, kk, -C3, dst, OP.mult, OP.add)
            drain("dve")
            emit("dve", "tensor_scalar", dst, dst, 0.0, TWO_PI, OP.max, OP.min)
            drain("dve")
            dv = emit("dve", "nop", inc=s_dve)
            return emit("act", "activation", dst, dst, AFT.Sin, bias=NEGPI,
                        scale=1.0, waits=[(s_dve, dv)], inc=s_act)

        a1 = wrap(ri, 0.0, kq[:, 0:HD], kv1)
        a2 = wrap(rr, PI_2, kq[:, HD:2 * HD], kv2)
        emit("dve", "tensor_scalar", rr, rr, ST[:, 11:12], None, OP.mult,
             waits=[(s_act, a2)])
        emit("dve", "tensor_scalar", ri, ri, ST[:, 11:12], None, OP.mult,
             waits=[(s_act, a1)])
        drain("dve")
        kq_guard[0] = emit("dve", "nop", inc=s_dve)

    def rot_headln(yt, sq, tmp, out_bf, rr, ri):
        y4 = yt.rearrange("p (h j two) -> p h j two", h=H, two=2)
        t4 = tmp.rearrange("p (h j two) -> p h j two", h=H, two=2)
        s4 = sq.rearrange("p (h j two) -> p h j two", h=H, two=2)
        rr4 = rr.unsqueeze(1).broadcast_to([P, H, HD]).rearrange(
            "p h (j two) -> p h j two", two=2)
        ri4 = ri.unsqueeze(1).broadcast_to([P, H, HD]).rearrange(
            "p h (j two) -> p h j two", two=2)
        ye, yo = y4[:, :, :, 0:1], y4[:, :, :, 1:2]
        emit("dve", "tensor_tensor", t4[:, :, :, 0:1], ye, rr4[:, :, :, 0:1],
             OP.mult)
        emit("dve", "tensor_tensor", t4[:, :, :, 1:2], ye, ri4[:, :, :, 0:1],
             OP.mult)
        emit("dve", "tensor_tensor", s4[:, :, :, 0:1], yo, ri4[:, :, :, 1:2],
             OP.mult)
        emit("dve", "tensor_tensor", s4[:, :, :, 1:2], yo, rr4[:, :, :, 1:2],
             OP.mult)
        drain("dve")
        emit("dve", "tensor_tensor", t4[:, :, :, 0:1], t4[:, :, :, 0:1],
             s4[:, :, :, 0:1], OP.subtract)
        emit("dve", "tensor_tensor", t4[:, :, :, 1:2], t4[:, :, :, 1:2],
             s4[:, :, :, 1:2], OP.add)
        drain("dve")
        dv = emit("dve", "nop", inc=s_dve)
        av = emit("act", "activation", sq, tmp, AFT.Square, waits=[(s_dve, dv)],
                  inc=s_act)
        t3 = tmp.rearrange("p (h d) -> p h d", h=H)
        s3 = sq.rearrange("p (h d) -> p h d", h=H)
        emit("dve", "reduce_sum", hl(0).unsqueeze(2), t3, axis=AX.X)
        emit("dve", "reduce_sum", hl(1).unsqueeze(2), s3, axis=AX.X,
             waits=[(s_act, av)])
        drain("dve")
        emit("dve", "tensor_scalar", HL[:, 2 * H:4 * H], HL[:, 0:2 * H],
             1.0 / HD, None, OP.mult)
        drain("dve")
        emit("dve", "tensor_tensor", hl(1), hl(2), hl(2), OP.mult)
        drain("dve")
        emit("dve", "tensor_tensor", hl(5), hl(3), hl(1), OP.subtract)
        drain("dve")
        emit("dve", "tensor_scalar", hl(5), hl(5), EPS, None, OP.add)
        drain("dve")
        dv = emit("dve", "nop", inc=s_dve)
        emit("act", "nop", waits=[(s_dve, dv)])
        rsqrt_ref(hl(3), hl(5), hl(0), hl(1))
        emit("dve", "tensor_tensor", hl(4), hl(2), hl(3), OP.mult)
        drain("dve")
        emit("dve", "tensor_scalar", hl(4), hl(4), -1.0, None, OP.mult)
        drain("dve")
        dv = emit("dve", "nop", inc=s_dve)
        emit("act", "nop", waits=[(s_dve, dv)])
        for h in range(H):
            av = emit("act", "activation", sq[:, h * HD:(h + 1) * HD],
                      tmp[:, h * HD:(h + 1) * HD], AFT.Identity,
                      bias=hl(4)[:, h:h + 1], scale=hl(3)[:, h:h + 1],
                      inc=(s_act if h == H - 1 else None))
        ge = A.b("lnhg_e", HD).unsqueeze(1).broadcast_to([P, H, HD])
        be = A.b("lnhb_e", HD).unsqueeze(1).broadcast_to([P, H, HD])
        emit("dve", "tensor_tensor", t3, s3, ge, OP.mult, waits=[(s_act, av)])
        drain("dve")
        ob3 = out_bf.rearrange("p (h d) -> p h d", h=H)
        emit("dve", "tensor_tensor", ob3, t3, be, OP.add)
        drain("dve")
        return emit("dve", "nop", inc=s_dve)

    def tm_to_fm(src_bf, dst_fn, nch, te_waits=()):
        last = None
        for k0 in range(0, nch, 4):
            kn = min(4, nch - k0)
            pb = 4 + (k0 // 4) % 2
            tv = None
            for kk in range(kn):
                w = list(te_waits) if (k0 == 0 and kk == 0) else []
                if kk == 0:
                    w += bgw(pb)
                tv = emit("te", "transpose", bankb(pb, P, o=kk * P),
                          src_bf[:, (k0 + kk) * P:(k0 + kk + 1) * P], IDENT,
                          waits=w, inc=(s_te if kk == kn - 1 else None))
            for kk in range(kn):
                emit("dve", "tensor_copy", dst_fn(k0 + kk),
                     bankb(pb, P, o=kk * P), waits=[(s_te, tv)] if kk == 0 else [])
            drain("dve")
            last = emit("dve", "nop", inc=s_dve)
            bg[pb] = (s_dve, last)
        return last

    A.alloc("wbig", DCH * D // 2)
    WQ = A.b("wbig", DCH * D)
    srcw = wq_bf[:, :].rearrange("(ki p) n -> p ki n", p=P)
    dstw = WQ.rearrange("p (ki n) -> p ki n", ki=DCH)
    wq_loaded = 0
    for kk in range(0, DCH, 4):
        kn = min(4, DCH - kk)
        wq_loaded = dma("gp", swwq, dstw[:, kk:kk + kn], srcw[:, kk:kk + kn],
                        waits=[(sptw, sptw.n), (s_dve, prep_all)]
                        if kk == 0 else [])

    for nm, u in [("xpipe", D), ("sq", D), ("xnn", D), ("xn_f", D),
                  ("tile_bf", D // 2), ("fm_bf", DCH * P // 2), ("qtm", D),
                  ("rrg", NTG * HD), ("rig", NTG * HD)]:
        A.alloc(nm, u)
    A.alloc_high("kq", 2 * HD)
    xp = A.f("xpipe", D)
    sq_s = A.f("sq", D)
    xnn_s = A.f("xnn", D)
    xnf_s = A.f("xn_f", D)
    tile_b = A.b("tile_bf", D)
    fm_b = A.b("fm_bf", DCH * P)
    fm3 = fm_b.rearrange("p (ki c) -> p ki c", ki=DCH)
    qtm = A.f("qtm", D)
    RRG = A.f("rrg", NTG * HD)
    RIG = A.f("rig", NTG * HD)

    x_guard = 0
    for t in range(NTO):
        xg_prev = x_guard if t > 0 else bc_done
        xv = dma("sp", sx, xp, xow_d[t * P:(t + 1) * P, :],
                 waits=[(s_dve, xg_prev)])
        emit("dve", "nop", waits=[(sx, xv)])
        emit("act", "nop", waits=[(sx, xv), (s_dve, xg_prev)])
        ln_stats(xp, sq_s)
        emit("dve", "tensor_scalar", tile_b, xp, ST[:, 2:3], ST[:, 12:13],
             OP.subtract, OP.mult)
        drain("dve")
        x_guard = emit("dve", "nop", inc=s_dve)
        dv = x_guard
        fmk = tm_to_fm(tile_b, lambda k: fm_b[:, k * P:(k + 1) * P], DCH,
                       te_waits=[(s_dve, dv)])
        dstx = bass.AP(xn_sp.tensor, t * P,
                       [[TLOC, P], [P * TLOC, DCH], [1, P]])
        wv = dma("sp", stw, dstx, fm3, waits=[(s_dve, fmk)])
        emit("dve", "tensor_scalar", tile_b, xp, ST[:, 2:3], ST[:, 14:15],
             OP.subtract, OP.mult, waits=[(stw, wv)])
        drain("dve")
        dv = emit("dve", "nop", inc=s_dve)
        tm_to_fm(tile_b, lambda k: fm_b[:, k * P:(k + 1) * P], DCH,
                 te_waits=[(s_dve, dv)])
        aq_done = emit("dve", "nop", inc=s_dve)
        emit("te", "nop", waits=[(swwq, wq_loaded), (s_dve, aq_done)])
        for ki in range(DCH):
            emit("te", "ldweights", fm_b[:, ki * P:(ki + 1) * P])
            for ci, (co, cw) in enumerate(COGS):
                w = bgw(ci) if ki == 0 else []
                tv = emit("te", "matmul", bank(ci, cw),
                          fm_b[:, ki * P:(ki + 1) * P],
                          WQ[:, ki * D + co: ki * D + co + cw],
                          start=(ki == 0), stop=(ki == DCH - 1), waits=w,
                          inc=(s_te if (ki == DCH - 1 and ci == NCG - 1)
                               else None))
        for ci, (co, cw) in enumerate(COGS):
            emit("dve", "scalar_tensor_tensor", qtm[:, co:co + cw], bank(ci, cw),
                 SCL, A.b("bqs_b", D)[:, co:co + cw], OP.mult, OP.add,
                 waits=[(s_te, tv)] if ci == 0 else [])
        drain("dve")
        dv = emit("dve", "nop", inc=s_dve)
        for ci in range(NCG):
            bg[ci] = (s_dve, dv)
        rot_tables(kqo_d[:, t * HD:(t + 1) * HD],
                   kqo2_d[:, t * HD:(t + 1) * HD],
                   TCO[:, t:t + 1], RRG[:, 0:HD], RIG[:, 0:HD])
        rot_headln(qtm, sq_s, xnn_s, tile_b, RRG[:, 0:HD], RIG[:, 0:HD])
        dv = emit("dve", "nop", inc=s_dve)
        fmk = tm_to_fm(tile_b, lambda k: fm_b[:, k * P:(k + 1) * P], DCH,
                       te_waits=[(s_dve, dv)])
        dstq = bass.AP(qn_sp.tensor, t * P,
                       [[TLOC, P], [P * TLOC, DCH], [1, P]])
        qv = dma("sp", stw, dstq, fm3, waits=[(s_dve, fmk)])
        x_guard = emit("dve", "nop", waits=[(stw, qv)], inc=s_dve)

    A.release("wbig", "qtm")

    q1_te_done = s_te.n
    prep_store_n = 2 * DCH * 16
    A.alloc("wkvh", DCH * D // 2)
    WKVH = A.b("wkvh", DCH * D)
    dstwh = WKVH.rearrange("p (ki n) -> p ki n", ki=DCH)
    A.alloc("akv_fm", DCH * GT // 2)
    akvb = A.b("akv_fm", DCH * GT)
    A.alloc("ktm2", D)
    ktm = A.f("ktm2", D)
    n_ptr = NGRP + 1
    ptr_slice(0, n_ptr)

    wkv_half_loaded = [0, 0]

    def load_wkv_half(half, wguards):
        src = wkv_bf[:, half * D:(half + 1) * D].rearrange("(ki p) n -> p ki n",
                                                           p=P)
        v = 0
        for kk in range(DCH):
            w = list(wguards) if kk == 0 else []
            v = dma("gp", swkv, dstwh[:, kk:kk + 1], src[:, kk:kk + 1], waits=w)
        return v

    x_guard2 = 0
    sq_gate = x_guard
    for g in range(NGRP):
        for t in range(NTG):
            gt = g * NTG + t
            xg_prev = sq_gate
            xv = dma("sp", sx, xp, x_d[gt * P:(gt + 1) * P, :],
                     waits=[(s_dve, xg_prev)])
            emit("dve", "nop", waits=[(sx, xv)])
            emit("act", "nop", waits=[(sx, xv), (s_dve, xg_prev)])
            ln_stats(xp, sq_s)
            emit("dve", "tensor_scalar", tile_b, xp, ST[:, 2:3], ST[:, 14:15],
                 OP.subtract, OP.mult)
            drain("dve")
            x_guard2 = emit("dve", "nop", inc=s_dve)
            emit("dve", "tensor_copy", NMSV[:, t:t + 1], ST[:, 11:12])
            drain("dve")
            rot_tables(kq_d[:, gt * HD:(gt + 1) * HD],
                       kq2_d[:, gt * HD:(gt + 1) * HD],
                       TCOLS[:, gt:gt + 1], RRG[:, t * HD:(t + 1) * HD],
                       RIG[:, t * HD:(t + 1) * HD])
            dv = emit("dve", "nop", inc=s_dve)
            sq_gate = tm_to_fm(tile_b,
                     lambda k, _t=t: akvb[:, k * GT + _t * P:k * GT + (_t + 1) * P],
                     DCH, te_waits=[(s_dve, dv)])
        akv_ready = emit("dve", "nop", inc=s_dve)
        kguard = ([(s_te, s_te.n)] if g > 0
                  else [(s_te, q1_te_done), (sptw, prep_store_n)])
        wkv_half_loaded[0] = load_wkv_half(0, kguard)
        emit("te", "nop", waits=[(swkv, wkv_half_loaded[0]),
                                 (s_dve, akv_ready)])
        for t in range(NTG):
            for ki in range(DCH):
                emit("te", "ldweights",
                     akvb[:, ki * GT + t * P: ki * GT + (t + 1) * P])
                for ci, (co, cw) in enumerate(COGS):
                    w = bgw(ci) if ki == 0 else []
                    tv = emit("te", "matmul", bank(ci, cw),
                              akvb[:, ki * GT + t * P: ki * GT + (t + 1) * P],
                              WKVH[:, ki * D + co: ki * D + co + cw],
                              start=(ki == 0), stop=(ki == DCH - 1), waits=w,
                              inc=(s_te if (ki == DCH - 1 and ci == NCG - 1)
                                   else None))
            for ci, (co, cw) in enumerate(COGS):
                emit("dve", "scalar_tensor_tensor", ktm[:, co:co + cw],
                     bank(ci, cw), SCL, A.b("bkvk_s", D)[:, co:co + cw],
                     OP.mult, OP.add, waits=[(s_te, tv)] if ci == 0 else [])
            drain("dve")
            dv = emit("dve", "nop", inc=s_dve)
            for ci in range(NCG):
                bg[ci] = (s_dve, dv)
            rot_headln(ktm, sq_s, xnn_s, tile_b, RRG[:, t * HD:(t + 1) * HD],
                       RIG[:, t * HD:(t + 1) * HD])
            dv = emit("dve", "nop", inc=s_dve)
            fmk = tm_to_fm(tile_b, lambda k: fm_b[:, k * P:(k + 1) * P], DCH,
                           te_waits=[(s_dve, dv)])
            dstk = bass.AP(st_kn.tensor, g * D * GT + t * P,
                           [[GT, P], [P * GT, DCH], [1, P]])
            kv = dma("sp", stw, dstk, fm3, waits=[(s_dve, fmk)])
            emit("dve", "nop", waits=[(stw, kv)])
        wkv_half_loaded[1] = load_wkv_half(1, [(s_te, s_te.n)])
        emit("te", "nop", waits=[(swkv, wkv_half_loaded[1])])
        for t in range(NTG):
            gt = g * NTG + t
            for ki in range(DCH):
                emit("te", "ldweights",
                     akvb[:, ki * GT + t * P: ki * GT + (t + 1) * P])
                for ci, (co, cw) in enumerate(COGS):
                    w = bgw(ci) if ki == 0 else []
                    tv = emit("te", "matmul", bank(ci, cw),
                              akvb[:, ki * GT + t * P: ki * GT + (t + 1) * P],
                              WKVH[:, ki * D + co: ki * D + co + cw],
                              start=(ki == 0), stop=(ki == DCH - 1), waits=w,
                              inc=(s_te if (ki == DCH - 1 and ci == NCG - 1)
                                   else None))
            for ci, (co, cw) in enumerate(COGS):
                emit("dve", "scalar_tensor_tensor", tile_b[:, co:co + cw],
                     bank(ci, cw), 1.0, A.b("bkvv_b", D)[:, co:co + cw],
                     OP.mult, OP.add, waits=[(s_te, tv)] if ci == 0 else [])
            drain("dve")
            dv = emit("dve", "nop", inc=s_dve)
            for ci in range(NCG):
                bg[ci] = (s_dve, dv)
            dstv = bass.AP(st_v.tensor, gt * P * HD,
                           [[HD, P], [NKC * P * HD, H], [1, HD]])
            t3v = tile_b.rearrange("p (h c) -> p h c", h=H)
            vv = dma("sp", stw, dstv, t3v, waits=[(s_dve, dv)])
            x_guard2 = emit("dve", "nop", waits=[(stw, vv)], inc=s_dve)
            sq_gate = x_guard2
        ptr_slice(g + 1, n_ptr)

    A.release("wkvh", "akv_fm", "ktm2", "xpipe", "sq", "xnn", "xn_f",
              "tile_bf", "fm_bf", "rrg", "rig", "kq",
              "bqs_b", "bkvk_s", "bkvv_b", "lnhg_e", "lnhb_e")
    kv_all_stored = stw.n

    A.alloc("xn_fm", DCH * TLOC // 2)
    XNF = A.b("xn_fm", DCH * TLOC)
    srcx = bass.AP(xn_sp.tensor, 0, [[TLOC, P], [P * TLOC, DCH], [1, TLOC]])
    ph1_te_done = s_te.n
    xn_rl = dma("sp", srd, XNF.rearrange("p (ki n) -> p ki n", ki=DCH), srcx,
                waits=[(stw, stw.n), (s_te, ph1_te_done), (s_dve, s_dve.n),
                       (sptw, sptw.n)])
    A.alloc("m_fm", ICH * TLOC // 2)
    MFM = A.b("m_fm", ICH * TLOC)
    A.alloc("wgu_t", 2 * DCH * P)
    GBUF = A.b("wgu_t", 4 * DCH * P)
    A.alloc("silu_t", TLOC)
    SILU = A.f("silu_t", TLOC)

    gu_guard = [None, None]
    for j in range(ICH):
        lv = 0
        for s_i, blk in enumerate((j, ICH + j)):
            dbuf = GBUF[:, ((j % 2) * 2 + s_i) * DCH * P:
                        ((j % 2) * 2 + s_i + 1) * DCH * P]
            srcb = bass.AP(wguT.tensor, blk * D * P,
                           [[P, P], [P * P, DCH], [1, P]])
            w = []
            if s_i == 0:
                w = [(srd, srd.n)]
            if j < 2:
                w += [(sptw, sptw.n), (s_te, ph1_te_done)]
            elif s_i == 0 and gu_guard[j % 2] is not None:
                w += [gu_guard[j % 2]]
            lv = dma("sp", srd, dbuf.rearrange("p (ki c) -> p ki c", ki=DCH),
                     srcb, waits=w)
        emit("te", "nop", waits=[(srd, lv), (srd, xn_rl)])
        for s_i in range(2):
            dbuf = GBUF[:, ((j % 2) * 2 + s_i) * DCH * P:
                        ((j % 2) * 2 + s_i + 1) * DCH * P]
            for ki in range(DCH):
                emit("te", "ldweights", dbuf[:, ki * P:(ki + 1) * P])
                for bi, (to, tw) in enumerate(TLB):
                    bidx = s_i * 2 + bi
                    w = bgw(bidx) if ki == 0 else []
                    tv = emit("te", "matmul", bank(bidx, tw),
                              dbuf[:, ki * P:(ki + 1) * P],
                              XNF[:, ki * TLOC + to: ki * TLOC + to + tw],
                              start=(ki == 0), stop=(ki == DCH - 1), waits=w,
                              inc=(s_te if (s_i == 1 and ki == DCH - 1
                                            and bi == NTB - 1) else None))
        gu_guard[j % 2] = (s_te, tv)
        for bi, (to, tw) in enumerate(TLB):
            av = emit("act", "activation", SILU[:, to:to + tw], bank(bi, tw),
                      AFT.Silu, waits=[(s_te, tv)] if bi == 0 else [],
                      inc=(s_act if bi == NTB - 1 else None))
        for bi, (to, tw) in enumerate(TLB):
            emit("dve", "tensor_tensor",
                 MFM[:, j * TLOC + to: j * TLOC + to + tw],
                 SILU[:, to:to + tw], bank(2 + bi, tw), OP.mult,
                 waits=[(s_act, av)] if bi == 0 else [])
        drain("dve")
        mk = emit("dve", "nop", inc=s_dve)
        for bidx in range(2 * NTB + 2):
            if bidx < 8:
                bg[bidx] = (s_dve, mk)
    A.release("wgu_t", "silu_t", "xn_fm")
    up_te_done = s_te.n
    up_dve_done = s_dve.n

    A.alloc("wdn_t", 2 * D)
    WDN = A.b("wdn_t", 2 * D)
    A.alloc("ffn_t", D)
    FFT = A.f("ffn_t", D)
    dn_guard = [None, None]
    out_stash = 0
    for tp in range(0, NTO, 2):
        ntp = min(2, NTO - tp)
        for i in range(ICH):
            dbuf = WDN[:, (i % 2) * D:(i % 2) * D + D]
            w = [(srd, srd.n)]
            if dn_guard[i % 2] is not None:
                w.append(dn_guard[i % 2])
            if tp == 0 and i < 2:
                w += [(sptw, sptw.n), (s_te, up_te_done), (s_dve, up_dve_done)]
            lv = dma("sp", srd, dbuf, wdnT[i * P:(i + 1) * P, :], waits=w)
            emit("te", "nop", waits=[(srd, lv)])
            for tti in range(ntp):
                tt = tp + tti
                emit("te", "ldweights",
                     MFM[:, i * TLOC + tt * P: i * TLOC + (tt + 1) * P])
                for ci, (co, cw) in enumerate(COGS):
                    bidx = tti * NCG + ci
                    w2 = bgw(bidx) if i == 0 else []
                    tvd = emit("te", "matmul", bank(bidx, cw),
                               MFM[:, i * TLOC + tt * P: i * TLOC + (tt + 1) * P],
                               WDN[:, (i % 2) * D + co:(i % 2) * D + co + cw],
                               start=(i == 0), stop=(i == ICH - 1), waits=w2,
                               inc=(s_te if (tti == ntp - 1 and ci == NCG - 1)
                                    else None))
            dn_guard[i % 2] = (s_te, tvd)
        tv = dn_guard[(ICH - 1) % 2][1]
        for tti in range(ntp):
            tt = tp + tti
            for ci, (co, cw) in enumerate(COGS):
                bidx = tti * NCG + ci
                emit("dve", "scalar_tensor_tensor", FFT[:, co:co + cw],
                     bank(bidx, cw), 1.0, A.b("bo_b", D)[:, co:co + cw],
                     OP.mult, OP.add,
                     waits=([(s_te, tv), (stw, out_stash)]
                            if (tti == 0 and ci == 0) else []))
            drain("dve")
            dv = emit("dve", "nop", inc=s_dve)
            out_stash = dma("sp", stw, ffn_sp[tt * P:(tt + 1) * P, :], FFT,
                            waits=[(s_dve, dv)])
            mk = emit("dve", "nop", waits=[(stw, out_stash)], inc=s_dve)
            for ci in range(NCG):
                bg[tti * NCG + ci] = (s_dve, mk)
    A.release("wdn_t", "ffn_t", "m_fm")

    A.alloc("qn_fm", DCH * TLOC // 2)
    QNF = A.b("qn_fm", DCH * TLOC)
    srcq = bass.AP(qn_sp.tensor, 0, [[TLOC, P], [P * TLOC, DCH], [1, TLOC]])
    ph2_te_done = s_te.n
    qn_rl = dma("sp", srd, QNF.rearrange("p (ki n) -> p ki n", ki=DCH), srcq,
                waits=[(s_te, ph2_te_done), (s_dve, s_dve.n)])
    A.alloc("kn_h", NKC * P)
    KNB = A.b("kn_h", 2 * NKC * P)
    A.alloc("v_h", NKC * HD)
    VB = A.b("v_h", 2 * NKC * HD)
    A.alloc("p_buf", NKC * QT // 2)
    PBUF = A.b("p_buf", NKC * QT)
    A.alloc("attn_fm", DCH * TLOC // 2)
    ATTN = A.b("attn_fm", DCH * TLOC)
    A.alloc("lrec", QT)
    LREC = A.f("lrec", QT)

    kn_loads = [0] * H
    head_done = [None] * (H + 2)

    def load_head(h):
        par = h % 2
        w = [(stw, kv_all_stored), (s_te, ph2_te_done), (srd, srd.n)]
        if h >= 2 and head_done[h - 2] is not None:
            w = [head_done[h - 2], (srd, srd.n)]
        last = 0
        for g in range(NGRP):
            last = dma("sp", srd,
                       KNB[:, par * NKC * P + g * GT:par * NKC * P + (g + 1) * GT],
                       st_kn[g * D + h * P: g * D + (h + 1) * P, :],
                       waits=w if g == 0 else [])
        srcv = bass.AP(st_v.tensor, h * NKC * P * HD,
                       [[HD, P], [P * HD, NKC], [1, HD]])
        last = dma("sp", srd,
                   VB[:, par * NKC * HD:(par + 1) * NKC * HD]
                   .rearrange("p (kc c) -> p kc c", kc=NKC), srcv)
        kn_loads[h] = last

    load_head(0)
    if H > 1:
        load_head(1)
    emit("act", "nop", waits=[(s_dve, negb_done)])
    for h in range(H):
        par = h % 2
        kn_h = KNB[:, par * NKC * P:(par + 1) * NKC * P]
        v_h = VB[:, par * NKC * HD:(par + 1) * NKC * HD]
        emit("te", "nop", waits=[(srd, kn_loads[h]), (srd, qn_rl),
                                 (s_dve, negb_done)])
        for qi in range(NQT):
            q0 = qi * QT
            rhs_q = QNF[:, h * TLOC + q0: h * TLOC + q0 + QT]
            exp_marks = []
            QKB = (0, 1, 4, 5)
            for c0 in range(0, NKC, 2):
                b_i = QKB[(c0 // 2) % 4]
                npair = min(2, NKC - c0)
                for cc in range(npair):
                    w = bgw(b_i) if cc == 0 else []
                    emit("te", "ldweights",
                         kn_h[:, (c0 + cc) * P:(c0 + cc + 1) * P])
                    tv = emit("te", "matmul", bank(b_i, QT, o=cc * QT),
                              kn_h[:, (c0 + cc) * P:(c0 + cc + 1) * P], rhs_q,
                              start=True, stop=True, waits=w,
                              inc=(s_te if cc == npair - 1 else None))
                av = emit("act", "activation",
                          PBUF[:, c0 * QT:(c0 + npair) * QT],
                          bank(b_i, npair * QT), AFT.Exp, bias=NEGB, scale=1.0,
                          waits=[(s_te, tv)], inc=s_act)
                exp_marks.append(av)
                bg[b_i] = (s_act, av)
            for c in range(NKC):
                w = [(s_act, exp_marks[c // 2])]
                if c == 0:
                    w += bgw(2)
                emit("te", "ldweights", v_h[:, c * HD:(c + 1) * HD], waits=w)
                av_tv = emit("te", "matmul", bank(2, QT),
                             v_h[:, c * HD:(c + 1) * HD],
                             PBUF[:, c * QT:(c + 1) * QT],
                             start=(c == 0), stop=(c == NKC - 1),
                             inc=(s_te if c == NKC - 1 else None))
            emit("te", "ldweights", ONES, waits=bgw(3))
            for c in range(NKC):
                l_tv = emit("te", "matmul", bank(3, QT)[0:1, :], ONES,
                            PBUF[:, c * QT:(c + 1) * QT],
                            start=(c == 0), stop=(c == NKC - 1),
                            inc=(s_te if c == NKC - 1 else None))
            emit("dve", "tensor_copy", LROW[0:1, :], bank(3, QT)[0:1, :],
                 waits=[(s_te, l_tv)])
            drain("dve")
            emit("dve", "reciprocal", LROW[0:1, :], LROW[0:1, :])
            drain("dve")
            dv = emit("dve", "nop", inc=s_dve)
            bg[3] = (s_dve, dv)
            tv = emit("te", "matmul", bank(6, QT), ONESF[0:1, :],
                      LROW[0:1, :], start=True, stop=True,
                      waits=[(s_dve, dv)] + bgw(6), inc=s_te)
            emit("dve", "tensor_copy", LREC, bank(6, QT), waits=[(s_te, tv)])
            drain("dve")
            lb = emit("dve", "nop", inc=s_dve)
            bg[6] = (s_dve, lb)
            emit("dve", "tensor_tensor",
                 ATTN[:, h * TLOC + q0: h * TLOC + q0 + QT], bank(2, QT), LREC,
                 OP.mult, waits=[(s_te, av_tv)])
            drain("dve")
            bg[2] = (s_dve, emit("dve", "nop", inc=s_dve))
        head_done[h] = (s_te, l_tv)
        if h + 2 < H:
            load_head(h + 2)
    attn_done = emit("dve", "nop", inc=s_dve)
    A.release("kn_h", "v_h", "p_buf", "qn_fm", "lrec")

    A.alloc("wo_r", DCH * D // 2)
    WOB = A.b("wo_r", DCH * D)
    srcw2 = wo_d[:, :].rearrange("(ki p) n -> p ki n", p=P)
    dstw2 = WOB.rearrange("p (ki n) -> p ki n", ki=DCH)
    wo_loaded = 0
    for kk in range(0, DCH, 4):
        kn = min(4, DCH - kk)
        wo_loaded = dma("gp", swwo, dstw2[:, kk:kk + kn], srcw2[:, kk:kk + kn],
                        waits=[(s_te, head_done[H - 1][1] if head_done[H - 1]
                                else 0), (s_dve, attn_done)] if kk == 0 else [])
    A.alloc("ffn_in", D)
    FIN = A.f("ffn_in", D)
    A.alloc("x_in", D)
    XIN = A.f("x_in", D)
    A.alloc("out_t", D)
    OUTT = A.f("out_t", D)
    emit("te", "nop", waits=[(swwo, wo_loaded), (s_dve, attn_done)])
    out_guard = 0
    for tt in range(NTO):
        fv = dma("sp", srd, FIN, ffn_sp[tt * P:(tt + 1) * P, :],
                 waits=[(s_dve, out_guard if tt > 0 else attn_done)])
        xv = dma("sp", srd, XIN, xow_d[tt * P:(tt + 1) * P, :])
        for ki in range(DCH):
            emit("te", "ldweights",
                 ATTN[:, ki * TLOC + tt * P: ki * TLOC + (tt + 1) * P])
            for ci, (co, cw) in enumerate(COGS):
                w = bgw(ci) if ki == 0 else []
                tv = emit("te", "matmul", bank(ci, cw),
                          ATTN[:, ki * TLOC + tt * P: ki * TLOC + (tt + 1) * P],
                          WOB[:, ki * D + co: ki * D + co + cw],
                          start=(ki == 0), stop=(ki == DCH - 1), waits=w,
                          inc=(s_te if (ki == DCH - 1 and ci == NCG - 1)
                               else None))
        for ci, (co, cw) in enumerate(COGS):
            w = [(s_te, tv), (srd, xv)] if ci == 0 else []
            emit("dve", "tensor_tensor", OUTT[:, co:co + cw], bank(ci, cw),
                 FIN[:, co:co + cw], OP.add, waits=w)
        drain("dve")
        emit("dve", "tensor_tensor", OUTT, OUTT, XIN, OP.add,
             waits=[(srd, xv)])
        drain("dve")
        dv = emit("dve", "nop", inc=s_dve)
        for ci in range(NCG):
            bg[ci] = (s_dve, dv)
        ov = dma("sp", stw, y_d[tt * P:(tt + 1) * P, :], OUTT,
                 waits=[(s_dve, dv)])
        out_guard = emit("dve", "nop", waits=[(stw, ov)], inc=s_dve)

    if dbg:
        dbgs = [("d_xn", xn_sp[:, :]), ("d_qn", qn_sp[:, :]),
                ("d_kn", st_kn[0:D, :]), ("d_v", st_v[0:NKC * P, :]),
                ("d_ffn", ffn_sp[:, :]), ("d_wgu", wguT[0:D, :]),
                ("d_wdn", wdnT[0:P, :])]
        for nm, srcd in dbgs:
            od = nc.dram_tensor(nm, list(srcd.shape),
                                srcd.dtype, kind="ExternalOutput").ap()
            dma("gp", swgp, od[:, :], srcd,
                waits=[(stw, stw.n), (sptw, sptw.n), (srd, srd.n)])
        emit("gp", "nop", waits=[(swgp, swgp.n)])
    emit("sp", "nop", waits=[(stw, stw.n)])
    emit("te", "nop", waits=[(stw, stw.n)])
    emit("gp", "nop", waits=[(stw, stw.n)])
    emit("act", "nop", waits=[(stw, stw.n)])

    def render(q):
        def body(eng):
            for meth, args, kw, waits, incs in Q[q]:
                for s, v in waits:
                    eng.wait_ge(s, v)
                if meth == "nop":
                    if incs:
                        ins = eng.nop()
                        for s, k in incs:
                            ins.then_inc(s, k)
                    continue
                ins = getattr(eng, meth)(*args, **kw)
                if ins is not None:
                    for s, k in incs:
                        ins.then_inc(s, k)
        return body

    with nc.Block() as blk:
        blk.tensor(render("te"))
        blk.scalar(render("act"))
        blk.vector(render("dve"))
        blk.sync(render("sp"))
        blk.gpsimd(render("gp"))
    ctx.close()
    return nc


def _consts(cfg: Cfg):
    f = ref_freqs(cfg.HD)
    fdup = np.repeat(f, 2)[None, :].repeat(P, 0).astype(np.float32)
    tcols = (np.arange(cfg.T, dtype=np.float32).reshape(cfg.T // P, P).T)
    return (np.ascontiguousarray(fdup), np.ascontiguousarray(tcols),
            np.eye(P).astype(ml_dtypes.bfloat16),
            np.ones((P, 1), ml_dtypes.bfloat16))


def _kq_tables(cfg: Cfg, t0, t1):
    f = ref_freqs(cfg.HD)
    fdup = np.repeat(f, 2).astype(np.float32)
    t_idx = np.arange(t0, t1, dtype=np.float32)
    ang32 = (t_idx[:, None] * fdup[None, :]).astype(np.float32)
    k1 = np.floor(ang32.astype(np.float64) / (2 * np.pi))
    k2 = np.floor((ang32.astype(np.float64) + np.float64(np.float32(np.pi / 2)))
                  / (2 * np.pi))
    nt = (t1 - t0) // P

    def lay(k):
        k = k.astype(np.float32).reshape(nt, P, cfg.HD)
        return np.ascontiguousarray(k.transpose(1, 0, 2).reshape(P, nt * cfg.HD))

    return lay(k1), lay(k2)


def _bc(v):
    return np.ascontiguousarray(
        np.broadcast_to(np.asarray(v, np.float32).reshape(1, -1), (P, len(np.asarray(v).reshape(-1)))))


def run(cfg: Cfg, inputs: dict, runner=None, dbg=False):
    nc = build(cfg, dbg=dbg)
    fdup, tcols, ident, ones = _consts(cfg)
    x = np.ascontiguousarray(np.asarray(inputs["L0_attn_x"], np.float32)
                             .reshape(cfg.T, cfg.D))
    base = {
        "x_full": x,
        "Wq": np.ascontiguousarray(np.asarray(inputs["L0_attn_Wq"], np.float32)),
        "Wkv": np.ascontiguousarray(np.asarray(inputs["L0_attn_Wkv"], np.float32)),
        "Wo": np.ascontiguousarray(np.asarray(inputs["L0_attn_Wo"], np.float32)),
        "Wgu": np.ascontiguousarray(np.asarray(inputs["L0_ffn_w_gate_up"],
                                               np.float32)),
        "Wdn": np.ascontiguousarray(np.asarray(inputs["L0_ffn_w_down"],
                                               np.float32)),
        "ln_q_g": _bc(inputs["L0_attn_ln_q_g"]),
        "ln_q_b": _bc(inputs["L0_attn_ln_q_b"]),
        "ln_kv_g": _bc(inputs["L0_attn_ln_kv_g"]),
        "ln_kv_b": _bc(inputs["L0_attn_ln_kv_b"]),
        "ln_h_g": _bc(inputs["L0_attn_ln_h_g"]),
        "ln_h_b": _bc(inputs["L0_attn_ln_h_b"]),
        "bq": _bc(inputs["L0_attn_bq"]),
        "bkv": _bc(inputs["L0_attn_bkv"]),
        "bo": _bc(inputs["L0_attn_bo"]),
        "ident_c": ident, "ones_c": ones, "fdup_c": fdup, "tcols_c": tcols,
        "onesf_c": np.ones((1, P), np.float32),
    }
    kq1, kq2 = _kq_tables(cfg, 0, cfg.T)
    base["kq_c"], base["kq2_c"] = kq1, kq2

    def _col(v):
        return np.ascontiguousarray(
            np.asarray(v, np.float32).reshape(cfg.DCH, P).T)
    base["gqc_c"] = _col(inputs["L0_attn_ln_q_g"])
    base["bqc_c"] = _col(inputs["L0_attn_ln_q_b"])
    base["gkc_c"] = _col(inputs["L0_attn_ln_kv_g"])
    base["bkc_c"] = _col(inputs["L0_attn_ln_kv_b"])
    base = {k: np.ascontiguousarray(v) for k, v in base.items()}
    in_maps = []
    tl = cfg.TLOC
    for c in range(NCORES):
        m = dict(base)
        m["x_own"] = np.ascontiguousarray(x[c * tl:(c + 1) * tl, :])
        m["tcown_c"] = np.ascontiguousarray(
            np.arange(c * tl, (c + 1) * tl, dtype=np.float32)
            .reshape(tl // P, P).T)
        m["kqo_c"], m["kqo2_c"] = _kq_tables(cfg, c * tl, (c + 1) * tl)
        in_maps.append(m)
    if runner is None:
        res = run_bass_kernel_spmd(nc, in_maps, core_ids=list(range(NCORES)))
        outs = [res.results[c]["y"] for c in range(NCORES)]
        if dbg:
            y = np.concatenate([np.asarray(o, np.float32) for o in outs], axis=0)
            return y.reshape(1, cfg.T, cfg.D), res.results
    else:
        outs = runner(nc, in_maps)
    y = np.concatenate([np.asarray(o, np.float32) for o in outs], axis=0)
    return y.reshape(1, cfg.T, cfg.D)


def kernel(**inputs):
    return run(FULL, inputs)


# version 09
# speedup vs baseline: 1.1021x; 1.0082x over previous
from contextlib import ExitStack
from dataclasses import dataclass

import numpy as np
import ml_dtypes

import concourse.bass as bass
import concourse.mybir as mybir
from concourse.bass_utils import run_bass_kernel_spmd
from concourse.alu_op_type import AluOpType as OP
F32 = mybir.dt.float32
BF16 = mybir.dt.bfloat16
AFT = mybir.ActivationFunctionType
AX = mybir.AxisListType

NCORES = 8
P = 128
EPS = 1e-5


@dataclass
class Cfg:
    D: int = 2048
    T: int = 8192
    H: int = 16
    HD: int = 128
    INTER: int = 8192
    QT: int = 256

    @property
    def TLOC(self): return self.T // NCORES
    @property
    def DCH(self): return self.D // P
    @property
    def ICH(self): return self.INTER // P
    @property
    def GT(self): return self.T // 8
    @property
    def NTG(self): return self.GT // P
    @property
    def NTO(self): return self.TLOC // P
    @property
    def NKC(self): return self.T // P
    @property
    def NQT(self): return self.TLOC // self.QT
    @property
    def COGS(self): return [(o, min(512, self.D - o)) for o in range(0, self.D, 512)]
    @property
    def TLB(self): return [(o, min(512, self.TLOC - o))
                           for o in range(0, self.TLOC, 512)]


FULL = Cfg()
MINI = Cfg(D=256, T=2048, H=2, HD=128, INTER=1024, QT=256)

SCL = float(np.float32(128.0 ** -0.25))
TWO_PI = float(np.float64(2 * np.pi))
INV_2PI = float(np.float32(1.0 / (2 * np.pi)))
_c1 = np.float32(6.28125)
_c2 = np.float32(np.float64(TWO_PI) - np.float64(_c1))
_c3 = np.float32(np.float64(TWO_PI) - np.float64(_c1) - np.float64(_c2))
C1, C2, C3 = float(_c1), float(_c2), float(_c3)
PI = float(np.float32(np.pi))
PI_2 = float(np.float32(np.pi / 2))


def ref_freqs(hd):
    lin = np.linspace(np.float32(0.0), np.float32(1.0), hd // 2, dtype=np.float32)
    f = np.float32(40.0) ** lin
    f = f * np.float32(200.0) / np.float32(1000.0)
    return (np.float32(200.0) * f / np.float32(1000.0)).astype(np.float32)


class Sem:
    def __init__(self, h):
        self.h, self.n = h, 0

    def inc(self, k=1):
        self.n += k
        return self.n


class Arena:
    def __init__(self, ap, total):
        self.ap, self.total = ap, total
        self.free = [(0, total)]
        self.live = {}

    def alloc(self, name, units):
        units = (units + 7) & ~7
        best = None
        for i, (off, sz) in enumerate(self.free):
            if sz >= units and (best is None or sz < self.free[best][1]):
                best = i
        if best is None:
            raise RuntimeError(f"arena OOM {name} needs {units} free={self.free} "
                               f"live={ {k: v[1] for k, v in self.live.items()} }")
        off, sz = self.free[best]
        self.free[best] = (off + units, sz - units)
        if self.free[best][1] == 0:
            self.free.pop(best)
        self.live[name] = (off, units)
        return self

    def alloc_high(self, name, units):
        units = (units + 7) & ~7
        for i in range(len(self.free) - 1, -1, -1):
            off, sz = self.free[i]
            if sz >= units:
                self.free[i] = (off, sz - units)
                if self.free[i][1] == 0:
                    self.free.pop(i)
                self.live[name] = (off + sz - units, units)
                return self
        raise RuntimeError(f"arena OOM(high) {name} needs {units}")

    def release(self, *names):
        for name in names:
            off, sz = self.live.pop(name)
            self.free.append((off, sz))
        self.free.sort()
        m = []
        for o, s in self.free:
            if m and m[-1][0] + m[-1][1] == o:
                m[-1][1] += s
            else:
                m.append([o, s])
        self.free = [tuple(x) for x in m]

    def f(self, name, cols=None):
        off, sz = self.live[name]
        return self.ap[:, off:off + (sz if cols is None else cols)]

    def b(self, name, cols=None):
        off, sz = self.live[name]
        v = self.ap[:, off:off + sz].bitcast(BF16)
        return v if cols is None else v[:, 0:cols]


def build(cfg: Cfg, dbg: bool = False):
    D, T, H, HD, INTER, QT = cfg.D, cfg.T, cfg.H, cfg.HD, cfg.INTER, cfg.QT
    DCH, ICH, GT, NTG, NTO = cfg.DCH, cfg.ICH, cfg.GT, cfg.NTG, cfg.NTO
    NKC, NQT, TLOC = cfg.NKC, cfg.NQT, cfg.TLOC
    COGS, TLB = cfg.COGS, cfg.TLB
    NCG, NTB = len(COGS), len(TLB)
    NGRP = T // GT

    nc = bass.Bass("TRN2", target_bir_lowering=False)

    x_d = nc.dram_tensor("x_full", [T, D], F32, kind="ExternalInput").ap()
    xow_d = nc.dram_tensor("x_own", [TLOC, D], F32, kind="ExternalInput").ap()
    wq_d = nc.dram_tensor("Wq", [D, D], F32, kind="ExternalInput").ap()
    wkv_d = nc.dram_tensor("Wkv", [D, 2 * D], F32, kind="ExternalInput").ap()
    wo_d = nc.dram_tensor("Wo", [D, D], F32, kind="ExternalInput").ap()
    wgu_d = nc.dram_tensor("Wgu", [2 * INTER, D], F32, kind="ExternalInput").ap()
    wdn_d = nc.dram_tensor("Wdn", [D, INTER], F32, kind="ExternalInput").ap()
    V = {}
    for nm, n in [("ln_q_g", D), ("ln_q_b", D), ("ln_kv_g", D), ("ln_kv_b", D),
                  ("ln_h_g", HD), ("ln_h_b", HD), ("bq", D), ("bkv", 2 * D),
                  ("bo", D)]:
        V[nm] = nc.dram_tensor(nm, [P, n], F32, kind="ExternalInput").ap()
    ident_d = nc.dram_tensor("ident_c", [P, P], BF16, kind="ExternalInput").ap()
    ones_d = nc.dram_tensor("ones_c", [P, 1], BF16, kind="ExternalInput").ap()
    onesf_d = nc.dram_tensor("onesf_c", [1, P], F32, kind="ExternalInput").ap()
    fdup_d = nc.dram_tensor("fdup_c", [P, HD], F32, kind="ExternalInput").ap()
    tcols_d = nc.dram_tensor("tcols_c", [P, T // P], F32, kind="ExternalInput").ap()
    tco_d = nc.dram_tensor("tcown_c", [P, NTO], F32, kind="ExternalInput").ap()
    kq_d = nc.dram_tensor("kq_c", [P, (T // P) * HD], F32,
                          kind="ExternalInput").ap()
    kq2_d = nc.dram_tensor("kq2_c", [P, (T // P) * HD], F32,
                           kind="ExternalInput").ap()
    kqo_d = nc.dram_tensor("kqo_c", [P, NTO * HD], F32,
                           kind="ExternalInput").ap()
    kqo2_d = nc.dram_tensor("kqo2_c", [P, NTO * HD], F32,
                            kind="ExternalInput").ap()
    gqc_d = nc.dram_tensor("gqc_c", [P, DCH], F32, kind="ExternalInput").ap()
    bqc_d = nc.dram_tensor("bqc_c", [P, DCH], F32, kind="ExternalInput").ap()
    gkc_d = nc.dram_tensor("gkc_c", [P, DCH], F32, kind="ExternalInput").ap()
    bkc_d = nc.dram_tensor("bkc_c", [P, DCH], F32, kind="ExternalInput").ap()
    y_d = nc.dram_tensor("y", [TLOC, D], F32, kind="ExternalOutput").ap()

    wq_bf = nc.dram_tensor("wq_bf", [D, D], BF16).ap()
    wkv_bf = nc.dram_tensor("wkv_bf", [D, 2 * D], BF16).ap()
    wgu_bf = nc.dram_tensor("wgu_bf", [2 * INTER, D], BF16).ap()
    wdn_bf = nc.dram_tensor("wdn_bf", [D, INTER], BF16).ap()
    wguT = nc.dram_tensor("wguT", [2 * ICH * D, P], BF16).ap()
    wdnT = nc.dram_tensor("wdnT", [ICH * P, D], BF16).ap()
    st_kn = nc.dram_tensor("st_kn", [NGRP * D, GT], BF16).ap()
    st_v = nc.dram_tensor("st_v", [H * NKC * P, HD], BF16).ap()
    qn_sp = nc.dram_tensor("qn_sp", [D, TLOC], BF16).ap()
    xn_sp = nc.dram_tensor("xn_sp", [D, TLOC], BF16).ap()
    ffn_sp = nc.dram_tensor("ffn_sp", [TLOC, D], F32).ap()

    ctx = ExitStack()
    AR_U = 50600
    arena_t = ctx.enter_context(nc.sbuf_tensor("arena", [P, AR_U], F32))
    A = Arena(arena_t[:], AR_U)
    small_t = ctx.enter_context(nc.sbuf_tensor("small", [P, 1280], F32))
    SM = small_t[:]
    psum_t = ctx.enter_context(nc.psum_tensor("psum", [P, 4096], F32))
    PS, PSB = psum_t[:], psum_t[:].bitcast(BF16)
    ident_t = ctx.enter_context(nc.sbuf_tensor("identS", [P, P], BF16))
    IDENT = ident_t[:]
    ones_t = ctx.enter_context(nc.sbuf_tensor("onesS", [P, 1], BF16))
    ONES = ones_t[:]
    onesf_t = ctx.enter_context(nc.sbuf_tensor("onesF", [1, P], F32))
    ONESF = onesf_t[:]
    trt_t = ctx.enter_context(nc.sbuf_tensor("trtS", [P, 2048], BF16,
                                             is_transpose_dest=True))
    TRT = trt_t[:]

    def bank(b, w=512, o=0):
        return PS[:, b * 512 + o: b * 512 + o + w]

    def bankb(b, w=1024, o=0):
        return PSB[:, b * 1024 + o: b * 1024 + o + w]

    sems = {n: Sem(ctx.enter_context(nc.semaphore(n))) for n in
            ("sx", "stw", "srd", "swgp", "swff", "ste", "sact", "sdve", "sgp",
             "swcast", "swwq", "swkv", "swwo", "sptw", "skq", "swgp2",
             "sx2", "skq2")}
    sx, stw, srd, swgp, swff = (sems[n] for n in ("sx", "stw", "srd", "swgp",
                                                  "swff"))
    s_te, s_act, s_dve, s_gp = (sems[n] for n in ("ste", "sact", "sdve", "sgp"))
    swcast, swwq, swkv, swwo, sptw, skq, swgp2 = (sems[n] for n in
                                                  ("swcast", "swwq", "swkv",
                                                   "swwo", "sptw", "skq",
                                                   "swgp2"))
    sxp = [sx, sems["sx2"]]
    skqp = [skq, sems["skq2"]]

    Q = {k: [] for k in ("te", "act", "dve", "sp", "gp")}

    def emit(q, meth, *args, waits=(), inc=None, inc16=None, **kw):
        r = None
        incs = []
        if inc is not None:
            r = inc.inc()
            incs.append((inc.h, 1))
        if inc16 is not None:
            r = inc16.inc(16)
            incs.append((inc16.h, 16))
        ws = [(s.h, v) for (s, v) in waits if s is not None and v and v > 0]
        Q[q].append((meth, args, kw, ws, incs))
        return r

    def drain(q):
        emit(q, "drain")

    def dma(q, sem, dst, src, waits=()):
        return emit(q, "dma_start", waits=waits, inc16=sem, out=dst, in_=src)

    def dma_tr(q, sem, dst, src, waits=()):
        return emit(q, "dma_start_transpose", dst, src, waits=waits, inc16=sem)

    bg = [None] * 8

    def bgw(b):
        return [] if bg[b] is None else [bg[b]]

    _smo = [0]

    def sm(n):
        o = _smo[0]
        _smo[0] += max(n, 1)
        assert _smo[0] <= 1280, _smo[0]
        return SM[:, o:o + n]

    FDUP = sm(HD)
    TCOLS = sm(T // P)
    TCO = sm(NTO)
    ST2 = sm(32)
    HL2 = sm(12 * H)
    RED1 = sm(2)
    BBFr = sm(8)
    NEGB = sm(1)
    NEGPI = sm(1)
    LROW = sm(QT)
    ANG2 = sm(2 * HD)
    WRK2X = sm(2 * HD)

    BBF = BBFr.bitcast(BF16)

    dma("sp", sx, IDENT, ident_d[:, :])
    dma("sp", sx, ONES, ones_d[:, :])
    dma("sp", sx, ONESF, onesf_d[:, :])
    dma("sp", sx, FDUP, fdup_d[:, :])
    dma("sp", sx, TCOLS, tcols_d[:, :])
    dma("sp", sx, TCO, tco_d[:, :])
    emit("dve", "memset", NEGPI, -PI)
    drain("dve")

    A.alloc("vrow", 2 * D)
    vrow = A.f("vrow")
    bc_specs = [("bqs_b", V["bq"][:, :], D, SCL),
                ("bkvk_s", V["bkv"][:, 0:D], D, SCL),
                ("bkvv_b", V["bkv"][:, D:2 * D], D, None),
                ("lnhg_e", V["ln_h_g"][:, :], HD, SCL),
                ("lnhb_e", V["ln_h_b"][:, :], HD, SCL),
                ("bo_b", V["bo"][:, :], D, None)]
    last_bc = 0
    for nm, srcv, n, scale in bc_specs:
        A.alloc(nm, (n + 1) // 2)
        vv = dma("gp", swgp, vrow[:, 0:n], srcv, waits=[(s_dve, last_bc)])
        if scale is None:
            last_bc = emit("dve", "tensor_copy", A.b(nm, n), vrow[:, 0:n],
                           waits=[(swgp, vv)], inc=s_dve)
        else:
            last_bc = emit("dve", "tensor_scalar_mul", A.b(nm, n), vrow[:, 0:n],
                           float(scale), waits=[(swgp, vv)], inc=s_dve)
        drain("dve")
    bc_done = last_bc
    av = emit("act", "activation", vrow[:, 0:HD], A.b("lnhg_e", HD), AFT.Square,
              waits=[(s_dve, last_bc)], inc=s_act)
    emit("dve", "reduce_max", RED1[:, 0:1], vrow[:, 0:HD], axis=AX.X,
         waits=[(s_act, av)])
    drain("dve")
    dv = emit("dve", "nop", inc=s_dve)
    av = emit("act", "activation", vrow[:, 0:HD], A.b("lnhb_e", HD), AFT.Square,
              waits=[(s_dve, dv)], inc=s_act)
    emit("dve", "reduce_sum", RED1[:, 1:2], vrow[:, 0:HD], axis=AX.X,
         waits=[(s_act, av)])
    drain("dve")
    dv = emit("dve", "nop", inc=s_dve)
    av = emit("act", "activation", BBFr[:, 2:4], RED1[:, 0:2], AFT.Sqrt,
              waits=[(s_dve, dv)], inc=s_act)
    emit("dve", "tensor_scalar", BBFr[:, 0:1], BBFr[:, 2:3],
         float(np.sqrt(HD)), None, OP.mult, waits=[(s_act, av)])
    drain("dve")
    emit("dve", "tensor_tensor", BBFr[:, 0:1], BBFr[:, 0:1], BBFr[:, 3:4], OP.add)
    drain("dve")
    emit("dve", "tensor_tensor", BBFr[:, 1:2], BBFr[:, 0:1], BBFr[:, 0:1], OP.mult)
    drain("dve")
    emit("dve", "tensor_scalar", NEGB, BBFr[:, 1:2], -1.0001, None, OP.mult)
    drain("dve")
    negb_done = emit("dve", "nop", inc=s_dve)

    A.alloc("wprep", 2 * D)
    A.alloc("gcolp", 2 * DCH + 4)
    WPREP = A.b("wprep", 4 * D)
    GCOL = A.f("gcolp", 2 * DCH)
    BCOLB = A.f("gcolp")[:, 2 * DCH:2 * DCH + 2].bitcast(BF16)
    gl1 = dma("sp", sx, GCOL[:, 0:DCH], gqc_d[:, :])
    gl2 = dma("sp", sx, GCOL[:, DCH:2 * DCH], gkc_d[:, :])
    A.alloc("bcolf", DCH * 2)
    BCOLF = A.f("bcolf", 2 * DCH)
    bl1 = dma("sp", sx, BCOLF[:, 0:DCH], bqc_d[:, :])
    bl2 = dma("sp", sx, BCOLF[:, DCH:2 * DCH], bkc_d[:, :])
    emit("dve", "nop", waits=[(sx, bl2)])
    prep_specs = [("q", wq_d, wq_bf, D, 0), ("kv", wkv_d, wkv_bf, 2 * D, 1)]
    A.alloc("brow", 3 * D)
    BROW = A.f("brow", 3 * D)
    prep_done = {}
    for nm, wsrc, wdst, cols, gi in prep_specs:
        ncog = cols // 512 if cols >= 512 else 1
        cw0 = min(512, cols)
        pg = None
        for ki in range(DCH):
            buf = WPREP[:, (ki % 2) * 2 * D:(ki % 2) * 2 * D + cols]
            w = [(s_dve, bc_done)] if ki == 0 and nm == "q" else []
            w.append((sptw, sptw.n))
            psem = swgp if ki % 2 == 0 else swgp2
            cv = dma("gp", psem, buf, wsrc[ki * P:(ki + 1) * P, :], waits=w)
            bv = emit("dve", "tensor_copy", BCOLB[:, gi:gi + 1],
                      BCOLF[:, gi * DCH + ki:gi * DCH + ki + 1],
                      waits=[(psem, cv)], inc=s_dve)
            drain("dve")
            for cg in range(ncog):
                tvp = emit("te", "matmul",
                           bank(cg % 4 + (4 if cg >= 4 else 0), cw0)[0:1, :],
                           BCOLB[:, gi:gi + 1], buf[:, cg * cw0:(cg + 1) * cw0],
                           start=(ki == 0), stop=(ki == DCH - 1),
                           waits=[(s_dve, bv)] if cg == 0 else [],
                           inc=(s_te if cg == ncog - 1 else None))
            sv = emit("dve", "tensor_scalar_mul", buf, buf,
                      GCOL[:, gi * DCH + ki:gi * DCH + ki + 1],
                      waits=[(s_te, tvp)], inc=s_dve)
            drain("dve")
            dma("sp", sptw, wdst[ki * P:(ki + 1) * P, :], buf,
                waits=[(s_dve, sv)])
        off = 0 if nm == "q" else D
        for cg in range(ncog):
            emit("dve", "tensor_copy", BROW[0:1, off + cg * cw0: off + (cg + 1) * cw0],
                 bank(cg % 4 + (4 if cg >= 4 else 0), cw0)[0:1, :],
                 waits=[(s_te, tvp)] if cg == 0 else [])
        drain("dve")
        prep_done[nm] = emit("dve", "nop", inc=s_dve)
        for b_ in range(8):
            bg[b_] = (s_dve, prep_done[nm])
    addsp = [("bqs_b", 0, D, SCL), ("bkvk_s", D, D, SCL), ("bkvv_b", 2 * D, D, None)]
    for nm, o0, n, scale in addsp:
        for cg in range(max(1, n // 512)):
            cw0 = min(512, n)
            tv = emit("te", "matmul", bank(6, cw0), ONESF[0:1, :],
                      BROW[0:1, o0 + cg * cw0: o0 + (cg + 1) * cw0],
                      start=True, stop=True,
                      waits=([(s_dve, prep_done["kv"])] + bgw(6)), inc=s_te)
            emit("dve", "scalar_tensor_tensor",
                 A.b(nm, n)[:, cg * cw0:(cg + 1) * cw0], bank(6, cw0),
                 (scale if scale is not None else 1.0),
                 A.b(nm, n)[:, cg * cw0:(cg + 1) * cw0], OP.mult, OP.add,
                 waits=[(s_te, tv)])
            drain("dve")
            bg[6] = (s_dve, emit("dve", "nop", inc=s_dve))
    prep_all = emit("dve", "nop", inc=s_dve)
    A.release("vrow", "wprep", "gcolp", "bcolf", "brow")

    ch = max(P, 2 * INTER // 8)
    for i in range(0, 2 * INTER, ch):
        dma("gp", swcast, wgu_bf[i:i + ch, :], wgu_d[i:i + ch, :])
    ch2 = max(P, D // 4)
    for i in range(0, D, ch2):
        dma("gp", swcast, wdn_bf[i:i + ch2, :], wdn_d[i:i + ch2, :])
    ffn_cast_done = swcast.n


    GU_ROWS = min(1024, 2 * INTER)
    DN_ROWS = min(1024, D)
    _ptr_tiles = []
    for j0 in range(0, 2 * INTER, GU_ROWS):
        for ki in range(DCH):
            _ptr_tiles.append(("gu", j0, ki))
    for h0 in range(0, D, DN_ROWS):
        for ic in range(ICH):
            _ptr_tiles.append(("dn", h0, ic))

    def ptr_slice(idx, total):
        n = len(_ptr_tiles)
        lo, hi = idx * n // total, (idx + 1) * n // total
        for ti in range(lo, hi):
            kind, a0, cc = _ptr_tiles[ti]
            buf_all = TRT
            rows = GU_ROWS if kind == "gu" else DN_ROWS
            buf = buf_all[:, (ti % 2) * 1024:(ti % 2) * 1024 + rows]
            w = [(swcast, ffn_cast_done)] if ti in (lo, lo + 1) else []
            w.append((sptw, sptw.n))
            if kind == "gu":
                src = wgu_bf[a0:a0 + rows, cc * P:(cc + 1) * P]
                tv = dma_tr("sp", swff, buf, src, waits=w)
                nblk = rows // P
                dst = bass.AP(wguT.tensor, ((a0 // P) * D + cc * P) * P,
                              [[P, P], [D * P, nblk], [1, P]])
                s3 = buf.rearrange("p (j c) -> p j c", j=nblk)
                dma("sp", sptw, dst, s3, waits=[(swff, tv)])
            else:
                src = wdn_bf[a0:a0 + rows, cc * P:(cc + 1) * P]
                tv = dma_tr("sp", swff, buf, src, waits=w)
                dma("sp", sptw, wdnT[cc * P:(cc + 1) * P, a0:a0 + rows], buf,
                    waits=[(swff, tv)])

    def rsqrt_ref(dst, src, t0, t1):
        av = emit("act", "activation", t0, src, AFT.Sqrt, inc=s_act)
        emit("dve", "reciprocal", dst, t0, waits=[(s_act, av)])
        drain("dve")
        emit("dve", "tensor_tensor", t0, dst, dst, OP.mult)
        drain("dve")
        emit("dve", "tensor_tensor", t0, t0, src, OP.mult)
        drain("dve")
        emit("dve", "tensor_scalar", t1, t0, -0.5, 1.5, OP.mult, OP.add)
        drain("dve")
        emit("dve", "tensor_tensor", dst, dst, t1, OP.mult)
        drain("dve")

    def ln_stats(par, xt, sq):
        ST = ST2[:, par * 16:(par + 1) * 16]
        emit("dve", "reduce_sum", ST[:, 0:1], xt, axis=AX.X)
        av = emit("act", "activation", sq, xt, AFT.Square, inc=s_act)
        emit("dve", "reduce_sum", ST[:, 1:2], sq, axis=AX.X, waits=[(s_act, av)])
        drain("dve")
        emit("dve", "tensor_scalar", ST[:, 2:4], ST[:, 0:2], 1.0 / D, None,
             OP.mult)
        drain("dve")
        emit("dve", "tensor_tensor", ST[:, 15:16], ST[:, 2:3], ST[:, 2:3], OP.mult)
        drain("dve")
        emit("dve", "tensor_tensor", ST[:, 3:4], ST[:, 3:4], ST[:, 15:16],
             OP.subtract)
        drain("dve")
        emit("dve", "tensor_scalar", ST[:, 4:5], ST[:, 3:4], EPS, None, OP.add)
        drain("dve")
        emit("dve", "reciprocal", ST[:, 7:8], ST[:, 4:5])
        drain("dve")
        emit("dve", "tensor_tensor", ST[:, 15:16], ST[:, 3:4], ST[:, 7:8], OP.mult)
        drain("dve")
        emit("dve", "tensor_scalar", ST[:, 5:6], ST[:, 15:16], EPS, None, OP.add)
        emit("dve", "tensor_scalar", ST[:, 6:7], ST[:, 15:16], float(D), None,
             OP.mult)
        drain("dve")
        dv = emit("dve", "nop", inc=s_dve)
        av = emit("act", "activation", ST[:, 8:11], ST[:, 4:7], AFT.Sqrt,
                  waits=[(s_dve, dv)], inc=s_act)
        emit("dve", "tensor_scalar", ST[:, 11:12], ST[:, 10:11], -1.0, None,
             OP.mult, waits=[(s_act, av)])
        emit("dve", "reciprocal", ST[:, 12:14], ST[:, 8:10])
        drain("dve")
        emit("dve", "tensor_tensor", ST[:, 14:16], ST[:, 12:14], ST[:, 12:14],
             OP.mult)
        drain("dve")
        emit("dve", "tensor_tensor", ST[:, 14:16], ST[:, 14:16], ST[:, 4:6],
             OP.mult)
        drain("dve")
        emit("dve", "tensor_scalar", ST[:, 14:16], ST[:, 14:16], -0.5, 1.5,
             OP.mult, OP.add)
        drain("dve")
        emit("dve", "tensor_tensor", ST[:, 12:14], ST[:, 12:14], ST[:, 14:16],
             OP.mult)
        drain("dve")
        emit("dve", "tensor_tensor", ST[:, 14:15], ST[:, 12:13], ST[:, 13:14],
             OP.mult)
        drain("dve")

    kq_guard = [0]
    tmp_guard = [0]

    def rot_tables(par, kqsrc1, kqsrc2, tcol, rr, ri):
        ST = ST2[:, par * 16:(par + 1) * 16]
        ANG = ANG2[:, par * HD:(par + 1) * HD]
        kq = A.f("kq", 2 * HD)
        kv1 = dma("sp", skq, kq[:, 0:HD], kqsrc1,
                  waits=[(s_dve, kq_guard[0])])
        kv2 = dma("sp", skq, kq[:, HD:2 * HD], kqsrc2)
        emit("dve", "tensor_scalar", ANG, FDUP, tcol, None, OP.mult)
        drain("dve")

        def wrap(dst, shift, kk, kwait):
            emit("dve", "scalar_tensor_tensor", dst, kk, -C1, ANG, OP.mult,
                 OP.add, waits=[(skq, kv2)])
            drain("dve")
            if shift != 0.0:
                emit("dve", "tensor_scalar", dst, dst, shift, None, OP.add)
                drain("dve")
            emit("dve", "scalar_tensor_tensor", dst, kk, -C2, dst, OP.mult, OP.add)
            drain("dve")
            emit("dve", "scalar_tensor_tensor", dst, kk, -C3, dst, OP.mult, OP.add)
            drain("dve")
            emit("dve", "tensor_scalar", dst, dst, 0.0, TWO_PI, OP.max, OP.min)
            drain("dve")
            dv = emit("dve", "nop", inc=s_dve)
            return emit("act", "activation", dst, dst, AFT.Sin, bias=NEGPI,
                        scale=1.0, waits=[(s_dve, dv)], inc=s_act)

        a1 = wrap(ri, 0.0, kq[:, 0:HD], kv1)
        a2 = wrap(rr, PI_2, kq[:, HD:2 * HD], kv2)
        emit("dve", "tensor_scalar", rr, rr, ST[:, 11:12], None, OP.mult,
             waits=[(s_act, a2)])
        emit("dve", "tensor_scalar", ri, ri, ST[:, 11:12], None, OP.mult,
             waits=[(s_act, a1)])
        drain("dve")
        kq_guard[0] = emit("dve", "nop", inc=s_dve)

    def rot_headln(par, yt, sq, tmp, out_bf, rr, ri):
        HL = HL2[:, par * 6 * H:(par + 1) * 6 * H]

        def hl(i):
            return HL[:, i * H:(i + 1) * H]
        y4 = yt.rearrange("p (h j two) -> p h j two", h=H, two=2)
        t4 = tmp.rearrange("p (h j two) -> p h j two", h=H, two=2)
        s4 = sq.rearrange("p (h j two) -> p h j two", h=H, two=2)
        rr4 = rr.unsqueeze(1).broadcast_to([P, H, HD]).rearrange(
            "p h (j two) -> p h j two", two=2)
        ri4 = ri.unsqueeze(1).broadcast_to([P, H, HD]).rearrange(
            "p h (j two) -> p h j two", two=2)
        ye, yo = y4[:, :, :, 0:1], y4[:, :, :, 1:2]
        emit("dve", "tensor_tensor", t4[:, :, :, 0:1], ye, rr4[:, :, :, 0:1],
             OP.mult, waits=[(s_act, tmp_guard[0])])
        emit("dve", "tensor_tensor", t4[:, :, :, 1:2], ye, ri4[:, :, :, 0:1],
             OP.mult)
        emit("dve", "tensor_tensor", s4[:, :, :, 0:1], yo, ri4[:, :, :, 1:2],
             OP.mult)
        emit("dve", "tensor_tensor", s4[:, :, :, 1:2], yo, rr4[:, :, :, 1:2],
             OP.mult)
        drain("dve")
        emit("dve", "tensor_tensor", t4[:, :, :, 0:1], t4[:, :, :, 0:1],
             s4[:, :, :, 0:1], OP.subtract)
        emit("dve", "tensor_tensor", t4[:, :, :, 1:2], t4[:, :, :, 1:2],
             s4[:, :, :, 1:2], OP.add)
        drain("dve")
        dv = emit("dve", "nop", inc=s_dve)
        av = emit("act", "activation", sq, tmp, AFT.Square, waits=[(s_dve, dv)],
                  inc=s_act)
        t3 = tmp.rearrange("p (h d) -> p h d", h=H)
        s3 = sq.rearrange("p (h d) -> p h d", h=H)
        emit("dve", "reduce_sum", hl(0).unsqueeze(2), t3, axis=AX.X)
        emit("dve", "reduce_sum", hl(1).unsqueeze(2), s3, axis=AX.X,
             waits=[(s_act, av)])
        drain("dve")
        emit("dve", "tensor_scalar", hl(2), hl(0), 1.0 / HD, None, OP.mult)
        emit("dve", "tensor_scalar", hl(3), hl(1), 1.0 / HD, None, OP.mult)
        drain("dve")
        emit("dve", "tensor_tensor", hl(1), hl(2), hl(2), OP.mult)
        drain("dve")
        emit("dve", "tensor_tensor", hl(5), hl(3), hl(1), OP.subtract)
        drain("dve")
        emit("dve", "tensor_scalar", hl(5), hl(5), EPS, None, OP.add)
        drain("dve")
        dv = emit("dve", "nop", inc=s_dve)
        emit("act", "nop", waits=[(s_dve, dv)])
        rsqrt_ref(hl(3), hl(5), hl(0), hl(1))
        emit("dve", "tensor_tensor", hl(4), hl(2), hl(3), OP.mult)
        drain("dve")
        emit("dve", "tensor_scalar", hl(4), hl(4), -1.0, None, OP.mult)
        drain("dve")
        dv = emit("dve", "nop", inc=s_dve)
        emit("act", "nop", waits=[(s_dve, dv)])
        for h in range(H):
            av = emit("act", "activation", sq[:, h * HD:(h + 1) * HD],
                      tmp[:, h * HD:(h + 1) * HD], AFT.Identity,
                      bias=hl(4)[:, h:h + 1], scale=hl(3)[:, h:h + 1],
                      inc=(s_act if h == H - 1 else None))
        tmp_guard[0] = av
        ge = A.b("lnhg_e", HD).unsqueeze(1).broadcast_to([P, H, HD])
        be = A.b("lnhb_e", HD).unsqueeze(1).broadcast_to([P, H, HD])
        emit("dve", "tensor_tensor", t3, s3, ge, OP.mult, waits=[(s_act, av)])
        drain("dve")
        ob3 = out_bf.rearrange("p (h d) -> p h d", h=H)
        emit("dve", "tensor_tensor", ob3, t3, be, OP.add)
        drain("dve")
        return emit("dve", "nop", inc=s_dve)

    def tm_to_fm(src_bf, dst_fn, nch, te_waits=()):
        last = None
        for k0 in range(0, nch, 4):
            kn = min(4, nch - k0)
            pb = 4 + (k0 // 4) % 2
            tv = None
            for kk in range(kn):
                w = list(te_waits) if (k0 == 0 and kk == 0) else []
                if kk == 0:
                    w += bgw(pb)
                tv = emit("te", "transpose", bankb(pb, P, o=kk * P),
                          src_bf[:, (k0 + kk) * P:(k0 + kk + 1) * P], IDENT,
                          waits=w, inc=(s_te if kk == kn - 1 else None))
            for kk in range(kn):
                emit("dve", "tensor_copy", dst_fn(k0 + kk),
                     bankb(pb, P, o=kk * P), waits=[(s_te, tv)] if kk == 0 else [])
            drain("dve")
            last = emit("dve", "nop", inc=s_dve)
            bg[pb] = (s_dve, last)
        return last

    A.alloc("wres", DCH * D // 2)
    A.alloc("akv_fm", DCH * GT // 2)
    WQ = A.b("wres", DCH * D)
    srcw = wq_bf[:, :].rearrange("(ki p) n -> p ki n", p=P)
    dstw = WQ.rearrange("p (ki n) -> p ki n", ki=DCH)
    wq_loaded = 0
    for kk in range(0, DCH, 4):
        kn = min(4, DCH - kk)
        wq_loaded = dma("gp", swwq, dstw[:, kk:kk + kn], srcw[:, kk:kk + kn],
                        waits=[(sptw, sptw.n), (s_dve, prep_all)]
                        if kk == 0 else [])

    for nm, u in [("xpipe", 2 * D), ("sq", 2 * D), ("xnn", D),
                  ("tile_bf", D), ("fm_bf", DCH * P), ("ptm", 2 * D),
                  ("rrg", NTG * HD), ("rig", NTG * HD)]:
        A.alloc(nm, u)
    A.alloc_high("kq", 2 * HD)
    XPP = [A.f("xpipe", 2 * D)[:, i * D:(i + 1) * D] for i in range(2)]
    SQP = [A.f("sq", 2 * D)[:, i * D:(i + 1) * D] for i in range(2)]
    _tmp = A.f("xnn", D)
    TMPP = [_tmp, _tmp]
    TBP = [A.b("tile_bf", 2 * D)[:, i * D:(i + 1) * D] for i in range(2)]
    FMP = [A.b("fm_bf", 2 * DCH * P)[:, i * DCH * P:(i + 1) * DCH * P]
           for i in range(2)]
    FM3P = [FMP[i].rearrange("p (ki c) -> p ki c", ki=DCH) for i in range(2)]
    QTMP = [A.f("ptm", 2 * D)[:, i * D:(i + 1) * D] for i in range(2)]
    KTMP = QTMP
    RRG = A.f("rrg", NTG * HD)
    RIG = A.f("rig", NTG * HD)

    chain_end = [prep_all, prep_all]
    pctr = [0]

    def chain_start():
        par = pctr[0] % 2
        pctr[0] += 1
        return par, chain_end[par]

    def q_tile_chain(t):
        par, gate = chain_start()
        xp, sq_s, tmp_s = XPP[par], SQP[par], TMPP[par]
        tile_b, fm_b, fm3, qtm = TBP[par], FMP[par], FM3P[par], QTMP[par]
        ST = ST2[:, par * 16:(par + 1) * 16]
        xv = dma("sp", sxp[par], xp, xow_d[t * P:(t + 1) * P, :],
                 waits=[(s_dve, gate)])
        emit("dve", "nop", waits=[(sxp[par], xv)])
        emit("act", "nop", waits=[(sxp[par], xv), (s_dve, gate)])
        ln_stats(par, xp, sq_s)
        emit("dve", "tensor_scalar", tile_b, xp, ST[:, 2:3], ST[:, 12:13],
             OP.subtract, OP.mult)
        drain("dve")
        dv = emit("dve", "nop", inc=s_dve)
        fmk = tm_to_fm(tile_b, lambda k: fm_b[:, k * P:(k + 1) * P], DCH,
                       te_waits=[(s_dve, dv)])
        dstx = bass.AP(xn_sp.tensor, t * P, [[TLOC, P], [P * TLOC, DCH], [1, P]])
        wv = dma("sp", stw, dstx, fm3, waits=[(s_dve, fmk)])
        emit("dve", "tensor_scalar", tile_b, xp, ST[:, 2:3], ST[:, 14:15],
             OP.subtract, OP.mult, waits=[(stw, wv)])
        drain("dve")
        dv = emit("dve", "nop", inc=s_dve)
        tm_to_fm(tile_b, lambda k: fm_b[:, k * P:(k + 1) * P], DCH,
                 te_waits=[(s_dve, dv)])
        aq_done = emit("dve", "nop", inc=s_dve)
        emit("te", "nop", waits=[(swwq, wq_loaded), (s_dve, aq_done)])
        for ki in range(DCH):
            emit("te", "ldweights", fm_b[:, ki * P:(ki + 1) * P])
            for ci, (co, cw) in enumerate(COGS):
                w = bgw(ci) if ki == 0 else []
                tv = emit("te", "matmul", bank(ci, cw),
                          fm_b[:, ki * P:(ki + 1) * P],
                          WQ[:, ki * D + co: ki * D + co + cw],
                          start=(ki == 0), stop=(ki == DCH - 1), waits=w,
                          inc=(s_te if (ki == DCH - 1 and ci == NCG - 1)
                               else None))
        for ci, (co, cw) in enumerate(COGS):
            emit("dve", "scalar_tensor_tensor", qtm[:, co:co + cw], bank(ci, cw),
                 SCL, A.b("bqs_b", D)[:, co:co + cw], OP.mult, OP.add,
                 waits=[(s_te, tv)] if ci == 0 else [])
        drain("dve")
        dv = emit("dve", "nop", inc=s_dve)
        for ci in range(NCG):
            bg[ci] = (s_dve, dv)
        rot_tables(par, kqo_d[:, t * HD:(t + 1) * HD],
                   kqo2_d[:, t * HD:(t + 1) * HD],
                   TCO[:, t:t + 1], RRG[:, par * HD:(par + 1) * HD],
                   RIG[:, par * HD:(par + 1) * HD])
        rot_headln(par, qtm, sq_s, tmp_s, tile_b,
                   RRG[:, par * HD:(par + 1) * HD],
                   RIG[:, par * HD:(par + 1) * HD])
        dv = emit("dve", "nop", inc=s_dve)
        fmk = tm_to_fm(tile_b, lambda k: fm_b[:, k * P:(k + 1) * P], DCH,
                       te_waits=[(s_dve, dv)])
        dstq = bass.AP(qn_sp.tensor, t * P, [[TLOC, P], [P * TLOC, DCH], [1, P]])
        qv = dma("sp", stw, dstq, fm3, waits=[(s_dve, fmk)])
        chain_end[par] = emit("dve", "nop", waits=[(stw, qv)], inc=s_dve)

    for t in range(NTO):
        q_tile_chain(t)

    A.release("bqs_b")

    q1_te_done = s_te.n
    prep_store_n = 2 * DCH * 16
    WKVH = A.b("wres", DCH * D)
    dstwh = WKVH.rearrange("p (ki n) -> p ki n", ki=DCH)
    akvb = A.b("akv_fm", DCH * GT)
    n_ptr = NGRP + 1
    ptr_slice(0, n_ptr)

    def load_wkv_half(half, wguards):
        srcl = wkv_bf[:, half * D:(half + 1) * D].rearrange(
            "(ki p) n -> p ki n", p=P)
        v = 0
        for kk in range(DCH):
            w = list(wguards) if kk == 0 else []
            v = dma("gp", swkv, dstwh[:, kk:kk + 1], srcl[:, kk:kk + 1], waits=w)
        return v

    def akv_tile_chain(g, t):
        par, gate = chain_start()
        xp, sq_s = XPP[par], SQP[par]
        tile_b = TBP[par]
        ST = ST2[:, par * 16:(par + 1) * 16]
        gt = g * NTG + t
        xv = dma("sp", sxp[par], xp, x_d[gt * P:(gt + 1) * P, :],
                 waits=[(s_dve, gate)])
        emit("dve", "nop", waits=[(sxp[par], xv)])
        emit("act", "nop", waits=[(sxp[par], xv), (s_dve, gate)])
        ln_stats(par, xp, sq_s)
        emit("dve", "tensor_scalar", tile_b, xp, ST[:, 2:3], ST[:, 14:15],
             OP.subtract, OP.mult)
        drain("dve")
        rot_tables(par, kq_d[:, gt * HD:(gt + 1) * HD],
                   kq2_d[:, gt * HD:(gt + 1) * HD],
                   TCOLS[:, gt:gt + 1], RRG[:, t * HD:(t + 1) * HD],
                   RIG[:, t * HD:(t + 1) * HD])
        dv = emit("dve", "nop", inc=s_dve)
        last = tm_to_fm(tile_b,
                        lambda k, _t=t: akvb[:, k * GT + _t * P:
                                             k * GT + (_t + 1) * P],
                        DCH, te_waits=[(s_dve, dv)])
        chain_end[par] = last

    def k_sweep_tile(g, t, tv_proj):
        par, gate = chain_start()
        sq_s, tmp_s = SQP[par], TMPP[par]
        tile_b, fm_b, fm3, ktm = TBP[par], FMP[par], FM3P[par], KTMP[par]
        for ci, (co, cw) in enumerate(COGS):
            w = [(s_te, tv_proj)] if ci == 0 else []
            if ci == 0:
                w.append((s_dve, gate))
            emit("dve", "scalar_tensor_tensor", ktm[:, co:co + cw],
                 bank(ci, cw), SCL, A.b("bkvk_s", D)[:, co:co + cw],
                 OP.mult, OP.add, waits=w)
        drain("dve")
        dv = emit("dve", "nop", inc=s_dve)
        for ci in range(NCG):
            bg[ci] = (s_dve, dv)
        emit("act", "nop", waits=[(s_dve, gate)])
        rot_headln(par, ktm, sq_s, tmp_s, tile_b,
                   RRG[:, t * HD:(t + 1) * HD], RIG[:, t * HD:(t + 1) * HD])
        dv = emit("dve", "nop", inc=s_dve)
        fmk = tm_to_fm(tile_b, lambda k: fm_b[:, k * P:(k + 1) * P], DCH,
                       te_waits=[(s_dve, dv)])
        dstk = bass.AP(st_kn.tensor, g * D * GT + t * P,
                       [[GT, P], [P * GT, DCH], [1, P]])
        kv = dma("sp", stw, dstk, fm3, waits=[(s_dve, fmk)])
        chain_end[par] = emit("dve", "nop", waits=[(stw, kv)], inc=s_dve)

    def v_sweep_tile(g, t, tv_proj):
        par, gate = chain_start()
        tile_b = TBP[par]
        gt = g * NTG + t
        for ci, (co, cw) in enumerate(COGS):
            w = [(s_te, tv_proj)] if ci == 0 else []
            if ci == 0:
                w.append((s_dve, gate))
            emit("dve", "scalar_tensor_tensor", tile_b[:, co:co + cw],
                 bank(ci, cw), 1.0, A.b("bkvv_b", D)[:, co:co + cw],
                 OP.mult, OP.add, waits=w)
        drain("dve")
        dv = emit("dve", "nop", inc=s_dve)
        for ci in range(NCG):
            bg[ci] = (s_dve, dv)
        dstv = bass.AP(st_v.tensor, gt * P * HD,
                       [[HD, P], [NKC * P * HD, H], [1, HD]])
        t3v = tile_b.rearrange("p (h c) -> p h c", h=H)
        vv = dma("sp", stw, dstv, t3v, waits=[(s_dve, dv)])
        chain_end[par] = emit("dve", "nop", waits=[(stw, vv)], inc=s_dve)

    def kv_proj_tile(t, half_w):
        for ki in range(DCH):
            emit("te", "ldweights",
                 akvb[:, ki * GT + t * P: ki * GT + (t + 1) * P])
            for ci, (co, cw) in enumerate(COGS):
                w = bgw(ci) if ki == 0 else []
                tv = emit("te", "matmul", bank(ci, cw),
                          akvb[:, ki * GT + t * P: ki * GT + (t + 1) * P],
                          WKVH[:, ki * D + co: ki * D + co + cw],
                          start=(ki == 0), stop=(ki == DCH - 1), waits=w,
                          inc=(s_te if (ki == DCH - 1 and ci == NCG - 1)
                               else None))
        return tv

    for g in range(NGRP):
        for t in range(NTG):
            akv_tile_chain(g, t)
        akv_ready = chain_end[(pctr[0] - 1) % 2]
        kguard = ([(s_te, s_te.n)] if g > 0
                  else [(s_te, q1_te_done), (sptw, prep_store_n)])
        wk = load_wkv_half(0, kguard)
        emit("te", "nop", waits=[(swkv, wk), (s_dve, akv_ready)])
        for t in range(NTG):
            tvp = kv_proj_tile(t, 0)
            k_sweep_tile(g, t, tvp)
        wk = load_wkv_half(1, [(s_te, s_te.n)])
        emit("te", "nop", waits=[(swkv, wk)])
        for t in range(NTG):
            tvp = kv_proj_tile(t, 1)
            v_sweep_tile(g, t, tvp)
        ptr_slice(g + 1, n_ptr)

    A.release("wres", "akv_fm", "ptm", "xpipe", "sq", "xnn",
              "tile_bf", "fm_bf", "rrg", "rig", "kq",
              "bkvk_s", "bkvv_b", "lnhg_e", "lnhb_e")
    kv_all_stored = stw.n

    A.alloc("xn_fm", DCH * TLOC // 2)
    XNF = A.b("xn_fm", DCH * TLOC)
    srcx = bass.AP(xn_sp.tensor, 0, [[TLOC, P], [P * TLOC, DCH], [1, TLOC]])
    ph1_te_done = s_te.n
    xn_rl = dma("sp", srd, XNF.rearrange("p (ki n) -> p ki n", ki=DCH), srcx,
                waits=[(stw, stw.n), (s_te, ph1_te_done), (s_dve, s_dve.n),
                       (sptw, sptw.n)])
    A.alloc("m_fm", ICH * TLOC // 2)
    MFM = A.b("m_fm", ICH * TLOC)
    A.alloc("wgu_t", 2 * DCH * P)
    GBUF = A.b("wgu_t", 4 * DCH * P)
    A.alloc("silu_t", TLOC)
    SILU = A.f("silu_t", TLOC)

    gu_guard = [None, None]
    for j in range(ICH):
        lv = 0
        for s_i, blk in enumerate((j, ICH + j)):
            dbuf = GBUF[:, ((j % 2) * 2 + s_i) * DCH * P:
                        ((j % 2) * 2 + s_i + 1) * DCH * P]
            srcb = bass.AP(wguT.tensor, blk * D * P,
                           [[P, P], [P * P, DCH], [1, P]])
            w = []
            if s_i == 0:
                w = [(srd, srd.n)]
            if j < 2:
                w += [(sptw, sptw.n), (s_te, ph1_te_done)]
            elif s_i == 0 and gu_guard[j % 2] is not None:
                w += [gu_guard[j % 2]]
            lv = dma("sp", srd, dbuf.rearrange("p (ki c) -> p ki c", ki=DCH),
                     srcb, waits=w)
        emit("te", "nop", waits=[(srd, lv), (srd, xn_rl)])
        for s_i in range(2):
            dbuf = GBUF[:, ((j % 2) * 2 + s_i) * DCH * P:
                        ((j % 2) * 2 + s_i + 1) * DCH * P]
            for ki in range(DCH):
                emit("te", "ldweights", dbuf[:, ki * P:(ki + 1) * P])
                for bi, (to, tw) in enumerate(TLB):
                    bidx = s_i * 2 + bi
                    w = bgw(bidx) if ki == 0 else []
                    tv = emit("te", "matmul", bank(bidx, tw),
                              dbuf[:, ki * P:(ki + 1) * P],
                              XNF[:, ki * TLOC + to: ki * TLOC + to + tw],
                              start=(ki == 0), stop=(ki == DCH - 1), waits=w,
                              inc=(s_te if (s_i == 1 and ki == DCH - 1
                                            and bi == NTB - 1) else None))
        gu_guard[j % 2] = (s_te, tv)
        for bi, (to, tw) in enumerate(TLB):
            av = emit("act", "activation", SILU[:, to:to + tw], bank(bi, tw),
                      AFT.Silu, waits=[(s_te, tv)] if bi == 0 else [],
                      inc=(s_act if bi == NTB - 1 else None))
        for bi, (to, tw) in enumerate(TLB):
            emit("dve", "tensor_tensor",
                 MFM[:, j * TLOC + to: j * TLOC + to + tw],
                 SILU[:, to:to + tw], bank(2 + bi, tw), OP.mult,
                 waits=[(s_act, av)] if bi == 0 else [])
        drain("dve")
        mk = emit("dve", "nop", inc=s_dve)
        for bidx in range(2 * NTB + 2):
            if bidx < 8:
                bg[bidx] = (s_dve, mk)
    A.release("wgu_t", "silu_t", "xn_fm")
    up_te_done = s_te.n
    up_dve_done = s_dve.n

    A.alloc("wdn_t", 2 * D)
    WDN = A.b("wdn_t", 2 * D)
    A.alloc("ffn_t", D)
    FFT = A.f("ffn_t", D)
    dn_guard = [None, None]
    out_stash = 0
    for tp in range(0, NTO, 2):
        ntp = min(2, NTO - tp)
        for i in range(ICH):
            dbuf = WDN[:, (i % 2) * D:(i % 2) * D + D]
            w = [(srd, srd.n)]
            if dn_guard[i % 2] is not None:
                w.append(dn_guard[i % 2])
            if tp == 0 and i < 2:
                w += [(sptw, sptw.n), (s_te, up_te_done), (s_dve, up_dve_done)]
            lv = dma("sp", srd, dbuf, wdnT[i * P:(i + 1) * P, :], waits=w)
            emit("te", "nop", waits=[(srd, lv)])
            for tti in range(ntp):
                tt = tp + tti
                emit("te", "ldweights",
                     MFM[:, i * TLOC + tt * P: i * TLOC + (tt + 1) * P])
                for ci, (co, cw) in enumerate(COGS):
                    bidx = tti * NCG + ci
                    w2 = bgw(bidx) if i == 0 else []
                    tvd = emit("te", "matmul", bank(bidx, cw),
                               MFM[:, i * TLOC + tt * P: i * TLOC + (tt + 1) * P],
                               WDN[:, (i % 2) * D + co:(i % 2) * D + co + cw],
                               start=(i == 0), stop=(i == ICH - 1), waits=w2,
                               inc=(s_te if (tti == ntp - 1 and ci == NCG - 1)
                                    else None))
            dn_guard[i % 2] = (s_te, tvd)
        tv = dn_guard[(ICH - 1) % 2][1]
        for tti in range(ntp):
            tt = tp + tti
            for ci, (co, cw) in enumerate(COGS):
                bidx = tti * NCG + ci
                emit("dve", "scalar_tensor_tensor", FFT[:, co:co + cw],
                     bank(bidx, cw), 1.0, A.b("bo_b", D)[:, co:co + cw],
                     OP.mult, OP.add,
                     waits=([(s_te, tv), (stw, out_stash)]
                            if (tti == 0 and ci == 0) else []))
            drain("dve")
            dv = emit("dve", "nop", inc=s_dve)
            out_stash = dma("sp", stw, ffn_sp[tt * P:(tt + 1) * P, :], FFT,
                            waits=[(s_dve, dv)])
            mk = emit("dve", "nop", waits=[(stw, out_stash)], inc=s_dve)
            for ci in range(NCG):
                bg[tti * NCG + ci] = (s_dve, mk)
    A.release("wdn_t", "ffn_t", "m_fm")

    A.alloc("qn_fm", DCH * TLOC // 2)
    QNF = A.b("qn_fm", DCH * TLOC)
    srcq = bass.AP(qn_sp.tensor, 0, [[TLOC, P], [P * TLOC, DCH], [1, TLOC]])
    ph2_te_done = s_te.n
    qn_rl = dma("sp", srd, QNF.rearrange("p (ki n) -> p ki n", ki=DCH), srcq,
                waits=[(s_te, ph2_te_done), (s_dve, s_dve.n)])
    A.alloc("kn_h", NKC * P)
    KNB = A.b("kn_h", 2 * NKC * P)
    A.alloc("v_h", NKC * HD)
    VB = A.b("v_h", 2 * NKC * HD)
    A.alloc("p_buf", NKC * QT // 2)
    PBUF = A.b("p_buf", NKC * QT)
    A.alloc("attn_fm", DCH * TLOC // 2)
    ATTN = A.b("attn_fm", DCH * TLOC)
    A.alloc("lrec", QT)
    LREC = A.f("lrec", QT)

    kn_loads = [0] * H
    head_done = [None] * (H + 2)

    def load_head(h):
        par = h % 2
        w = [(stw, kv_all_stored), (s_te, ph2_te_done), (srd, srd.n)]
        if h >= 2 and head_done[h - 2] is not None:
            w = [head_done[h - 2], (srd, srd.n)]
        last = 0
        for g in range(NGRP):
            last = dma("sp", srd,
                       KNB[:, par * NKC * P + g * GT:par * NKC * P + (g + 1) * GT],
                       st_kn[g * D + h * P: g * D + (h + 1) * P, :],
                       waits=w if g == 0 else [])
        srcv = bass.AP(st_v.tensor, h * NKC * P * HD,
                       [[HD, P], [P * HD, NKC], [1, HD]])
        last = dma("sp", srd,
                   VB[:, par * NKC * HD:(par + 1) * NKC * HD]
                   .rearrange("p (kc c) -> p kc c", kc=NKC), srcv)
        kn_loads[h] = last

    load_head(0)
    if H > 1:
        load_head(1)
    emit("act", "nop", waits=[(s_dve, negb_done)])
    for h in range(H):
        par = h % 2
        kn_h = KNB[:, par * NKC * P:(par + 1) * NKC * P]
        v_h = VB[:, par * NKC * HD:(par + 1) * NKC * HD]
        emit("te", "nop", waits=[(srd, kn_loads[h]), (srd, qn_rl),
                                 (s_dve, negb_done)])
        for qi in range(NQT):
            q0 = qi * QT
            rhs_q = QNF[:, h * TLOC + q0: h * TLOC + q0 + QT]
            exp_marks = []
            QKB = (0, 1, 4, 5)
            for c0 in range(0, NKC, 2):
                b_i = QKB[(c0 // 2) % 4]
                npair = min(2, NKC - c0)
                for cc in range(npair):
                    w = bgw(b_i) if cc == 0 else []
                    emit("te", "ldweights",
                         kn_h[:, (c0 + cc) * P:(c0 + cc + 1) * P])
                    tv = emit("te", "matmul", bank(b_i, QT, o=cc * QT),
                              kn_h[:, (c0 + cc) * P:(c0 + cc + 1) * P], rhs_q,
                              start=True, stop=True, waits=w,
                              inc=(s_te if cc == npair - 1 else None))
                av = emit("act", "activation",
                          PBUF[:, c0 * QT:(c0 + npair) * QT],
                          bank(b_i, npair * QT), AFT.Exp, bias=NEGB, scale=1.0,
                          waits=[(s_te, tv)], inc=s_act)
                exp_marks.append(av)
                bg[b_i] = (s_act, av)
            for c in range(NKC):
                w = [(s_act, exp_marks[c // 2])]
                if c == 0:
                    w += bgw(2)
                emit("te", "ldweights", v_h[:, c * HD:(c + 1) * HD], waits=w)
                av_tv = emit("te", "matmul", bank(2, QT),
                             v_h[:, c * HD:(c + 1) * HD],
                             PBUF[:, c * QT:(c + 1) * QT],
                             start=(c == 0), stop=(c == NKC - 1),
                             inc=(s_te if c == NKC - 1 else None))
            emit("te", "ldweights", ONES, waits=bgw(3))
            for c in range(NKC):
                l_tv = emit("te", "matmul", bank(3, QT)[0:1, :], ONES,
                            PBUF[:, c * QT:(c + 1) * QT],
                            start=(c == 0), stop=(c == NKC - 1),
                            inc=(s_te if c == NKC - 1 else None))
            emit("dve", "tensor_copy", LROW[0:1, :], bank(3, QT)[0:1, :],
                 waits=[(s_te, l_tv)])
            drain("dve")
            emit("dve", "reciprocal", LROW[0:1, :], LROW[0:1, :])
            drain("dve")
            dv = emit("dve", "nop", inc=s_dve)
            bg[3] = (s_dve, dv)
            tv = emit("te", "matmul", bank(6, QT), ONESF[0:1, :],
                      LROW[0:1, :], start=True, stop=True,
                      waits=[(s_dve, dv)] + bgw(6), inc=s_te)
            emit("dve", "tensor_copy", LREC, bank(6, QT), waits=[(s_te, tv)])
            drain("dve")
            lb = emit("dve", "nop", inc=s_dve)
            bg[6] = (s_dve, lb)
            emit("dve", "tensor_tensor",
                 ATTN[:, h * TLOC + q0: h * TLOC + q0 + QT], bank(2, QT), LREC,
                 OP.mult, waits=[(s_te, av_tv)])
            drain("dve")
            bg[2] = (s_dve, emit("dve", "nop", inc=s_dve))
        head_done[h] = (s_te, l_tv)
        if h + 2 < H:
            load_head(h + 2)
    attn_done = emit("dve", "nop", inc=s_dve)
    A.release("kn_h", "v_h", "p_buf", "qn_fm", "lrec")

    A.alloc("wo_r", DCH * D // 2)
    WOB = A.b("wo_r", DCH * D)
    srcw2 = wo_d[:, :].rearrange("(ki p) n -> p ki n", p=P)
    dstw2 = WOB.rearrange("p (ki n) -> p ki n", ki=DCH)
    wo_loaded = 0
    for kk in range(0, DCH, 4):
        kn = min(4, DCH - kk)
        wo_loaded = dma("gp", swwo, dstw2[:, kk:kk + kn], srcw2[:, kk:kk + kn],
                        waits=[(s_te, head_done[H - 1][1] if head_done[H - 1]
                                else 0), (s_dve, attn_done)] if kk == 0 else [])
    A.alloc("ffn_in", D)
    FIN = A.f("ffn_in", D)
    A.alloc("x_in", D)
    XIN = A.f("x_in", D)
    A.alloc("out_t", D)
    OUTT = A.f("out_t", D)
    emit("te", "nop", waits=[(swwo, wo_loaded), (s_dve, attn_done)])
    out_guard = 0
    for tt in range(NTO):
        fv = dma("sp", srd, FIN, ffn_sp[tt * P:(tt + 1) * P, :],
                 waits=[(s_dve, out_guard if tt > 0 else attn_done)])
        xv = dma("sp", srd, XIN, xow_d[tt * P:(tt + 1) * P, :])
        for ki in range(DCH):
            emit("te", "ldweights",
                 ATTN[:, ki * TLOC + tt * P: ki * TLOC + (tt + 1) * P])
            for ci, (co, cw) in enumerate(COGS):
                w = bgw(ci) if ki == 0 else []
                tv = emit("te", "matmul", bank(ci, cw),
                          ATTN[:, ki * TLOC + tt * P: ki * TLOC + (tt + 1) * P],
                          WOB[:, ki * D + co: ki * D + co + cw],
                          start=(ki == 0), stop=(ki == DCH - 1), waits=w,
                          inc=(s_te if (ki == DCH - 1 and ci == NCG - 1)
                               else None))
        for ci, (co, cw) in enumerate(COGS):
            w = [(s_te, tv), (srd, xv)] if ci == 0 else []
            emit("dve", "tensor_tensor", OUTT[:, co:co + cw], bank(ci, cw),
                 FIN[:, co:co + cw], OP.add, waits=w)
        drain("dve")
        emit("dve", "tensor_tensor", OUTT, OUTT, XIN, OP.add,
             waits=[(srd, xv)])
        drain("dve")
        dv = emit("dve", "nop", inc=s_dve)
        for ci in range(NCG):
            bg[ci] = (s_dve, dv)
        ov = dma("sp", stw, y_d[tt * P:(tt + 1) * P, :], OUTT,
                 waits=[(s_dve, dv)])
        out_guard = emit("dve", "nop", waits=[(stw, ov)], inc=s_dve)

    if dbg:
        dbgs = [("d_xn", xn_sp[:, :]), ("d_qn", qn_sp[:, :]),
                ("d_kn", st_kn[0:D, :]), ("d_v", st_v[0:NKC * P, :]),
                ("d_ffn", ffn_sp[:, :]), ("d_wgu", wguT[0:D, :]),
                ("d_wdn", wdnT[0:P, :])]
        for nm, srcd in dbgs:
            od = nc.dram_tensor(nm, list(srcd.shape),
                                srcd.dtype, kind="ExternalOutput").ap()
            dma("gp", swgp, od[:, :], srcd,
                waits=[(stw, stw.n), (sptw, sptw.n), (srd, srd.n)])
        emit("gp", "nop", waits=[(swgp, swgp.n)])
    emit("sp", "nop", waits=[(stw, stw.n)])
    emit("te", "nop", waits=[(stw, stw.n)])
    emit("gp", "nop", waits=[(stw, stw.n)])
    emit("act", "nop", waits=[(stw, stw.n)])

    def render(q):
        def body(eng):
            for meth, args, kw, waits, incs in Q[q]:
                for s, v in waits:
                    eng.wait_ge(s, v)
                if meth == "nop":
                    if incs:
                        ins = eng.nop()
                        for s, k in incs:
                            ins.then_inc(s, k)
                    continue
                ins = getattr(eng, meth)(*args, **kw)
                if ins is not None:
                    for s, k in incs:
                        ins.then_inc(s, k)
        return body

    with nc.Block() as blk:
        blk.tensor(render("te"))
        blk.scalar(render("act"))
        blk.vector(render("dve"))
        blk.sync(render("sp"))
        blk.gpsimd(render("gp"))
    ctx.close()
    return nc


def _consts(cfg: Cfg):
    f = ref_freqs(cfg.HD)
    fdup = np.repeat(f, 2)[None, :].repeat(P, 0).astype(np.float32)
    tcols = (np.arange(cfg.T, dtype=np.float32).reshape(cfg.T // P, P).T)
    return (np.ascontiguousarray(fdup), np.ascontiguousarray(tcols),
            np.eye(P).astype(ml_dtypes.bfloat16),
            np.ones((P, 1), ml_dtypes.bfloat16))


def _kq_tables(cfg: Cfg, t0, t1):
    f = ref_freqs(cfg.HD)
    fdup = np.repeat(f, 2).astype(np.float32)
    t_idx = np.arange(t0, t1, dtype=np.float32)
    ang32 = (t_idx[:, None] * fdup[None, :]).astype(np.float32)
    k1 = np.floor(ang32.astype(np.float64) / (2 * np.pi))
    k2 = np.floor((ang32.astype(np.float64) + np.float64(np.float32(np.pi / 2)))
                  / (2 * np.pi))
    nt = (t1 - t0) // P

    def lay(k):
        k = k.astype(np.float32).reshape(nt, P, cfg.HD)
        return np.ascontiguousarray(k.transpose(1, 0, 2).reshape(P, nt * cfg.HD))

    return lay(k1), lay(k2)


def _bc(v):
    return np.ascontiguousarray(
        np.broadcast_to(np.asarray(v, np.float32).reshape(1, -1), (P, len(np.asarray(v).reshape(-1)))))


def run(cfg: Cfg, inputs: dict, runner=None, dbg=False):
    nc = build(cfg, dbg=dbg)
    fdup, tcols, ident, ones = _consts(cfg)
    x = np.ascontiguousarray(np.asarray(inputs["L0_attn_x"], np.float32)
                             .reshape(cfg.T, cfg.D))
    base = {
        "x_full": x,
        "Wq": np.ascontiguousarray(np.asarray(inputs["L0_attn_Wq"], np.float32)),
        "Wkv": np.ascontiguousarray(np.asarray(inputs["L0_attn_Wkv"], np.float32)),
        "Wo": np.ascontiguousarray(np.asarray(inputs["L0_attn_Wo"], np.float32)),
        "Wgu": np.ascontiguousarray(np.asarray(inputs["L0_ffn_w_gate_up"],
                                               np.float32)),
        "Wdn": np.ascontiguousarray(np.asarray(inputs["L0_ffn_w_down"],
                                               np.float32)),
        "ln_q_g": _bc(inputs["L0_attn_ln_q_g"]),
        "ln_q_b": _bc(inputs["L0_attn_ln_q_b"]),
        "ln_kv_g": _bc(inputs["L0_attn_ln_kv_g"]),
        "ln_kv_b": _bc(inputs["L0_attn_ln_kv_b"]),
        "ln_h_g": _bc(inputs["L0_attn_ln_h_g"]),
        "ln_h_b": _bc(inputs["L0_attn_ln_h_b"]),
        "bq": _bc(inputs["L0_attn_bq"]),
        "bkv": _bc(inputs["L0_attn_bkv"]),
        "bo": _bc(inputs["L0_attn_bo"]),
        "ident_c": ident, "ones_c": ones, "fdup_c": fdup, "tcols_c": tcols,
        "onesf_c": np.ones((1, P), np.float32),
    }
    kq1, kq2 = _kq_tables(cfg, 0, cfg.T)
    base["kq_c"], base["kq2_c"] = kq1, kq2

    def _col(v):
        return np.ascontiguousarray(
            np.asarray(v, np.float32).reshape(cfg.DCH, P).T)
    base["gqc_c"] = _col(inputs["L0_attn_ln_q_g"])
    base["bqc_c"] = _col(inputs["L0_attn_ln_q_b"])
    base["gkc_c"] = _col(inputs["L0_attn_ln_kv_g"])
    base["bkc_c"] = _col(inputs["L0_attn_ln_kv_b"])
    base = {k: np.ascontiguousarray(v) for k, v in base.items()}
    in_maps = []
    tl = cfg.TLOC
    for c in range(NCORES):
        m = dict(base)
        m["x_own"] = np.ascontiguousarray(x[c * tl:(c + 1) * tl, :])
        m["tcown_c"] = np.ascontiguousarray(
            np.arange(c * tl, (c + 1) * tl, dtype=np.float32)
            .reshape(tl // P, P).T)
        m["kqo_c"], m["kqo2_c"] = _kq_tables(cfg, c * tl, (c + 1) * tl)
        in_maps.append(m)
    if runner is None:
        res = run_bass_kernel_spmd(nc, in_maps, core_ids=list(range(NCORES)))
        outs = [res.results[c]["y"] for c in range(NCORES)]
        if dbg:
            y = np.concatenate([np.asarray(o, np.float32) for o in outs], axis=0)
            return y.reshape(1, cfg.T, cfg.D), res.results
    else:
        outs = runner(nc, in_maps)
    y = np.concatenate([np.asarray(o, np.float32) for o in outs], axis=0)
    return y.reshape(1, cfg.T, cfg.D)


def kernel(**inputs):
    return run(FULL, inputs)


# version 10
# speedup vs baseline: 1.1163x; 1.0129x over previous
from contextlib import ExitStack
from dataclasses import dataclass

import numpy as np
import ml_dtypes

import concourse.bass as bass
import concourse.mybir as mybir
from concourse.bass_utils import run_bass_kernel_spmd
from concourse.alu_op_type import AluOpType as OP
F32 = mybir.dt.float32
BF16 = mybir.dt.bfloat16
AFT = mybir.ActivationFunctionType
AX = mybir.AxisListType

NCORES = 8
P = 128
EPS = 1e-5


@dataclass
class Cfg:
    D: int = 2048
    T: int = 8192
    H: int = 16
    HD: int = 128
    INTER: int = 8192
    QT: int = 256

    @property
    def TLOC(self): return self.T // NCORES
    @property
    def DCH(self): return self.D // P
    @property
    def ICH(self): return self.INTER // P
    @property
    def GT(self): return self.T // 8
    @property
    def NTG(self): return self.GT // P
    @property
    def NTO(self): return self.TLOC // P
    @property
    def NKC(self): return self.T // P
    @property
    def NQT(self): return self.TLOC // self.QT
    @property
    def COGS(self): return [(o, min(512, self.D - o)) for o in range(0, self.D, 512)]
    @property
    def TLB(self): return [(o, min(512, self.TLOC - o))
                           for o in range(0, self.TLOC, 512)]


FULL = Cfg()
MINI = Cfg(D=256, T=2048, H=2, HD=128, INTER=1024, QT=256)

SCL = float(np.float32(128.0 ** -0.25))
TWO_PI = float(np.float64(2 * np.pi))
INV_2PI = float(np.float32(1.0 / (2 * np.pi)))
_c1 = np.float32(6.28125)
_c2 = np.float32(np.float64(TWO_PI) - np.float64(_c1))
_c3 = np.float32(np.float64(TWO_PI) - np.float64(_c1) - np.float64(_c2))
C1, C2, C3 = float(_c1), float(_c2), float(_c3)
PI = float(np.float32(np.pi))
PI_2 = float(np.float32(np.pi / 2))


def ref_freqs(hd):
    lin = np.linspace(np.float32(0.0), np.float32(1.0), hd // 2, dtype=np.float32)
    f = np.float32(40.0) ** lin
    f = f * np.float32(200.0) / np.float32(1000.0)
    return (np.float32(200.0) * f / np.float32(1000.0)).astype(np.float32)


class Sem:
    def __init__(self, h):
        self.h, self.n = h, 0

    def inc(self, k=1):
        self.n += k
        return self.n


class Arena:
    def __init__(self, ap, total):
        self.ap, self.total = ap, total
        self.free = [(0, total)]
        self.live = {}

    def alloc(self, name, units):
        units = (units + 7) & ~7
        best = None
        for i, (off, sz) in enumerate(self.free):
            if sz >= units and (best is None or sz < self.free[best][1]):
                best = i
        if best is None:
            raise RuntimeError(f"arena OOM {name} needs {units} free={self.free} "
                               f"live={ {k: v[1] for k, v in self.live.items()} }")
        off, sz = self.free[best]
        self.free[best] = (off + units, sz - units)
        if self.free[best][1] == 0:
            self.free.pop(best)
        self.live[name] = (off, units)
        return self

    def alloc_high(self, name, units):
        units = (units + 7) & ~7
        for i in range(len(self.free) - 1, -1, -1):
            off, sz = self.free[i]
            if sz >= units:
                self.free[i] = (off, sz - units)
                if self.free[i][1] == 0:
                    self.free.pop(i)
                self.live[name] = (off + sz - units, units)
                return self
        raise RuntimeError(f"arena OOM(high) {name} needs {units}")

    def release(self, *names):
        for name in names:
            off, sz = self.live.pop(name)
            self.free.append((off, sz))
        self.free.sort()
        m = []
        for o, s in self.free:
            if m and m[-1][0] + m[-1][1] == o:
                m[-1][1] += s
            else:
                m.append([o, s])
        self.free = [tuple(x) for x in m]

    def f(self, name, cols=None):
        off, sz = self.live[name]
        return self.ap[:, off:off + (sz if cols is None else cols)]

    def b(self, name, cols=None):
        off, sz = self.live[name]
        v = self.ap[:, off:off + sz].bitcast(BF16)
        return v if cols is None else v[:, 0:cols]


def build(cfg: Cfg, dbg: bool = False):
    D, T, H, HD, INTER, QT = cfg.D, cfg.T, cfg.H, cfg.HD, cfg.INTER, cfg.QT
    DCH, ICH, GT, NTG, NTO = cfg.DCH, cfg.ICH, cfg.GT, cfg.NTG, cfg.NTO
    NKC, NQT, TLOC = cfg.NKC, cfg.NQT, cfg.TLOC
    COGS, TLB = cfg.COGS, cfg.TLB
    NCG, NTB = len(COGS), len(TLB)
    NGRP = T // GT

    nc = bass.Bass("TRN2", target_bir_lowering=False)

    x_d = nc.dram_tensor("x_full", [T, D], F32, kind="ExternalInput").ap()
    xow_d = nc.dram_tensor("x_own", [TLOC, D], F32, kind="ExternalInput").ap()
    wq_d = nc.dram_tensor("Wq", [D, D], F32, kind="ExternalInput").ap()
    wkv_d = nc.dram_tensor("Wkv", [D, 2 * D], F32, kind="ExternalInput").ap()
    wo_d = nc.dram_tensor("Wo", [D, D], F32, kind="ExternalInput").ap()
    wgu_d = nc.dram_tensor("Wgu", [2 * INTER, D], F32, kind="ExternalInput").ap()
    wdn_d = nc.dram_tensor("Wdn", [D, INTER], F32, kind="ExternalInput").ap()
    V = {}
    for nm, n in [("ln_q_g", D), ("ln_q_b", D), ("ln_kv_g", D), ("ln_kv_b", D),
                  ("ln_h_g", HD), ("ln_h_b", HD), ("bq", D), ("bkv", 2 * D),
                  ("bo", D)]:
        V[nm] = nc.dram_tensor(nm, [P, n], F32, kind="ExternalInput").ap()
    ident_d = nc.dram_tensor("ident_c", [P, P], BF16, kind="ExternalInput").ap()
    ones_d = nc.dram_tensor("ones_c", [P, 1], BF16, kind="ExternalInput").ap()
    onesf_d = nc.dram_tensor("onesf_c", [1, P], F32, kind="ExternalInput").ap()
    onesbr_d = nc.dram_tensor("onesbr_c", [1, P], BF16, kind="ExternalInput").ap()
    fdup_d = nc.dram_tensor("fdup_c", [P, HD], F32, kind="ExternalInput").ap()
    tcols_d = nc.dram_tensor("tcols_c", [P, T // P], F32, kind="ExternalInput").ap()
    tco_d = nc.dram_tensor("tcown_c", [P, NTO], F32, kind="ExternalInput").ap()
    kq_d = nc.dram_tensor("kq_c", [P, (T // P) * HD], F32,
                          kind="ExternalInput").ap()
    kq2_d = nc.dram_tensor("kq2_c", [P, (T // P) * HD], F32,
                           kind="ExternalInput").ap()
    kqo_d = nc.dram_tensor("kqo_c", [P, NTO * HD], F32,
                           kind="ExternalInput").ap()
    kqo2_d = nc.dram_tensor("kqo2_c", [P, NTO * HD], F32,
                            kind="ExternalInput").ap()
    gqc_d = nc.dram_tensor("gqc_c", [P, DCH], F32, kind="ExternalInput").ap()
    bqc_d = nc.dram_tensor("bqc_c", [P, DCH], F32, kind="ExternalInput").ap()
    gkc_d = nc.dram_tensor("gkc_c", [P, DCH], F32, kind="ExternalInput").ap()
    bkc_d = nc.dram_tensor("bkc_c", [P, DCH], F32, kind="ExternalInput").ap()
    y_d = nc.dram_tensor("y", [TLOC, D], F32, kind="ExternalOutput").ap()

    wq_bf = nc.dram_tensor("wq_bf", [D, D], BF16).ap()
    wkv_bf = nc.dram_tensor("wkv_bf", [D, 2 * D], BF16).ap()
    wgu_bf = nc.dram_tensor("wgu_bf", [2 * INTER, D], BF16).ap()
    wdn_bf = nc.dram_tensor("wdn_bf", [D, INTER], BF16).ap()
    wguT = nc.dram_tensor("wguT", [2 * ICH * D, P], BF16).ap()
    wdnT = nc.dram_tensor("wdnT", [ICH * P, D], BF16).ap()
    st_kn = nc.dram_tensor("st_kn", [NGRP * D, GT], BF16).ap()
    st_v = nc.dram_tensor("st_v", [H * NKC * P, HD], BF16).ap()
    qn_sp = nc.dram_tensor("qn_sp", [D, TLOC], BF16).ap()
    xn_sp = nc.dram_tensor("xn_sp", [D, TLOC], BF16).ap()
    ffn_sp = nc.dram_tensor("ffn_sp", [TLOC, D], F32).ap()

    ctx = ExitStack()
    AR_U = 49600
    arena_t = ctx.enter_context(nc.sbuf_tensor("arena", [P, AR_U], F32))
    A = Arena(arena_t[:], AR_U)
    small_t = ctx.enter_context(nc.sbuf_tensor("small", [P, 1280], F32))
    SM = small_t[:]
    psum_t = ctx.enter_context(nc.psum_tensor("psum", [P, 4096], F32))
    PS, PSB = psum_t[:], psum_t[:].bitcast(BF16)
    ident_t = ctx.enter_context(nc.sbuf_tensor("identS", [P, P], BF16))
    IDENT = ident_t[:]
    ones_t = ctx.enter_context(nc.sbuf_tensor("onesS", [P, 1], BF16))
    ONES = ones_t[:]
    onesf_t = ctx.enter_context(nc.sbuf_tensor("onesF", [1, P], F32))
    ONESF = onesf_t[:]
    trt_t = ctx.enter_context(nc.sbuf_tensor("trtS", [P, 2048], BF16,
                                             is_transpose_dest=True))
    TRT = trt_t[:]
    onesbr_t = ctx.enter_context(nc.sbuf_tensor("onesbrS", [1, P], BF16))
    ONESBR = onesbr_t[:]

    def bank(b, w=512, o=0):
        return PS[:, b * 512 + o: b * 512 + o + w]

    def bankb(b, w=1024, o=0):
        return PSB[:, b * 1024 + o: b * 1024 + o + w]

    sems = {n: Sem(ctx.enter_context(nc.semaphore(n))) for n in
            ("sx", "stw", "srd", "swgp", "swff", "ste", "sact", "sdve", "sgp",
             "swcast", "swwq", "swkv", "swwo", "sptw", "skq", "swgp2",
             "sx2", "skq2")}
    sx, stw, srd, swgp, swff = (sems[n] for n in ("sx", "stw", "srd", "swgp",
                                                  "swff"))
    s_te, s_act, s_dve, s_gp = (sems[n] for n in ("ste", "sact", "sdve", "sgp"))
    swcast, swwq, swkv, swwo, sptw, skq, swgp2 = (sems[n] for n in
                                                  ("swcast", "swwq", "swkv",
                                                   "swwo", "sptw", "skq",
                                                   "swgp2"))
    sxp = [sx, sems["sx2"]]
    skqp = [skq, sems["skq2"]]

    Q = {k: [] for k in ("te", "act", "dve", "sp", "gp")}

    def emit(q, meth, *args, waits=(), inc=None, inc16=None, **kw):
        r = None
        incs = []
        if inc is not None:
            r = inc.inc()
            incs.append((inc.h, 1))
        if inc16 is not None:
            r = inc16.inc(16)
            incs.append((inc16.h, 16))
        ws = [(s.h, v) for (s, v) in waits if s is not None and v and v > 0]
        Q[q].append((meth, args, kw, ws, incs))
        return r

    def drain(q):
        emit(q, "drain")

    def dma(q, sem, dst, src, waits=()):
        return emit(q, "dma_start", waits=waits, inc16=sem, out=dst, in_=src)

    def dma_tr(q, sem, dst, src, waits=()):
        return emit(q, "dma_start_transpose", dst, src, waits=waits, inc16=sem)

    bg = [None] * 8

    def bgw(b):
        return [] if bg[b] is None else [bg[b]]

    _smo = [0]

    def sm(n):
        o = _smo[0]
        _smo[0] += max(n, 1)
        assert _smo[0] <= 1280, _smo[0]
        return SM[:, o:o + n]

    FDUP = sm(HD)
    TCOLS = sm(T // P)
    TCO = sm(NTO)
    ST2 = sm(32)
    HL2 = sm(12 * H)
    RED1 = sm(2)
    BBFr = sm(8)
    NEGB = sm(1)
    NEGPI = sm(1)
    LROW = sm(QT)
    ANG2 = sm(2 * HD)
    WRK2X = sm(2 * HD)

    BBF = BBFr.bitcast(BF16)

    dma("sp", sx, IDENT, ident_d[:, :])
    dma("sp", sx, ONES, ones_d[:, :])
    dma("sp", sx, ONESF, onesf_d[:, :])
    dma("sp", sx, ONESBR, onesbr_d[:, :])
    dma("sp", sx, FDUP, fdup_d[:, :])
    dma("sp", sx, TCOLS, tcols_d[:, :])
    dma("sp", sx, TCO, tco_d[:, :])
    emit("dve", "memset", NEGPI, -PI)
    drain("dve")

    A.alloc("vrow", 2 * D)
    vrow = A.f("vrow")
    A.alloc("brows", 3 * D // 2)
    BROWS = A.b("brows", 3 * D)
    bc_specs = [("lnhg_e", V["ln_h_g"][:, :], HD, SCL),
                ("lnhb_e", V["ln_h_b"][:, :], HD, SCL),
                ("bo_b", V["bo"][:, :], D, None)]
    last_bc = 0
    for nm, srcv, n, scale in bc_specs:
        A.alloc(nm, (n + 1) // 2)
        vv = dma("gp", swgp, vrow[:, 0:n], srcv, waits=[(s_dve, last_bc)])
        if scale is None:
            last_bc = emit("dve", "tensor_copy", A.b(nm, n), vrow[:, 0:n],
                           waits=[(swgp, vv)], inc=s_dve)
        else:
            last_bc = emit("dve", "tensor_scalar_mul", A.b(nm, n), vrow[:, 0:n],
                           float(scale), waits=[(swgp, vv)], inc=s_dve)
        drain("dve")
    bc_done = last_bc
    av = emit("act", "activation", vrow[:, 0:HD], A.b("lnhg_e", HD), AFT.Square,
              waits=[(s_dve, last_bc)], inc=s_act)
    emit("dve", "reduce_max", RED1[:, 0:1], vrow[:, 0:HD], axis=AX.X,
         waits=[(s_act, av)])
    drain("dve")
    dv = emit("dve", "nop", inc=s_dve)
    av = emit("act", "activation", vrow[:, 0:HD], A.b("lnhb_e", HD), AFT.Square,
              waits=[(s_dve, dv)], inc=s_act)
    emit("dve", "reduce_sum", RED1[:, 1:2], vrow[:, 0:HD], axis=AX.X,
         waits=[(s_act, av)])
    drain("dve")
    dv = emit("dve", "nop", inc=s_dve)
    av = emit("act", "activation", BBFr[:, 2:4], RED1[:, 0:2], AFT.Sqrt,
              waits=[(s_dve, dv)], inc=s_act)
    emit("dve", "tensor_scalar", BBFr[:, 0:1], BBFr[:, 2:3],
         float(np.sqrt(HD)), None, OP.mult, waits=[(s_act, av)])
    drain("dve")
    emit("dve", "tensor_tensor", BBFr[:, 0:1], BBFr[:, 0:1], BBFr[:, 3:4], OP.add)
    drain("dve")
    emit("dve", "tensor_tensor", BBFr[:, 1:2], BBFr[:, 0:1], BBFr[:, 0:1], OP.mult)
    drain("dve")
    emit("dve", "tensor_scalar", NEGB, BBFr[:, 1:2], -1.0001, None, OP.mult)
    drain("dve")
    negb_done = emit("dve", "nop", inc=s_dve)

    A.alloc("wprep", 2 * D)
    A.alloc("gcolp", 2 * DCH + 4)
    WPREP = A.b("wprep", 4 * D)
    GCOL = A.f("gcolp", 2 * DCH)
    BCOLB = A.f("gcolp")[:, 2 * DCH:2 * DCH + 2].bitcast(BF16)
    gl1 = dma("sp", sx, GCOL[:, 0:DCH], gqc_d[:, :])
    gl2 = dma("sp", sx, GCOL[:, DCH:2 * DCH], gkc_d[:, :])
    A.alloc("bcolf", DCH * 2)
    BCOLF = A.f("bcolf", 2 * DCH)
    bl1 = dma("sp", sx, BCOLF[:, 0:DCH], bqc_d[:, :])
    bl2 = dma("sp", sx, BCOLF[:, DCH:2 * DCH], bkc_d[:, :])
    emit("dve", "nop", waits=[(sx, sx.n)])
    emit("dve", "tensor_scalar_mul", GCOL[:, 0:DCH], GCOL[:, 0:DCH], SCL)
    drain("dve")
    prep_specs = [("q", wq_d, wq_bf, D, 0), ("kv", wkv_d, wkv_bf, 2 * D, 1)]
    A.alloc("brow", 3 * D)
    BROW = A.f("brow", 3 * D)
    prep_done = {}
    for nm, wsrc, wdst, cols, gi in prep_specs:
        ncog = cols // 512 if cols >= 512 else 1
        cw0 = min(512, cols)
        pg = None
        for ki in range(DCH):
            buf = WPREP[:, (ki % 2) * 2 * D:(ki % 2) * 2 * D + cols]
            w = [(s_dve, bc_done)] if ki == 0 and nm == "q" else []
            w.append((sptw, sptw.n))
            psem = swgp if ki % 2 == 0 else swgp2
            cv = dma("gp", psem, buf, wsrc[ki * P:(ki + 1) * P, :], waits=w)
            bv = emit("dve", "tensor_copy", BCOLB[:, gi:gi + 1],
                      BCOLF[:, gi * DCH + ki:gi * DCH + ki + 1],
                      waits=[(psem, cv)], inc=s_dve)
            drain("dve")
            for cg in range(ncog):
                tvp = emit("te", "matmul",
                           bank(cg % 4 + (4 if cg >= 4 else 0), cw0)[0:1, :],
                           BCOLB[:, gi:gi + 1], buf[:, cg * cw0:(cg + 1) * cw0],
                           start=(ki == 0), stop=(ki == DCH - 1),
                           waits=[(s_dve, bv)] if cg == 0 else [],
                           inc=(s_te if cg == ncog - 1 else None))
            sv = emit("dve", "tensor_scalar_mul", buf, buf,
                      GCOL[:, gi * DCH + ki:gi * DCH + ki + 1],
                      waits=[(s_te, tvp)], inc=s_dve)
            drain("dve")
            if nm == "kv":
                sv = emit("dve", "tensor_scalar_mul", buf[:, 0:D], buf[:, 0:D],
                          SCL, inc=s_dve)
                drain("dve")
            dma("sp", sptw, wdst[ki * P:(ki + 1) * P, :], buf,
                waits=[(s_dve, sv)])
        off = 0 if nm == "q" else D
        for cg in range(ncog):
            emit("dve", "tensor_copy", BROW[0:1, off + cg * cw0: off + (cg + 1) * cw0],
                 bank(cg % 4 + (4 if cg >= 4 else 0), cw0)[0:1, :],
                 waits=[(s_te, tvp)] if cg == 0 else [])
        drain("dve")
        prep_done[nm] = emit("dve", "nop", inc=s_dve)
        for b_ in range(8):
            bg[b_] = (s_dve, prep_done[nm])
    A.alloc("rowt", 2 * D)
    ROWT = A.f("rowt", 2 * D)
    r1 = dma("sp", sx, ROWT[0:1, 0:D], V["bq"][0:1, :],
             waits=[(s_dve, prep_done["kv"])])
    r2 = dma("sp", sx, ROWT[0:1, D:2 * D], V["bkv"][0:1, 0:D])
    emit("dve", "nop", waits=[(sx, r2)])
    emit("dve", "tensor_tensor", ROWT[0:1, 0:D], ROWT[0:1, 0:D], BROW[0:1, 0:D],
         OP.add)
    emit("dve", "tensor_tensor", ROWT[0:1, D:2 * D], ROWT[0:1, D:2 * D],
         BROW[0:1, D:2 * D], OP.add)
    drain("dve")
    emit("dve", "tensor_scalar", BROWS[0:1, 0:2 * D], ROWT[0:1, 0:2 * D], SCL,
         None, OP.mult)
    drain("dve")
    bvd = emit("dve", "nop", inc=s_dve)
    r3 = dma("sp", sx, ROWT[0:1, 0:D], V["bkv"][0:1, D:2 * D],
             waits=[(s_dve, bvd)])
    emit("dve", "nop", waits=[(sx, r3)])
    emit("dve", "tensor_tensor", BROWS[0:1, 2 * D:3 * D], ROWT[0:1, 0:D],
         BROW[0:1, 2 * D:3 * D], OP.add)
    drain("dve")
    A.release("rowt")
    prep_all = emit("dve", "nop", inc=s_dve)
    A.release("vrow", "wprep", "gcolp", "bcolf", "brow")

    ch = max(P, 2 * INTER // 8)
    for i in range(0, 2 * INTER, ch):
        dma("gp", swcast, wgu_bf[i:i + ch, :], wgu_d[i:i + ch, :])
    ch2 = max(P, D // 4)
    for i in range(0, D, ch2):
        dma("gp", swcast, wdn_bf[i:i + ch2, :], wdn_d[i:i + ch2, :])
    ffn_cast_done = swcast.n


    GU_ROWS = min(1024, 2 * INTER)
    DN_ROWS = min(1024, D)
    _ptr_tiles = []
    for j0 in range(0, 2 * INTER, GU_ROWS):
        for ki in range(DCH):
            _ptr_tiles.append(("gu", j0, ki))
    for h0 in range(0, D, DN_ROWS):
        for ic in range(ICH):
            _ptr_tiles.append(("dn", h0, ic))

    def ptr_slice(idx, total):
        n = len(_ptr_tiles)
        lo, hi = idx * n // total, (idx + 1) * n // total
        for ti in range(lo, hi):
            kind, a0, cc = _ptr_tiles[ti]
            buf_all = TRT
            rows = GU_ROWS if kind == "gu" else DN_ROWS
            buf = buf_all[:, (ti % 2) * 1024:(ti % 2) * 1024 + rows]
            w = [(swcast, ffn_cast_done)] if ti in (lo, lo + 1) else []
            w.append((sptw, sptw.n))
            if kind == "gu":
                src = wgu_bf[a0:a0 + rows, cc * P:(cc + 1) * P]
                tv = dma_tr("sp", swff, buf, src, waits=w)
                nblk = rows // P
                dst = bass.AP(wguT.tensor, ((a0 // P) * D + cc * P) * P,
                              [[P, P], [D * P, nblk], [1, P]])
                s3 = buf.rearrange("p (j c) -> p j c", j=nblk)
                dma("sp", sptw, dst, s3, waits=[(swff, tv)])
            else:
                src = wdn_bf[a0:a0 + rows, cc * P:(cc + 1) * P]
                tv = dma_tr("sp", swff, buf, src, waits=w)
                dma("sp", sptw, wdnT[cc * P:(cc + 1) * P, a0:a0 + rows], buf,
                    waits=[(swff, tv)])

    def rsqrt_ref(dst, src, t0, t1):
        av = emit("act", "activation", t0, src, AFT.Sqrt, inc=s_act)
        emit("dve", "reciprocal", dst, t0, waits=[(s_act, av)])
        drain("dve")
        emit("dve", "tensor_tensor", t0, dst, dst, OP.mult)
        drain("dve")
        emit("dve", "tensor_tensor", t0, t0, src, OP.mult)
        drain("dve")
        emit("dve", "tensor_scalar", t1, t0, -0.5, 1.5, OP.mult, OP.add)
        drain("dve")
        emit("dve", "tensor_tensor", dst, dst, t1, OP.mult)
        drain("dve")

    def ln_stats(par, xt, sq):
        ST = ST2[:, par * 16:(par + 1) * 16]
        emit("dve", "reduce_sum", ST[:, 0:1], xt, axis=AX.X)
        av = emit("act", "activation", sq, xt, AFT.Square, inc=s_act)
        emit("dve", "reduce_sum", ST[:, 1:2], sq, axis=AX.X, waits=[(s_act, av)])
        drain("dve")
        emit("dve", "tensor_scalar", ST[:, 2:4], ST[:, 0:2], 1.0 / D, None,
             OP.mult)
        drain("dve")
        emit("dve", "tensor_tensor", ST[:, 15:16], ST[:, 2:3], ST[:, 2:3], OP.mult)
        drain("dve")
        emit("dve", "tensor_tensor", ST[:, 3:4], ST[:, 3:4], ST[:, 15:16],
             OP.subtract)
        drain("dve")
        emit("dve", "tensor_scalar", ST[:, 4:5], ST[:, 3:4], EPS, None, OP.add)
        drain("dve")
        emit("dve", "reciprocal", ST[:, 7:8], ST[:, 4:5])
        drain("dve")
        emit("dve", "tensor_tensor", ST[:, 15:16], ST[:, 3:4], ST[:, 7:8], OP.mult)
        drain("dve")
        emit("dve", "tensor_scalar", ST[:, 5:6], ST[:, 15:16], EPS, None, OP.add)
        emit("dve", "tensor_scalar", ST[:, 6:7], ST[:, 15:16], float(D), None,
             OP.mult)
        drain("dve")
        dv = emit("dve", "nop", inc=s_dve)
        av = emit("act", "activation", ST[:, 8:11], ST[:, 4:7], AFT.Sqrt,
                  waits=[(s_dve, dv)], inc=s_act)
        emit("dve", "tensor_scalar", ST[:, 11:12], ST[:, 10:11], -1.0, None,
             OP.mult, waits=[(s_act, av)])
        emit("dve", "reciprocal", ST[:, 12:14], ST[:, 8:10])
        drain("dve")
        emit("dve", "tensor_tensor", ST[:, 14:16], ST[:, 12:14], ST[:, 12:14],
             OP.mult)
        drain("dve")
        emit("dve", "tensor_tensor", ST[:, 14:16], ST[:, 14:16], ST[:, 4:6],
             OP.mult)
        drain("dve")
        emit("dve", "tensor_scalar", ST[:, 14:16], ST[:, 14:16], -0.5, 1.5,
             OP.mult, OP.add)
        drain("dve")
        emit("dve", "tensor_tensor", ST[:, 12:14], ST[:, 12:14], ST[:, 14:16],
             OP.mult)
        drain("dve")
        emit("dve", "tensor_tensor", ST[:, 14:15], ST[:, 12:13], ST[:, 13:14],
             OP.mult)
        drain("dve")

    kq_guard = [0]
    tmp_guard = [0]

    def rot_tables(par, kqsrc1, kqsrc2, tcol, rr, ri):
        ST = ST2[:, par * 16:(par + 1) * 16]
        ANG = ANG2[:, par * HD:(par + 1) * HD]
        kq = A.f("kq", 2 * HD)
        kv1 = dma("sp", skq, kq[:, 0:HD], kqsrc1,
                  waits=[(s_dve, kq_guard[0])])
        kv2 = dma("sp", skq, kq[:, HD:2 * HD], kqsrc2)
        emit("dve", "tensor_scalar", ANG, FDUP, tcol, None, OP.mult)
        drain("dve")

        def wrap(dst, shift, kk, kwait):
            emit("dve", "scalar_tensor_tensor", dst, kk, -C1, ANG, OP.mult,
                 OP.add, waits=[(skq, kv2)])
            drain("dve")
            if shift != 0.0:
                emit("dve", "tensor_scalar", dst, dst, shift, None, OP.add)
                drain("dve")
            emit("dve", "scalar_tensor_tensor", dst, kk, -C2, dst, OP.mult, OP.add)
            drain("dve")
            emit("dve", "scalar_tensor_tensor", dst, kk, -C3, dst, OP.mult, OP.add)
            drain("dve")
            emit("dve", "tensor_scalar", dst, dst, 0.0, TWO_PI, OP.max, OP.min)
            drain("dve")
            dv = emit("dve", "nop", inc=s_dve)
            return emit("act", "activation", dst, dst, AFT.Sin, bias=NEGPI,
                        scale=1.0, waits=[(s_dve, dv)], inc=s_act)

        a1 = wrap(ri, 0.0, kq[:, 0:HD], kv1)
        a2 = wrap(rr, PI_2, kq[:, HD:2 * HD], kv2)
        emit("dve", "tensor_scalar", rr, rr, ST[:, 11:12], None, OP.mult,
             waits=[(s_act, a2)])
        emit("dve", "tensor_scalar", ri, ri, ST[:, 11:12], None, OP.mult,
             waits=[(s_act, a1)])
        drain("dve")
        kq_guard[0] = emit("dve", "nop", inc=s_dve)

    def rot_headln(par, yt, sq, tmp, out_bf, rr, ri):
        HL = HL2[:, par * 6 * H:(par + 1) * 6 * H]

        def hl(i):
            return HL[:, i * H:(i + 1) * H]
        y4 = yt.rearrange("p (h j two) -> p h j two", h=H, two=2)
        t4 = tmp.rearrange("p (h j two) -> p h j two", h=H, two=2)
        s4 = sq.rearrange("p (h j two) -> p h j two", h=H, two=2)
        rr4 = rr.unsqueeze(1).broadcast_to([P, H, HD]).rearrange(
            "p h (j two) -> p h j two", two=2)
        ri4 = ri.unsqueeze(1).broadcast_to([P, H, HD]).rearrange(
            "p h (j two) -> p h j two", two=2)
        ye, yo = y4[:, :, :, 0:1], y4[:, :, :, 1:2]
        emit("dve", "tensor_tensor", t4[:, :, :, 0:1], ye, rr4[:, :, :, 0:1],
             OP.mult, waits=[(s_act, tmp_guard[0])])
        emit("dve", "tensor_tensor", t4[:, :, :, 1:2], ye, ri4[:, :, :, 0:1],
             OP.mult)
        emit("dve", "tensor_tensor", s4[:, :, :, 0:1], yo, ri4[:, :, :, 1:2],
             OP.mult)
        emit("dve", "tensor_tensor", s4[:, :, :, 1:2], yo, rr4[:, :, :, 1:2],
             OP.mult)
        drain("dve")
        emit("dve", "tensor_tensor", t4[:, :, :, 0:1], t4[:, :, :, 0:1],
             s4[:, :, :, 0:1], OP.subtract)
        emit("dve", "tensor_tensor", t4[:, :, :, 1:2], t4[:, :, :, 1:2],
             s4[:, :, :, 1:2], OP.add)
        drain("dve")
        dv = emit("dve", "nop", inc=s_dve)
        av = emit("act", "activation", sq, tmp, AFT.Square, waits=[(s_dve, dv)],
                  inc=s_act)
        t3 = tmp.rearrange("p (h d) -> p h d", h=H)
        s3 = sq.rearrange("p (h d) -> p h d", h=H)
        emit("dve", "reduce_sum", hl(0).unsqueeze(2), t3, axis=AX.X)
        emit("dve", "reduce_sum", hl(1).unsqueeze(2), s3, axis=AX.X,
             waits=[(s_act, av)])
        drain("dve")
        emit("dve", "tensor_scalar", hl(2), hl(0), 1.0 / HD, None, OP.mult)
        emit("dve", "tensor_scalar", hl(3), hl(1), 1.0 / HD, None, OP.mult)
        drain("dve")
        emit("dve", "tensor_tensor", hl(1), hl(2), hl(2), OP.mult)
        drain("dve")
        emit("dve", "tensor_tensor", hl(5), hl(3), hl(1), OP.subtract)
        drain("dve")
        emit("dve", "tensor_scalar", hl(5), hl(5), EPS, None, OP.add)
        drain("dve")
        dv = emit("dve", "nop", inc=s_dve)
        emit("act", "nop", waits=[(s_dve, dv)])
        rsqrt_ref(hl(3), hl(5), hl(0), hl(1))
        emit("dve", "tensor_tensor", hl(4), hl(2), hl(3), OP.mult)
        drain("dve")
        emit("dve", "tensor_scalar", hl(4), hl(4), -1.0, None, OP.mult)
        drain("dve")
        dv = emit("dve", "nop", inc=s_dve)
        emit("act", "nop", waits=[(s_dve, dv)])
        for h in range(H):
            av = emit("act", "activation", sq[:, h * HD:(h + 1) * HD],
                      tmp[:, h * HD:(h + 1) * HD], AFT.Identity,
                      bias=hl(4)[:, h:h + 1], scale=hl(3)[:, h:h + 1],
                      inc=(s_act if h == H - 1 else None))
        tmp_guard[0] = av
        ge = A.b("lnhg_e", HD).unsqueeze(1).broadcast_to([P, H, HD])
        be = A.b("lnhb_e", HD).unsqueeze(1).broadcast_to([P, H, HD])
        emit("dve", "tensor_tensor", t3, s3, ge, OP.mult, waits=[(s_act, av)])
        drain("dve")
        ob3 = out_bf.rearrange("p (h d) -> p h d", h=H)
        emit("dve", "tensor_tensor", ob3, t3, be, OP.add)
        drain("dve")
        return emit("dve", "nop", inc=s_dve)

    def tm_to_fm(src_bf, dst_fn, nch, te_waits=()):
        last = None
        for k0 in range(0, nch, 4):
            kn = min(4, nch - k0)
            pb = 4 + (k0 // 4) % 2
            tv = None
            for kk in range(kn):
                w = list(te_waits) if (k0 == 0 and kk == 0) else []
                if kk == 0:
                    w += bgw(pb)
                tv = emit("te", "transpose", bankb(pb, P, o=kk * P),
                          src_bf[:, (k0 + kk) * P:(k0 + kk + 1) * P], IDENT,
                          waits=w, inc=(s_te if kk == kn - 1 else None))
            for kk in range(kn):
                emit("dve", "tensor_copy", dst_fn(k0 + kk),
                     bankb(pb, P, o=kk * P), waits=[(s_te, tv)] if kk == 0 else [])
            drain("dve")
            last = emit("dve", "nop", inc=s_dve)
            bg[pb] = (s_dve, last)
        return last

    A.alloc("wres", DCH * D // 2)
    A.alloc("akv_fm", DCH * GT // 2)
    WQ = A.b("wres", DCH * D)
    akvb = A.b("akv_fm", DCH * GT)
    srcw = wq_bf[:, :].rearrange("(ki p) n -> p ki n", p=P)
    dstw = WQ.rearrange("p (ki n) -> p ki n", ki=DCH)
    wq_loaded = 0
    for kk in range(0, DCH, 4):
        kn = min(4, DCH - kk)
        wq_loaded = dma("gp", swwq, dstw[:, kk:kk + kn], srcw[:, kk:kk + kn],
                        waits=[(sptw, sptw.n), (s_dve, prep_all)]
                        if kk == 0 else [])

    for nm, u in [("xpipe", 2 * D), ("sq", 2 * D), ("xnn", D),
                  ("tile_bf", D), ("fm_bf", DCH * P), ("ptm", 2 * D),
                  ("rrg", NTG * HD), ("rig", NTG * HD)]:
        A.alloc(nm, u)
    A.alloc_high("kq", 2 * HD)
    XPP = [A.f("xpipe", 2 * D)[:, i * D:(i + 1) * D] for i in range(2)]
    SQP = [A.f("sq", 2 * D)[:, i * D:(i + 1) * D] for i in range(2)]
    _tmp = A.f("xnn", D)
    TMPP = [_tmp, _tmp]
    TBP = [A.b("tile_bf", 2 * D)[:, i * D:(i + 1) * D] for i in range(2)]
    FMP = [A.b("fm_bf", 2 * DCH * P)[:, i * DCH * P:(i + 1) * DCH * P]
           for i in range(2)]
    FM3P = [FMP[i].rearrange("p (ki c) -> p ki c", ki=DCH) for i in range(2)]
    QTMP = [A.f("ptm", 2 * D)[:, i * D:(i + 1) * D] for i in range(2)]
    KTMP = QTMP
    RRG = A.f("rrg", NTG * HD)
    RIG = A.f("rig", NTG * HD)
    NMS = sm(NTG)

    chain_end = [prep_all, prep_all]
    PREP_SPTW = sptw.n
    pctr = [0]

    def chain_start():
        par = pctr[0] % 2
        pctr[0] += 1
        return par, chain_end[par]

    def pre_tile_chain(xsrc, kq1, kq2, tcol, t, with_xnspill, aq_scl_col):
        par, gate = chain_start()
        xp, sq_s, tile_b = XPP[par], SQP[par], TBP[par]
        fm_b, fm3 = FMP[par], FM3P[par]
        ST = ST2[:, par * 16:(par + 1) * 16]
        xw = [(s_dve, gate)]
        if pctr[0] <= 2:
            xw.append((sptw, PREP_SPTW))
        xv = dma("sp", sxp[par], xp, xsrc, waits=xw)
        emit("dve", "nop", waits=[(sxp[par], xv)])
        emit("act", "nop", waits=[(sxp[par], xv), (s_dve, gate)])
        ln_stats(par, xp, sq_s)
        if with_xnspill:
            emit("dve", "tensor_scalar", tile_b, xp, ST[:, 2:3], ST[:, 12:13],
                 OP.subtract, OP.mult)
            drain("dve")
            dv = emit("dve", "nop", inc=s_dve)
            fmk = tm_to_fm(tile_b, lambda k: fm_b[:, k * P:(k + 1) * P], DCH,
                           te_waits=[(s_dve, dv)])
            dstx = bass.AP(xn_sp.tensor, t * P,
                           [[TLOC, P], [P * TLOC, DCH], [1, P]])
            wv = dma("sp", stw, dstx, fm3, waits=[(s_dve, fmk), (stw, stw.n)])
            emit("dve", "nop", waits=[(stw, wv)])
        emit("dve", "tensor_scalar", tile_b, xp, ST[:, 2:3],
             ST[:, aq_scl_col:aq_scl_col + 1], OP.subtract, OP.mult)
        drain("dve")
        emit("dve", "tensor_copy", NMS[:, t:t + 1], ST[:, 11:12])
        drain("dve")
        dv = emit("dve", "nop", inc=s_dve)
        last = tm_to_fm(tile_b,
                        lambda k, _t=t: akvb[:, k * GT + _t * P:
                                             k * GT + (_t + 1) * P],
                        DCH, te_waits=[(s_dve, dv)])
        rot_tables_pre(par, kq1, kq2, tcol, t)
        chain_end[par] = emit("dve", "nop", inc=s_dve)

    def rot_tables_pre(par, kqsrc1, kqsrc2, tcol, t):
        ST = ST2[:, par * 16:(par + 1) * 16]
        ANG = ANG2[:, par * HD:(par + 1) * HD]
        rr = RRG[:, t * HD:(t + 1) * HD]
        ri = RIG[:, t * HD:(t + 1) * HD]
        kq = A.f("kq", 2 * HD)
        kv1 = dma("sp", skq, kq[:, 0:HD], kqsrc1, waits=[(s_dve, kq_guard[0])])
        kv2 = dma("sp", skq, kq[:, HD:2 * HD], kqsrc2)
        emit("dve", "tensor_scalar", ANG, FDUP, tcol, None, OP.mult)
        drain("dve")

        def wrap(dst, shift, kk):
            emit("dve", "scalar_tensor_tensor", dst, kk, -C1, ANG, OP.mult,
                 OP.add, waits=[(skq, kv2)])
            drain("dve")
            if shift != 0.0:
                emit("dve", "tensor_scalar", dst, dst, shift, None, OP.add)
                drain("dve")
            emit("dve", "scalar_tensor_tensor", dst, kk, -C2, dst, OP.mult, OP.add)
            drain("dve")
            emit("dve", "scalar_tensor_tensor", dst, kk, -C3, dst, OP.mult, OP.add)
            drain("dve")
            emit("dve", "tensor_scalar", dst, dst, 0.0 - PI, PI, OP.max, OP.min,
                 waits=[]) if False else None
            emit("dve", "tensor_scalar", dst, dst, 0.0, TWO_PI, OP.max, OP.min)
            drain("dve")

        wrap(ri, 0.0, kq[:, 0:HD])
        wrap(rr, PI_2, kq[:, HD:2 * HD])
        kq_guard[0] = emit("dve", "nop", inc=s_dve)

    def group_tables(nt):
        dv = emit("dve", "nop", inc=s_dve)
        a1 = emit("act", "activation", RIG[:, 0:nt * HD], RIG[:, 0:nt * HD],
                  AFT.Sin, bias=NEGPI, scale=1.0, waits=[(s_dve, dv)],
                  inc=s_act)
        a2 = emit("act", "activation", RRG[:, 0:nt * HD], RRG[:, 0:nt * HD],
                  AFT.Sin, bias=NEGPI, scale=1.0, inc=s_act)
        for t in range(nt):
            emit("dve", "tensor_scalar", RRG[:, t * HD:(t + 1) * HD],
                 RRG[:, t * HD:(t + 1) * HD], NMS[:, t:t + 1], None, OP.mult,
                 waits=[(s_act, a2)] if t == 0 else [])
            emit("dve", "tensor_scalar", RIG[:, t * HD:(t + 1) * HD],
                 RIG[:, t * HD:(t + 1) * HD], NMS[:, t:t + 1], None, OP.mult)
        drain("dve")
        return emit("dve", "nop", inc=s_dve)

    def proj_tiles(t, wbuf, woff, cols_off, brow_off, banksets=None):
        emit("te", "nop", waits=[]) if False else None
        for ci, (co, cw) in enumerate(COGS):
            w = bgw(ci) if True else []
            emit("te", "matmul", bank(ci, cw), ONESBR,
                 BROWS[0:1, brow_off + co: brow_off + co + cw],
                 start=True, stop=False, waits=w)
        for ki in range(DCH):
            emit("te", "ldweights",
                 akvb[:, ki * GT + t * P: ki * GT + (t + 1) * P])
            for ci, (co, cw) in enumerate(COGS):
                tv = emit("te", "matmul", bank(ci, cw),
                          akvb[:, ki * GT + t * P: ki * GT + (t + 1) * P],
                          wbuf[:, ki * cols_off + woff + co:
                               ki * cols_off + woff + co + cw],
                          start=False, stop=(ki == DCH - 1),
                          inc=(s_te if (ki == DCH - 1 and ci == NCG - 1)
                               else None))
        return tv

    def evac_act(tv_proj, dst, gate=None):
        for ci, (co, cw) in enumerate(COGS):
            w = [(s_te, tv_proj)] if ci == 0 else []
            if gate is not None and ci == 0:
                w.append(gate)
            av = emit("act", "activation", dst[:, co:co + cw], bank(ci, cw),
                      AFT.Identity, waits=w,
                      inc=(s_act if ci == NCG - 1 else None))
        for ci in range(NCG):
            bg[ci] = (s_act, av)
        return av

    def q_main_chain(t):
        par, gate = chain_start()
        sq_s, tmp_s = SQP[par], TMPP[par]
        tile_b, fm_b, fm3, qtm = TBP[par], FMP[par], FM3P[par], QTMP[par]
        emit("te", "nop", waits=[(swwq, wq_loaded)] if t == 0 else [])
        tvp = proj_tiles(t, WQ, 0, D, 0)
        av = evac_act(tvp, qtm, gate=(s_dve, gate))
        emit("dve", "nop", waits=[(s_act, av)])
        rot_headln(par, qtm, sq_s, tmp_s, tile_b,
                   RRG[:, t * HD:(t + 1) * HD], RIG[:, t * HD:(t + 1) * HD])
        dv = emit("dve", "nop", inc=s_dve)
        fmk = tm_to_fm(tile_b, lambda k: fm_b[:, k * P:(k + 1) * P], DCH,
                       te_waits=[(s_dve, dv)])
        dstq = bass.AP(qn_sp.tensor, t * P, [[TLOC, P], [P * TLOC, DCH], [1, P]])
        qv = dma("sp", stw, dstq, fm3, waits=[(s_dve, fmk), (stw, stw.n)])
        chain_end[par] = emit("dve", "nop", waits=[(stw, qv)], inc=s_dve)

    for t in range(NTO):
        pre_tile_chain(xow_d[t * P:(t + 1) * P, :],
                       kqo_d[:, t * HD:(t + 1) * HD],
                       kqo2_d[:, t * HD:(t + 1) * HD],
                       TCO[:, t:t + 1], t, True, 14)
    group_tables(NTO)
    for t in range(NTO):
        q_main_chain(t)

    q1_te_done = s_te.n
    prep_store_n = 2 * DCH * 16
    WKVH = A.b("wres", DCH * D)
    dstwh = WKVH.rearrange("p (ki n) -> p ki n", ki=DCH)
    n_ptr = NGRP + 1
    ptr_slice(0, n_ptr)

    def load_wkv_half(half, wguards):
        srcl = wkv_bf[:, half * D:(half + 1) * D].rearrange(
            "(ki p) n -> p ki n", p=P)
        v = 0
        for kk in range(DCH):
            w = list(wguards) if kk == 0 else []
            v = dma("gp", swkv, dstwh[:, kk:kk + 1], srcl[:, kk:kk + 1], waits=w)
        return v

    def k_sweep_tile(g, t, tv_proj):
        par, gate = chain_start()
        sq_s, tmp_s = SQP[par], TMPP[par]
        tile_b, fm_b, fm3, ktm = TBP[par], FMP[par], FM3P[par], KTMP[par]
        av = evac_act(tv_proj, ktm, gate=(s_dve, gate))
        emit("dve", "nop", waits=[(s_act, av)])
        emit("act", "nop", waits=[(s_dve, gate)])
        rot_headln(par, ktm, sq_s, tmp_s, tile_b,
                   RRG[:, t * HD:(t + 1) * HD], RIG[:, t * HD:(t + 1) * HD])
        dv = emit("dve", "nop", inc=s_dve)
        fmk = tm_to_fm(tile_b, lambda k: fm_b[:, k * P:(k + 1) * P], DCH,
                       te_waits=[(s_dve, dv)])
        dstk = bass.AP(st_kn.tensor, g * D * GT + t * P,
                       [[GT, P], [P * GT, DCH], [1, P]])
        kv = dma("sp", stw, dstk, fm3, waits=[(s_dve, fmk), (stw, stw.n)])
        chain_end[par] = emit("dve", "nop", waits=[(stw, kv)], inc=s_dve)

    def v_sweep_tile(g, t, tv_proj):
        par, gate = chain_start()
        tile_b = TBP[par]
        gt = g * NTG + t
        for ci, (co, cw) in enumerate(COGS):
            w = [(s_te, tv_proj)] if ci == 0 else []
            if ci == 0:
                w.append((s_dve, gate))
            av = emit("act", "activation", tile_b[:, co:co + cw], bank(ci, cw),
                      AFT.Identity, waits=w,
                      inc=(s_act if ci == NCG - 1 else None))
        for ci in range(NCG):
            bg[ci] = (s_act, av)
        dstv = bass.AP(st_v.tensor, gt * P * HD,
                       [[HD, P], [NKC * P * HD, H], [1, HD]])
        t3v = tile_b.rearrange("p (h c) -> p h c", h=H)
        vv = dma("sp", stw, dstv, t3v, waits=[(s_act, av), (stw, stw.n)])
        chain_end[par] = emit("dve", "nop", waits=[(stw, vv)], inc=s_dve)

    for g in range(NGRP):
        for t in range(NTG):
            gt = g * NTG + t
            pre_tile_chain(x_d[gt * P:(gt + 1) * P, :],
                           kq_d[:, gt * HD:(gt + 1) * HD],
                           kq2_d[:, gt * HD:(gt + 1) * HD],
                           TCOLS[:, gt:gt + 1], t, False, 14)
        tables_done = group_tables(NTG)
        kguard = ([(s_te, s_te.n)] if g > 0
                  else [(s_te, q1_te_done), (sptw, prep_store_n)])
        wk = load_wkv_half(0, kguard)
        emit("te", "nop", waits=[(swkv, wk), (s_dve, tables_done)])
        for t in range(NTG):
            tvp = proj_tiles(t, WKVH, 0, D, D)
            k_sweep_tile(g, t, tvp)
        wk = load_wkv_half(1, [(s_te, s_te.n)])
        emit("te", "nop", waits=[(swkv, wk)])
        for t in range(NTG):
            tvp = proj_tiles(t, WKVH, 0, D, 2 * D)
            v_sweep_tile(g, t, tvp)
        ptr_slice(g + 1, n_ptr)

    A.release("wres", "akv_fm", "ptm", "xpipe", "sq", "xnn",
              "tile_bf", "fm_bf", "rrg", "rig", "kq",
              "lnhg_e", "lnhb_e", "brows")
    kv_all_stored = stw.n

    A.alloc("xn_fm", DCH * TLOC // 2)
    XNF = A.b("xn_fm", DCH * TLOC)
    srcx = bass.AP(xn_sp.tensor, 0, [[TLOC, P], [P * TLOC, DCH], [1, TLOC]])
    ph1_te_done = s_te.n
    xn_rl = dma("sp", srd, XNF.rearrange("p (ki n) -> p ki n", ki=DCH), srcx,
                waits=[(stw, stw.n), (s_te, ph1_te_done), (s_dve, s_dve.n),
                       (sptw, sptw.n)])
    A.alloc("m_fm", ICH * TLOC // 2)
    MFM = A.b("m_fm", ICH * TLOC)
    A.alloc("wgu_t", 2 * DCH * P)
    GBUF = A.b("wgu_t", 4 * DCH * P)
    A.alloc("silu_t", TLOC)
    SILU = A.f("silu_t", TLOC)

    gu_guard = [None, None]
    for j in range(ICH):
        lv = 0
        for s_i, blk in enumerate((j, ICH + j)):
            dbuf = GBUF[:, ((j % 2) * 2 + s_i) * DCH * P:
                        ((j % 2) * 2 + s_i + 1) * DCH * P]
            srcb = bass.AP(wguT.tensor, blk * D * P,
                           [[P, P], [P * P, DCH], [1, P]])
            w = []
            if s_i == 0:
                w = [(srd, srd.n)]
            if j < 2:
                w += [(sptw, sptw.n), (s_te, ph1_te_done)]
            elif s_i == 0 and gu_guard[j % 2] is not None:
                w += [gu_guard[j % 2]]
            lv = dma("sp", srd, dbuf.rearrange("p (ki c) -> p ki c", ki=DCH),
                     srcb, waits=w)
        emit("te", "nop", waits=[(srd, lv), (srd, xn_rl)])
        for s_i in range(2):
            dbuf = GBUF[:, ((j % 2) * 2 + s_i) * DCH * P:
                        ((j % 2) * 2 + s_i + 1) * DCH * P]
            for ki in range(DCH):
                emit("te", "ldweights", dbuf[:, ki * P:(ki + 1) * P])
                for bi, (to, tw) in enumerate(TLB):
                    bidx = s_i * 2 + bi
                    w = bgw(bidx) if ki == 0 else []
                    tv = emit("te", "matmul", bank(bidx, tw),
                              dbuf[:, ki * P:(ki + 1) * P],
                              XNF[:, ki * TLOC + to: ki * TLOC + to + tw],
                              start=(ki == 0), stop=(ki == DCH - 1), waits=w,
                              inc=(s_te if (s_i == 1 and ki == DCH - 1
                                            and bi == NTB - 1) else None))
        gu_guard[j % 2] = (s_te, tv)
        for bi, (to, tw) in enumerate(TLB):
            av = emit("act", "activation", SILU[:, to:to + tw], bank(bi, tw),
                      AFT.Silu, waits=[(s_te, tv)] if bi == 0 else [],
                      inc=(s_act if bi == NTB - 1 else None))
        for bi, (to, tw) in enumerate(TLB):
            emit("dve", "tensor_tensor",
                 MFM[:, j * TLOC + to: j * TLOC + to + tw],
                 SILU[:, to:to + tw], bank(2 + bi, tw), OP.mult,
                 waits=[(s_act, av)] if bi == 0 else [])
        drain("dve")
        mk = emit("dve", "nop", inc=s_dve)
        for bidx in range(2 * NTB + 2):
            if bidx < 8:
                bg[bidx] = (s_dve, mk)
    A.release("wgu_t", "silu_t", "xn_fm")
    up_te_done = s_te.n
    up_dve_done = s_dve.n

    A.alloc("wdn_t", 2 * D)
    WDN = A.b("wdn_t", 2 * D)
    A.alloc("ffn_t", D)
    FFT = A.f("ffn_t", D)
    dn_guard = [None, None]
    out_stash = 0
    for tp in range(0, NTO, 2):
        ntp = min(2, NTO - tp)
        for i in range(ICH):
            dbuf = WDN[:, (i % 2) * D:(i % 2) * D + D]
            w = [(srd, srd.n)]
            if dn_guard[i % 2] is not None:
                w.append(dn_guard[i % 2])
            if tp == 0 and i < 2:
                w += [(sptw, sptw.n), (s_te, up_te_done), (s_dve, up_dve_done)]
            lv = dma("sp", srd, dbuf, wdnT[i * P:(i + 1) * P, :], waits=w)
            emit("te", "nop", waits=[(srd, lv)])
            for tti in range(ntp):
                tt = tp + tti
                emit("te", "ldweights",
                     MFM[:, i * TLOC + tt * P: i * TLOC + (tt + 1) * P])
                for ci, (co, cw) in enumerate(COGS):
                    bidx = tti * NCG + ci
                    w2 = bgw(bidx) if i == 0 else []
                    tvd = emit("te", "matmul", bank(bidx, cw),
                               MFM[:, i * TLOC + tt * P: i * TLOC + (tt + 1) * P],
                               WDN[:, (i % 2) * D + co:(i % 2) * D + co + cw],
                               start=(i == 0), stop=(i == ICH - 1), waits=w2,
                               inc=(s_te if (tti == ntp - 1 and ci == NCG - 1)
                                    else None))
            dn_guard[i % 2] = (s_te, tvd)
        tv = dn_guard[(ICH - 1) % 2][1]
        for tti in range(ntp):
            tt = tp + tti
            for ci, (co, cw) in enumerate(COGS):
                bidx = tti * NCG + ci
                emit("dve", "scalar_tensor_tensor", FFT[:, co:co + cw],
                     bank(bidx, cw), 1.0, A.b("bo_b", D)[:, co:co + cw],
                     OP.mult, OP.add,
                     waits=([(s_te, tv), (stw, out_stash)]
                            if (tti == 0 and ci == 0) else []))
            drain("dve")
            dv = emit("dve", "nop", inc=s_dve)
            out_stash = dma("sp", stw, ffn_sp[tt * P:(tt + 1) * P, :], FFT,
                            waits=[(s_dve, dv), (stw, stw.n)])
            mk = emit("dve", "nop", waits=[(stw, out_stash)], inc=s_dve)
            for ci in range(NCG):
                bg[tti * NCG + ci] = (s_dve, mk)
    A.release("wdn_t", "ffn_t", "m_fm")

    A.alloc("qn_fm", DCH * TLOC // 2)
    QNF = A.b("qn_fm", DCH * TLOC)
    srcq = bass.AP(qn_sp.tensor, 0, [[TLOC, P], [P * TLOC, DCH], [1, TLOC]])
    ph2_te_done = s_te.n
    qn_rl = dma("sp", srd, QNF.rearrange("p (ki n) -> p ki n", ki=DCH), srcq,
                waits=[(s_te, ph2_te_done), (s_dve, s_dve.n)])
    A.alloc("kn_h", NKC * P)
    KNB = A.b("kn_h", 2 * NKC * P)
    A.alloc("v_h", NKC * HD)
    VB = A.b("v_h", 2 * NKC * HD)
    A.alloc("p_buf", NKC * QT // 2)
    PBUF = A.b("p_buf", NKC * QT)
    A.alloc("attn_fm", DCH * TLOC // 2)
    ATTN = A.b("attn_fm", DCH * TLOC)
    A.alloc("lrec", QT)
    LREC = A.f("lrec", QT)

    kn_loads = [0] * H
    head_done = [None] * (H + 2)

    def load_head(h):
        par = h % 2
        w = [(stw, kv_all_stored), (s_te, ph2_te_done), (srd, srd.n)]
        if h >= 2 and head_done[h - 2] is not None:
            w = [head_done[h - 2], (srd, srd.n)]
        last = 0
        for g in range(NGRP):
            last = dma("sp", srd,
                       KNB[:, par * NKC * P + g * GT:par * NKC * P + (g + 1) * GT],
                       st_kn[g * D + h * P: g * D + (h + 1) * P, :],
                       waits=w if g == 0 else [])
        srcv = bass.AP(st_v.tensor, h * NKC * P * HD,
                       [[HD, P], [P * HD, NKC], [1, HD]])
        last = dma("sp", srd,
                   VB[:, par * NKC * HD:(par + 1) * NKC * HD]
                   .rearrange("p (kc c) -> p kc c", kc=NKC), srcv)
        kn_loads[h] = last

    load_head(0)
    if H > 1:
        load_head(1)
    emit("act", "nop", waits=[(s_dve, negb_done)])
    for h in range(H):
        par = h % 2
        kn_h = KNB[:, par * NKC * P:(par + 1) * NKC * P]
        v_h = VB[:, par * NKC * HD:(par + 1) * NKC * HD]
        emit("te", "nop", waits=[(srd, kn_loads[h]), (srd, qn_rl),
                                 (s_dve, negb_done)])
        for qi in range(NQT):
            q0 = qi * QT
            rhs_q = QNF[:, h * TLOC + q0: h * TLOC + q0 + QT]
            exp_marks = []
            QKB = (0, 1, 4, 5)
            for c0 in range(0, NKC, 2):
                b_i = QKB[(c0 // 2) % 4]
                npair = min(2, NKC - c0)
                for cc in range(npair):
                    w = bgw(b_i) if cc == 0 else []
                    emit("te", "ldweights",
                         kn_h[:, (c0 + cc) * P:(c0 + cc + 1) * P])
                    tv = emit("te", "matmul", bank(b_i, QT, o=cc * QT),
                              kn_h[:, (c0 + cc) * P:(c0 + cc + 1) * P], rhs_q,
                              start=True, stop=True, waits=w,
                              inc=(s_te if cc == npair - 1 else None))
                av = emit("act", "activation",
                          PBUF[:, c0 * QT:(c0 + npair) * QT],
                          bank(b_i, npair * QT), AFT.Exp, bias=NEGB, scale=1.0,
                          waits=[(s_te, tv)], inc=s_act)
                exp_marks.append(av)
                bg[b_i] = (s_act, av)
            for c in range(NKC):
                w = [(s_act, exp_marks[c // 2])]
                if c == 0:
                    w += bgw(2)
                emit("te", "ldweights", v_h[:, c * HD:(c + 1) * HD], waits=w)
                av_tv = emit("te", "matmul", bank(2, QT),
                             v_h[:, c * HD:(c + 1) * HD],
                             PBUF[:, c * QT:(c + 1) * QT],
                             start=(c == 0), stop=(c == NKC - 1),
                             inc=(s_te if c == NKC - 1 else None))
            emit("te", "ldweights", ONES, waits=bgw(3))
            for c in range(NKC):
                l_tv = emit("te", "matmul", bank(3, QT)[0:1, :], ONES,
                            PBUF[:, c * QT:(c + 1) * QT],
                            start=(c == 0), stop=(c == NKC - 1),
                            inc=(s_te if c == NKC - 1 else None))
            emit("dve", "tensor_copy", LROW[0:1, :], bank(3, QT)[0:1, :],
                 waits=[(s_te, l_tv)])
            drain("dve")
            emit("dve", "reciprocal", LROW[0:1, :], LROW[0:1, :])
            drain("dve")
            dv = emit("dve", "nop", inc=s_dve)
            bg[3] = (s_dve, dv)
            tv = emit("te", "matmul", bank(6, QT), ONESF[0:1, :],
                      LROW[0:1, :], start=True, stop=True,
                      waits=[(s_dve, dv)] + bgw(6), inc=s_te)
            emit("dve", "tensor_copy", LREC, bank(6, QT), waits=[(s_te, tv)])
            drain("dve")
            lb = emit("dve", "nop", inc=s_dve)
            bg[6] = (s_dve, lb)
            emit("dve", "tensor_tensor",
                 ATTN[:, h * TLOC + q0: h * TLOC + q0 + QT], bank(2, QT), LREC,
                 OP.mult, waits=[(s_te, av_tv)])
            drain("dve")
            bg[2] = (s_dve, emit("dve", "nop", inc=s_dve))
        head_done[h] = (s_te, l_tv)
        if h + 2 < H:
            load_head(h + 2)
    attn_done = emit("dve", "nop", inc=s_dve)
    A.release("kn_h", "v_h", "p_buf", "qn_fm", "lrec")

    A.alloc("wo_r", DCH * D // 2)
    WOB = A.b("wo_r", DCH * D)
    srcw2 = wo_d[:, :].rearrange("(ki p) n -> p ki n", p=P)
    dstw2 = WOB.rearrange("p (ki n) -> p ki n", ki=DCH)
    wo_loaded = 0
    for kk in range(0, DCH, 4):
        kn = min(4, DCH - kk)
        wo_loaded = dma("gp", swwo, dstw2[:, kk:kk + kn], srcw2[:, kk:kk + kn],
                        waits=[(s_te, head_done[H - 1][1] if head_done[H - 1]
                                else 0), (s_dve, attn_done)] if kk == 0 else [])
    A.alloc("ffn_in", D)
    FIN = A.f("ffn_in", D)
    A.alloc("x_in", D)
    XIN = A.f("x_in", D)
    A.alloc("out_t", D)
    OUTT = A.f("out_t", D)
    emit("te", "nop", waits=[(swwo, wo_loaded), (s_dve, attn_done)])
    out_guard = 0
    for tt in range(NTO):
        fv = dma("sp", srd, FIN, ffn_sp[tt * P:(tt + 1) * P, :],
                 waits=[(s_dve, out_guard if tt > 0 else attn_done)])
        xv = dma("sp", srd, XIN, xow_d[tt * P:(tt + 1) * P, :])
        for ki in range(DCH):
            emit("te", "ldweights",
                 ATTN[:, ki * TLOC + tt * P: ki * TLOC + (tt + 1) * P])
            for ci, (co, cw) in enumerate(COGS):
                w = bgw(ci) if ki == 0 else []
                tv = emit("te", "matmul", bank(ci, cw),
                          ATTN[:, ki * TLOC + tt * P: ki * TLOC + (tt + 1) * P],
                          WOB[:, ki * D + co: ki * D + co + cw],
                          start=(ki == 0), stop=(ki == DCH - 1), waits=w,
                          inc=(s_te if (ki == DCH - 1 and ci == NCG - 1)
                               else None))
        for ci, (co, cw) in enumerate(COGS):
            w = [(s_te, tv), (srd, xv)] if ci == 0 else []
            emit("dve", "tensor_tensor", OUTT[:, co:co + cw], bank(ci, cw),
                 FIN[:, co:co + cw], OP.add, waits=w)
        drain("dve")
        emit("dve", "tensor_tensor", OUTT, OUTT, XIN, OP.add,
             waits=[(srd, xv)])
        drain("dve")
        dv = emit("dve", "nop", inc=s_dve)
        for ci in range(NCG):
            bg[ci] = (s_dve, dv)
        ov = dma("sp", stw, y_d[tt * P:(tt + 1) * P, :], OUTT,
                 waits=[(s_dve, dv), (stw, stw.n)])
        out_guard = emit("dve", "nop", waits=[(stw, ov)], inc=s_dve)

    if dbg:
        dbgs = [("d_xn", xn_sp[:, :]), ("d_qn", qn_sp[:, :]),
                ("d_kn", st_kn[0:D, :]), ("d_v", st_v[0:NKC * P, :]),
                ("d_ffn", ffn_sp[:, :]), ("d_wgu", wguT[0:D, :]),
                ("d_wdn", wdnT[0:P, :])]
        for nm, srcd in dbgs:
            od = nc.dram_tensor(nm, list(srcd.shape),
                                srcd.dtype, kind="ExternalOutput").ap()
            dma("gp", swgp, od[:, :], srcd,
                waits=[(stw, stw.n), (sptw, sptw.n), (srd, srd.n)])
        emit("gp", "nop", waits=[(swgp, swgp.n)])
    emit("sp", "nop", waits=[(stw, stw.n)])
    emit("te", "nop", waits=[(stw, stw.n)])
    emit("gp", "nop", waits=[(stw, stw.n)])
    emit("act", "nop", waits=[(stw, stw.n)])

    def render(q):
        def body(eng):
            for meth, args, kw, waits, incs in Q[q]:
                for s, v in waits:
                    eng.wait_ge(s, v)
                if meth == "nop":
                    if incs:
                        ins = eng.nop()
                        for s, k in incs:
                            ins.then_inc(s, k)
                    continue
                ins = getattr(eng, meth)(*args, **kw)
                if ins is not None:
                    for s, k in incs:
                        ins.then_inc(s, k)
        return body

    with nc.Block() as blk:
        blk.tensor(render("te"))
        blk.scalar(render("act"))
        blk.vector(render("dve"))
        blk.sync(render("sp"))
        blk.gpsimd(render("gp"))
    ctx.close()
    return nc


def _consts(cfg: Cfg):
    f = ref_freqs(cfg.HD)
    fdup = np.repeat(f, 2)[None, :].repeat(P, 0).astype(np.float32)
    tcols = (np.arange(cfg.T, dtype=np.float32).reshape(cfg.T // P, P).T)
    return (np.ascontiguousarray(fdup), np.ascontiguousarray(tcols),
            np.eye(P).astype(ml_dtypes.bfloat16),
            np.ones((P, 1), ml_dtypes.bfloat16))


def _kq_tables(cfg: Cfg, t0, t1):
    f = ref_freqs(cfg.HD)
    fdup = np.repeat(f, 2).astype(np.float32)
    t_idx = np.arange(t0, t1, dtype=np.float32)
    ang32 = (t_idx[:, None] * fdup[None, :]).astype(np.float32)
    k1 = np.floor(ang32.astype(np.float64) / (2 * np.pi))
    k2 = np.floor((ang32.astype(np.float64) + np.float64(np.float32(np.pi / 2)))
                  / (2 * np.pi))
    nt = (t1 - t0) // P

    def lay(k):
        k = k.astype(np.float32).reshape(nt, P, cfg.HD)
        return np.ascontiguousarray(k.transpose(1, 0, 2).reshape(P, nt * cfg.HD))

    return lay(k1), lay(k2)


def _bc(v):
    return np.ascontiguousarray(
        np.broadcast_to(np.asarray(v, np.float32).reshape(1, -1), (P, len(np.asarray(v).reshape(-1)))))


def run(cfg: Cfg, inputs: dict, runner=None, dbg=False):
    nc = build(cfg, dbg=dbg)
    fdup, tcols, ident, ones = _consts(cfg)
    x = np.ascontiguousarray(np.asarray(inputs["L0_attn_x"], np.float32)
                             .reshape(cfg.T, cfg.D))
    base = {
        "x_full": x,
        "Wq": np.ascontiguousarray(np.asarray(inputs["L0_attn_Wq"], np.float32)),
        "Wkv": np.ascontiguousarray(np.asarray(inputs["L0_attn_Wkv"], np.float32)),
        "Wo": np.ascontiguousarray(np.asarray(inputs["L0_attn_Wo"], np.float32)),
        "Wgu": np.ascontiguousarray(np.asarray(inputs["L0_ffn_w_gate_up"],
                                               np.float32)),
        "Wdn": np.ascontiguousarray(np.asarray(inputs["L0_ffn_w_down"],
                                               np.float32)),
        "ln_q_g": _bc(inputs["L0_attn_ln_q_g"]),
        "ln_q_b": _bc(inputs["L0_attn_ln_q_b"]),
        "ln_kv_g": _bc(inputs["L0_attn_ln_kv_g"]),
        "ln_kv_b": _bc(inputs["L0_attn_ln_kv_b"]),
        "ln_h_g": _bc(inputs["L0_attn_ln_h_g"]),
        "ln_h_b": _bc(inputs["L0_attn_ln_h_b"]),
        "bq": _bc(inputs["L0_attn_bq"]),
        "bkv": _bc(inputs["L0_attn_bkv"]),
        "bo": _bc(inputs["L0_attn_bo"]),
        "ident_c": ident, "ones_c": ones, "fdup_c": fdup, "tcols_c": tcols,
        "onesf_c": np.ones((1, P), np.float32),
        "onesbr_c": np.ones((1, P), ml_dtypes.bfloat16),
    }
    kq1, kq2 = _kq_tables(cfg, 0, cfg.T)
    base["kq_c"], base["kq2_c"] = kq1, kq2

    def _col(v):
        return np.ascontiguousarray(
            np.asarray(v, np.float32).reshape(cfg.DCH, P).T)
    base["gqc_c"] = _col(inputs["L0_attn_ln_q_g"])
    base["bqc_c"] = _col(inputs["L0_attn_ln_q_b"])
    base["gkc_c"] = _col(inputs["L0_attn_ln_kv_g"])
    base["bkc_c"] = _col(inputs["L0_attn_ln_kv_b"])
    base = {k: np.ascontiguousarray(v) for k, v in base.items()}
    in_maps = []
    tl = cfg.TLOC
    for c in range(NCORES):
        m = dict(base)
        m["x_own"] = np.ascontiguousarray(x[c * tl:(c + 1) * tl, :])
        m["tcown_c"] = np.ascontiguousarray(
            np.arange(c * tl, (c + 1) * tl, dtype=np.float32)
            .reshape(tl // P, P).T)
        m["kqo_c"], m["kqo2_c"] = _kq_tables(cfg, c * tl, (c + 1) * tl)
        in_maps.append(m)
    if runner is None:
        res = run_bass_kernel_spmd(nc, in_maps, core_ids=list(range(NCORES)))
        outs = [res.results[c]["y"] for c in range(NCORES)]
        if dbg:
            y = np.concatenate([np.asarray(o, np.float32) for o in outs], axis=0)
            return y.reshape(1, cfg.T, cfg.D), res.results
    else:
        outs = runner(nc, in_maps)
    y = np.concatenate([np.asarray(o, np.float32) for o in outs], axis=0)
    return y.reshape(1, cfg.T, cfg.D)


def kernel(**inputs):
    return run(FULL, inputs)


# version 11
# speedup vs baseline: 1.1394x; 1.0207x over previous
from contextlib import ExitStack
from dataclasses import dataclass

import numpy as np
import ml_dtypes

import concourse.bass as bass
import concourse.mybir as mybir
from concourse.bass_utils import run_bass_kernel_spmd
from concourse.alu_op_type import AluOpType as OP
F32 = mybir.dt.float32
BF16 = mybir.dt.bfloat16
AFT = mybir.ActivationFunctionType
AX = mybir.AxisListType

NCORES = 8
P = 128
EPS = 1e-5


@dataclass
class Cfg:
    D: int = 2048
    T: int = 8192
    H: int = 16
    HD: int = 128
    INTER: int = 8192
    QT: int = 256

    @property
    def TLOC(self): return self.T // NCORES
    @property
    def DCH(self): return self.D // P
    @property
    def ICH(self): return self.INTER // P
    @property
    def GT(self): return self.T // 8
    @property
    def NTG(self): return self.GT // P
    @property
    def NTO(self): return self.TLOC // P
    @property
    def NKC(self): return self.T // P
    @property
    def NQT(self): return self.TLOC // self.QT
    @property
    def COGS(self): return [(o, min(512, self.D - o)) for o in range(0, self.D, 512)]
    @property
    def TLB(self): return [(o, min(512, self.TLOC - o))
                           for o in range(0, self.TLOC, 512)]


FULL = Cfg()
MINI = Cfg(D=256, T=2048, H=2, HD=128, INTER=1024, QT=256)

SCL = float(np.float32(128.0 ** -0.25))
TWO_PI = float(np.float64(2 * np.pi))
INV_2PI = float(np.float32(1.0 / (2 * np.pi)))
_c1 = np.float32(6.28125)
_c2 = np.float32(np.float64(TWO_PI) - np.float64(_c1))
_c3 = np.float32(np.float64(TWO_PI) - np.float64(_c1) - np.float64(_c2))
C1, C2, C3 = float(_c1), float(_c2), float(_c3)
PI = float(np.float32(np.pi))
PI_2 = float(np.float32(np.pi / 2))


def ref_freqs(hd):
    lin = np.linspace(np.float32(0.0), np.float32(1.0), hd // 2, dtype=np.float32)
    f = np.float32(40.0) ** lin
    f = f * np.float32(200.0) / np.float32(1000.0)
    return (np.float32(200.0) * f / np.float32(1000.0)).astype(np.float32)


class Sem:
    def __init__(self, h):
        self.h, self.n = h, 0

    def inc(self, k=1):
        self.n += k
        return self.n


class Arena:
    def __init__(self, ap, total):
        self.ap, self.total = ap, total
        self.free = [(0, total)]
        self.live = {}

    def alloc(self, name, units):
        units = (units + 7) & ~7
        best = None
        for i, (off, sz) in enumerate(self.free):
            if sz >= units and (best is None or sz < self.free[best][1]):
                best = i
        if best is None:
            raise RuntimeError(f"arena OOM {name} needs {units} free={self.free} "
                               f"live={ {k: v[1] for k, v in self.live.items()} }")
        off, sz = self.free[best]
        self.free[best] = (off + units, sz - units)
        if self.free[best][1] == 0:
            self.free.pop(best)
        self.live[name] = (off, units)
        return self

    def alloc_high(self, name, units):
        units = (units + 7) & ~7
        for i in range(len(self.free) - 1, -1, -1):
            off, sz = self.free[i]
            if sz >= units:
                self.free[i] = (off, sz - units)
                if self.free[i][1] == 0:
                    self.free.pop(i)
                self.live[name] = (off + sz - units, units)
                return self
        raise RuntimeError(f"arena OOM(high) {name} needs {units}")

    def release(self, *names):
        for name in names:
            off, sz = self.live.pop(name)
            self.free.append((off, sz))
        self.free.sort()
        m = []
        for o, s in self.free:
            if m and m[-1][0] + m[-1][1] == o:
                m[-1][1] += s
            else:
                m.append([o, s])
        self.free = [tuple(x) for x in m]

    def f(self, name, cols=None):
        off, sz = self.live[name]
        return self.ap[:, off:off + (sz if cols is None else cols)]

    def b(self, name, cols=None):
        off, sz = self.live[name]
        v = self.ap[:, off:off + sz].bitcast(BF16)
        return v if cols is None else v[:, 0:cols]


def build(cfg: Cfg, dbg: bool = False):
    D, T, H, HD, INTER, QT = cfg.D, cfg.T, cfg.H, cfg.HD, cfg.INTER, cfg.QT
    DCH, ICH, GT, NTG, NTO = cfg.DCH, cfg.ICH, cfg.GT, cfg.NTG, cfg.NTO
    NKC, NQT, TLOC = cfg.NKC, cfg.NQT, cfg.TLOC
    COGS, TLB = cfg.COGS, cfg.TLB
    NCG, NTB = len(COGS), len(TLB)
    NGRP = T // GT

    nc = bass.Bass("TRN2", target_bir_lowering=False)

    x_d = nc.dram_tensor("x_full", [T, D], F32, kind="ExternalInput").ap()
    xow_d = nc.dram_tensor("x_own", [TLOC, D], F32, kind="ExternalInput").ap()
    wq_d = nc.dram_tensor("Wq", [D, D], F32, kind="ExternalInput").ap()
    wkv_d = nc.dram_tensor("Wkv", [D, 2 * D], F32, kind="ExternalInput").ap()
    wo_d = nc.dram_tensor("Wo", [D, D], F32, kind="ExternalInput").ap()
    wgu_d = nc.dram_tensor("Wgu", [2 * INTER, D], F32, kind="ExternalInput").ap()
    wdn_d = nc.dram_tensor("Wdn", [D, INTER], F32, kind="ExternalInput").ap()
    V = {}
    for nm, n in [("ln_q_g", D), ("ln_q_b", D), ("ln_kv_g", D), ("ln_kv_b", D),
                  ("ln_h_g", HD), ("ln_h_b", HD), ("bq", D), ("bkv", 2 * D),
                  ("bo", D)]:
        V[nm] = nc.dram_tensor(nm, [P, n], F32, kind="ExternalInput").ap()
    ident_d = nc.dram_tensor("ident_c", [P, P], BF16, kind="ExternalInput").ap()
    ones_d = nc.dram_tensor("ones_c", [P, 1], BF16, kind="ExternalInput").ap()
    onesf_d = nc.dram_tensor("onesf_c", [1, P], F32, kind="ExternalInput").ap()
    onesbr_d = nc.dram_tensor("onesbr_c", [1, P], BF16, kind="ExternalInput").ap()
    fdup_d = nc.dram_tensor("fdup_c", [P, HD], F32, kind="ExternalInput").ap()
    tcols_d = nc.dram_tensor("tcols_c", [P, T // P], F32, kind="ExternalInput").ap()
    tco_d = nc.dram_tensor("tcown_c", [P, NTO], F32, kind="ExternalInput").ap()
    kq_d = nc.dram_tensor("kq_c", [P, (T // P) * HD], F32,
                          kind="ExternalInput").ap()
    kq2_d = nc.dram_tensor("kq2_c", [P, (T // P) * HD], F32,
                           kind="ExternalInput").ap()
    kqo_d = nc.dram_tensor("kqo_c", [P, NTO * HD], F32,
                           kind="ExternalInput").ap()
    kqo2_d = nc.dram_tensor("kqo2_c", [P, NTO * HD], F32,
                            kind="ExternalInput").ap()
    gqc_d = nc.dram_tensor("gqc_c", [P, DCH], F32, kind="ExternalInput").ap()
    bqc_d = nc.dram_tensor("bqc_c", [P, DCH], F32, kind="ExternalInput").ap()
    gkc_d = nc.dram_tensor("gkc_c", [P, DCH], F32, kind="ExternalInput").ap()
    bkc_d = nc.dram_tensor("bkc_c", [P, DCH], F32, kind="ExternalInput").ap()
    y_d = nc.dram_tensor("y", [TLOC, D], F32, kind="ExternalOutput").ap()

    wq_bf = nc.dram_tensor("wq_bf", [D, D], BF16).ap()
    wkv_bf = nc.dram_tensor("wkv_bf", [D, 2 * D], BF16).ap()
    wgu_bf = nc.dram_tensor("wgu_bf", [2 * INTER, D], BF16).ap()
    wdn_bf = nc.dram_tensor("wdn_bf", [D, INTER], BF16).ap()
    wguT = nc.dram_tensor("wguT", [2 * ICH * D, P], BF16).ap()
    wdnT = nc.dram_tensor("wdnT", [ICH * P, D], BF16).ap()
    st_kn = nc.dram_tensor("st_kn", [NGRP * D, GT], BF16).ap()
    st_v = nc.dram_tensor("st_v", [H * NKC * P, HD], BF16).ap()
    qn_sp = nc.dram_tensor("qn_sp", [D, TLOC], BF16).ap()
    xn_sp = nc.dram_tensor("xn_sp", [D, TLOC], BF16).ap()
    ffn_sp = nc.dram_tensor("ffn_sp", [TLOC, D], F32).ap()

    ctx = ExitStack()
    AR_U = 49600
    arena_t = ctx.enter_context(nc.sbuf_tensor("arena", [P, AR_U], F32))
    A = Arena(arena_t[:], AR_U)
    small_t = ctx.enter_context(nc.sbuf_tensor("small", [P, 1280], F32))
    SM = small_t[:]
    psum_t = ctx.enter_context(nc.psum_tensor("psum", [P, 4096], F32))
    PS, PSB = psum_t[:], psum_t[:].bitcast(BF16)
    ident_t = ctx.enter_context(nc.sbuf_tensor("identS", [P, P], BF16))
    IDENT = ident_t[:]
    ones_t = ctx.enter_context(nc.sbuf_tensor("onesS", [P, 1], BF16))
    ONES = ones_t[:]
    onesf_t = ctx.enter_context(nc.sbuf_tensor("onesF", [1, P], F32))
    ONESF = onesf_t[:]
    trt_t = ctx.enter_context(nc.sbuf_tensor("trtS", [P, 2048], BF16,
                                             is_transpose_dest=True))
    TRT = trt_t[:]
    onesbr_t = ctx.enter_context(nc.sbuf_tensor("onesbrS", [1, P], BF16))
    ONESBR = onesbr_t[:]

    def bank(b, w=512, o=0):
        return PS[:, b * 512 + o: b * 512 + o + w]

    def bankb(b, w=1024, o=0):
        return PSB[:, b * 1024 + o: b * 1024 + o + w]

    sems = {n: Sem(ctx.enter_context(nc.semaphore(n))) for n in
            ("sx", "stw", "srd", "swgp", "swff", "ste", "sact", "sdve", "sgp",
             "swcast", "swwq", "swkv", "swwo", "sptw", "skq", "swgp2",
             "sx2", "skq2")}
    sx, stw, srd, swgp, swff = (sems[n] for n in ("sx", "stw", "srd", "swgp",
                                                  "swff"))
    s_te, s_act, s_dve, s_gp = (sems[n] for n in ("ste", "sact", "sdve", "sgp"))
    swcast, swwq, swkv, swwo, sptw, skq, swgp2 = (sems[n] for n in
                                                  ("swcast", "swwq", "swkv",
                                                   "swwo", "sptw", "skq",
                                                   "swgp2"))
    sxp = [sx, sems["sx2"]]
    skqp = [skq, sems["skq2"]]

    Q = {k: [] for k in ("te", "act", "dve", "sp", "gp")}

    def emit(q, meth, *args, waits=(), inc=None, inc16=None, **kw):
        r = None
        incs = []
        if inc is not None:
            r = inc.inc()
            incs.append((inc.h, 1))
        if inc16 is not None:
            r = inc16.inc(16)
            incs.append((inc16.h, 16))
        ws = [(s.h, v) for (s, v) in waits if s is not None and v and v > 0]
        Q[q].append((meth, args, kw, ws, incs))
        return r

    def drain(q):
        emit(q, "drain")

    def dma(q, sem, dst, src, waits=()):
        return emit(q, "dma_start", waits=waits, inc16=sem, out=dst, in_=src)

    def dma_tr(q, sem, dst, src, waits=()):
        return emit(q, "dma_start_transpose", dst, src, waits=waits, inc16=sem)

    bg = [None] * 8

    def bgw(b):
        return [] if bg[b] is None else [bg[b]]

    _smo = [0]

    def sm(n):
        o = _smo[0]
        _smo[0] += max(n, 1)
        assert _smo[0] <= 1280, _smo[0]
        return SM[:, o:o + n]

    FDUP = sm(HD)
    TCOLS = sm(T // P)
    TCO = sm(NTO)
    ST2 = sm(32)
    HL2 = sm(12 * H)
    RED1 = sm(2)
    BBFr = sm(8)
    NEGB = sm(1)
    NEGPI = sm(1)
    LROW = sm(QT)
    ANG2 = sm(2 * HD)
    WRK2X = sm(2 * HD)

    BBF = BBFr.bitcast(BF16)

    dma("sp", sx, IDENT, ident_d[:, :])
    dma("sp", sx, ONES, ones_d[:, :])
    dma("sp", sx, ONESF, onesf_d[:, :])
    dma("sp", sx, ONESBR, onesbr_d[:, :])
    dma("sp", sx, FDUP, fdup_d[:, :])
    dma("sp", sx, TCOLS, tcols_d[:, :])
    dma("sp", sx, TCO, tco_d[:, :])
    emit("dve", "memset", NEGPI, -PI)
    drain("dve")

    A.alloc("vrow", 2 * D)
    vrow = A.f("vrow")
    A.alloc("brows", 3 * D // 2)
    BROWS = A.b("brows", 3 * D)
    bc_specs = [("lnhg_e", V["ln_h_g"][:, :], HD, SCL),
                ("lnhb_e", V["ln_h_b"][:, :], HD, SCL),
                ("bo_b", V["bo"][:, :], D, None)]
    last_bc = 0
    for nm, srcv, n, scale in bc_specs:
        A.alloc(nm, (n + 1) // 2)
        vv = dma("gp", swgp, vrow[:, 0:n], srcv, waits=[(s_dve, last_bc)])
        if scale is None:
            last_bc = emit("dve", "tensor_copy", A.b(nm, n), vrow[:, 0:n],
                           waits=[(swgp, vv)], inc=s_dve)
        else:
            last_bc = emit("dve", "tensor_scalar_mul", A.b(nm, n), vrow[:, 0:n],
                           float(scale), waits=[(swgp, vv)], inc=s_dve)
        drain("dve")
    bc_done = last_bc
    av = emit("act", "activation", vrow[:, 0:HD], A.b("lnhg_e", HD), AFT.Square,
              waits=[(s_dve, last_bc)], inc=s_act)
    emit("dve", "reduce_max", RED1[:, 0:1], vrow[:, 0:HD], axis=AX.X,
         waits=[(s_act, av)])
    drain("dve")
    dv = emit("dve", "nop", inc=s_dve)
    av = emit("act", "activation", vrow[:, 0:HD], A.b("lnhb_e", HD), AFT.Square,
              waits=[(s_dve, dv)], inc=s_act)
    emit("dve", "reduce_sum", RED1[:, 1:2], vrow[:, 0:HD], axis=AX.X,
         waits=[(s_act, av)])
    drain("dve")
    dv = emit("dve", "nop", inc=s_dve)
    av = emit("act", "activation", BBFr[:, 2:4], RED1[:, 0:2], AFT.Sqrt,
              waits=[(s_dve, dv)], inc=s_act)
    emit("dve", "tensor_scalar", BBFr[:, 0:1], BBFr[:, 2:3],
         float(np.sqrt(HD)), None, OP.mult, waits=[(s_act, av)])
    drain("dve")
    emit("dve", "tensor_tensor", BBFr[:, 0:1], BBFr[:, 0:1], BBFr[:, 3:4], OP.add)
    drain("dve")
    emit("dve", "tensor_tensor", BBFr[:, 1:2], BBFr[:, 0:1], BBFr[:, 0:1], OP.mult)
    drain("dve")
    emit("dve", "tensor_scalar", NEGB, BBFr[:, 1:2], -1.0001, None, OP.mult)
    drain("dve")
    negb_done = emit("dve", "nop", inc=s_dve)

    A.alloc("wprep", 2 * D)
    A.alloc("gcolp", 2 * DCH + 4)
    WPREP = A.b("wprep", 4 * D)
    GCOL = A.f("gcolp", 2 * DCH)
    BCOLB = A.f("gcolp")[:, 2 * DCH:2 * DCH + 2].bitcast(BF16)
    gl1 = dma("sp", sx, GCOL[:, 0:DCH], gqc_d[:, :])
    gl2 = dma("sp", sx, GCOL[:, DCH:2 * DCH], gkc_d[:, :])
    A.alloc("bcolf", DCH * 2)
    BCOLF = A.f("bcolf", 2 * DCH)
    bl1 = dma("sp", sx, BCOLF[:, 0:DCH], bqc_d[:, :])
    bl2 = dma("sp", sx, BCOLF[:, DCH:2 * DCH], bkc_d[:, :])
    emit("dve", "nop", waits=[(sx, sx.n)])
    emit("dve", "tensor_scalar_mul", GCOL[:, 0:DCH], GCOL[:, 0:DCH], SCL)
    drain("dve")
    prep_specs = [("q", wq_d, wq_bf, D, 0), ("kv", wkv_d, wkv_bf, 2 * D, 1)]
    A.alloc("brow", 3 * D)
    BROW = A.f("brow", 3 * D)
    prep_done = {}
    for nm, wsrc, wdst, cols, gi in prep_specs:
        ncog = cols // 512 if cols >= 512 else 1
        cw0 = min(512, cols)
        pg = None
        for ki in range(DCH):
            buf = WPREP[:, (ki % 2) * 2 * D:(ki % 2) * 2 * D + cols]
            w = [(s_dve, bc_done)] if ki == 0 and nm == "q" else []
            w.append((sptw, sptw.n))
            psem = swgp if ki % 2 == 0 else swgp2
            cv = dma("gp", psem, buf, wsrc[ki * P:(ki + 1) * P, :], waits=w)
            bv = emit("dve", "tensor_copy", BCOLB[:, gi:gi + 1],
                      BCOLF[:, gi * DCH + ki:gi * DCH + ki + 1],
                      waits=[(psem, cv)], inc=s_dve)
            drain("dve")
            for cg in range(ncog):
                tvp = emit("te", "matmul",
                           bank(cg % 4 + (4 if cg >= 4 else 0), cw0)[0:1, :],
                           BCOLB[:, gi:gi + 1], buf[:, cg * cw0:(cg + 1) * cw0],
                           start=(ki == 0), stop=(ki == DCH - 1),
                           waits=[(s_dve, bv)] if cg == 0 else [],
                           inc=(s_te if cg == ncog - 1 else None))
            sv = emit("dve", "tensor_scalar_mul", buf, buf,
                      GCOL[:, gi * DCH + ki:gi * DCH + ki + 1],
                      waits=[(s_te, tvp)], inc=s_dve)
            drain("dve")
            if nm == "kv":
                sv = emit("dve", "tensor_scalar_mul", buf[:, 0:D], buf[:, 0:D],
                          SCL, inc=s_dve)
                drain("dve")
            dma("sp", sptw, wdst[ki * P:(ki + 1) * P, :], buf,
                waits=[(s_dve, sv)])
        off = 0 if nm == "q" else D
        for cg in range(ncog):
            emit("dve", "tensor_copy", BROW[0:1, off + cg * cw0: off + (cg + 1) * cw0],
                 bank(cg % 4 + (4 if cg >= 4 else 0), cw0)[0:1, :],
                 waits=[(s_te, tvp)] if cg == 0 else [])
        drain("dve")
        prep_done[nm] = emit("dve", "nop", inc=s_dve)
        for b_ in range(8):
            bg[b_] = (s_dve, prep_done[nm])
    A.alloc("rowt", 2 * D)
    ROWT = A.f("rowt", 2 * D)
    r1 = dma("sp", sx, ROWT[0:1, 0:D], V["bq"][0:1, :],
             waits=[(s_dve, prep_done["kv"])])
    r2 = dma("sp", sx, ROWT[0:1, D:2 * D], V["bkv"][0:1, 0:D])
    emit("dve", "nop", waits=[(sx, r2)])
    emit("dve", "tensor_tensor", ROWT[0:1, 0:D], ROWT[0:1, 0:D], BROW[0:1, 0:D],
         OP.add)
    emit("dve", "tensor_tensor", ROWT[0:1, D:2 * D], ROWT[0:1, D:2 * D],
         BROW[0:1, D:2 * D], OP.add)
    drain("dve")
    emit("dve", "tensor_scalar", BROWS[0:1, 0:2 * D], ROWT[0:1, 0:2 * D], SCL,
         None, OP.mult)
    drain("dve")
    bvd = emit("dve", "nop", inc=s_dve)
    r3 = dma("sp", sx, ROWT[0:1, 0:D], V["bkv"][0:1, D:2 * D],
             waits=[(s_dve, bvd)])
    emit("dve", "nop", waits=[(sx, r3)])
    emit("dve", "tensor_tensor", BROWS[0:1, 2 * D:3 * D], ROWT[0:1, 0:D],
         BROW[0:1, 2 * D:3 * D], OP.add)
    drain("dve")
    A.release("rowt")
    prep_all = emit("dve", "nop", inc=s_dve)
    A.release("vrow", "wprep", "gcolp", "bcolf", "brow")

    ch = max(P, 2 * INTER // 8)
    for i in range(0, 2 * INTER, ch):
        dma("gp", swcast, wgu_bf[i:i + ch, :], wgu_d[i:i + ch, :])
    ch2 = max(P, D // 4)
    for i in range(0, D, ch2):
        dma("gp", swcast, wdn_bf[i:i + ch2, :], wdn_d[i:i + ch2, :])
    ffn_cast_done = swcast.n


    GU_ROWS = min(1024, 2 * INTER)
    DN_ROWS = min(1024, D)
    _ptr_tiles = []
    for j0 in range(0, 2 * INTER, GU_ROWS):
        for ki in range(DCH):
            _ptr_tiles.append(("gu", j0, ki))
    for h0 in range(0, D, DN_ROWS):
        for ic in range(ICH):
            _ptr_tiles.append(("dn", h0, ic))

    def ptr_slice(idx, total):
        n = len(_ptr_tiles)
        lo, hi = idx * n // total, (idx + 1) * n // total
        for ti in range(lo, hi):
            kind, a0, cc = _ptr_tiles[ti]
            buf_all = TRT
            rows = GU_ROWS if kind == "gu" else DN_ROWS
            buf = buf_all[:, (ti % 2) * 1024:(ti % 2) * 1024 + rows]
            w = [(swcast, ffn_cast_done)] if ti in (lo, lo + 1) else []
            w.append((sptw, sptw.n))
            if kind == "gu":
                src = wgu_bf[a0:a0 + rows, cc * P:(cc + 1) * P]
                tv = dma_tr("sp", swff, buf, src, waits=w)
                nblk = rows // P
                dst = bass.AP(wguT.tensor, ((a0 // P) * D + cc * P) * P,
                              [[P, P], [D * P, nblk], [1, P]])
                s3 = buf.rearrange("p (j c) -> p j c", j=nblk)
                dma("sp", sptw, dst, s3, waits=[(swff, tv)])
            else:
                src = wdn_bf[a0:a0 + rows, cc * P:(cc + 1) * P]
                tv = dma_tr("sp", swff, buf, src, waits=w)
                dma("sp", sptw, wdnT[cc * P:(cc + 1) * P, a0:a0 + rows], buf,
                    waits=[(swff, tv)])

    def rsqrt_ref(dst, src, t0, t1):
        av = emit("act", "activation", t0, src, AFT.Sqrt, inc=s_act)
        emit("dve", "reciprocal", dst, t0, waits=[(s_act, av)])
        drain("dve")
        emit("dve", "tensor_tensor", t0, dst, dst, OP.mult)
        drain("dve")
        emit("dve", "tensor_tensor", t0, t0, src, OP.mult)
        drain("dve")
        emit("dve", "tensor_scalar", t1, t0, -0.5, 1.5, OP.mult, OP.add)
        drain("dve")
        emit("dve", "tensor_tensor", dst, dst, t1, OP.mult)
        drain("dve")

    def ln_stats(par, xt, sq):
        ST = ST2[:, par * 16:(par + 1) * 16]
        emit("dve", "reduce_sum", ST[:, 0:1], xt, axis=AX.X)
        av = emit("act", "activation", sq, xt, AFT.Square, inc=s_act)
        emit("dve", "reduce_sum", ST[:, 1:2], sq, axis=AX.X, waits=[(s_act, av)])
        drain("dve")
        emit("dve", "tensor_scalar", ST[:, 2:4], ST[:, 0:2], 1.0 / D, None,
             OP.mult)
        drain("dve")
        emit("dve", "tensor_tensor", ST[:, 15:16], ST[:, 2:3], ST[:, 2:3], OP.mult)
        drain("dve")
        emit("dve", "tensor_tensor", ST[:, 3:4], ST[:, 3:4], ST[:, 15:16],
             OP.subtract)
        drain("dve")
        emit("dve", "tensor_scalar", ST[:, 4:5], ST[:, 3:4], EPS, None, OP.add)
        drain("dve")
        emit("dve", "reciprocal", ST[:, 7:8], ST[:, 4:5])
        drain("dve")
        emit("dve", "tensor_tensor", ST[:, 15:16], ST[:, 3:4], ST[:, 7:8], OP.mult)
        drain("dve")
        emit("dve", "tensor_scalar", ST[:, 5:6], ST[:, 15:16], EPS, None, OP.add)
        emit("dve", "tensor_scalar", ST[:, 6:7], ST[:, 15:16], float(D), None,
             OP.mult)
        drain("dve")
        dv = emit("dve", "nop", inc=s_dve)
        av = emit("act", "activation", ST[:, 8:11], ST[:, 4:7], AFT.Sqrt,
                  waits=[(s_dve, dv)], inc=s_act)
        emit("dve", "tensor_scalar", ST[:, 11:12], ST[:, 10:11], -1.0, None,
             OP.mult, waits=[(s_act, av)])
        emit("dve", "reciprocal", ST[:, 12:14], ST[:, 8:10])
        drain("dve")
        emit("dve", "tensor_tensor", ST[:, 14:16], ST[:, 12:14], ST[:, 12:14],
             OP.mult)
        drain("dve")
        emit("dve", "tensor_tensor", ST[:, 14:16], ST[:, 14:16], ST[:, 4:6],
             OP.mult)
        drain("dve")
        emit("dve", "tensor_scalar", ST[:, 14:16], ST[:, 14:16], -0.5, 1.5,
             OP.mult, OP.add)
        drain("dve")
        emit("dve", "tensor_tensor", ST[:, 12:14], ST[:, 12:14], ST[:, 14:16],
             OP.mult)
        drain("dve")
        emit("dve", "tensor_tensor", ST[:, 14:15], ST[:, 12:13], ST[:, 13:14],
             OP.mult)
        drain("dve")

    kq_guard = [0]
    tmp_guard = [0]

    def rot_tables(par, kqsrc1, kqsrc2, tcol, rr, ri):
        ST = ST2[:, par * 16:(par + 1) * 16]
        ANG = ANG2[:, par * HD:(par + 1) * HD]
        kq = A.f("kq", 2 * HD)
        kv1 = dma("sp", skq, kq[:, 0:HD], kqsrc1,
                  waits=[(s_dve, kq_guard[0])])
        kv2 = dma("sp", skq, kq[:, HD:2 * HD], kqsrc2)
        emit("dve", "tensor_scalar", ANG, FDUP, tcol, None, OP.mult)
        drain("dve")

        def wrap(dst, shift, kk, kwait):
            emit("dve", "scalar_tensor_tensor", dst, kk, -C1, ANG, OP.mult,
                 OP.add, waits=[(skq, kv2)])
            drain("dve")
            if shift != 0.0:
                emit("dve", "tensor_scalar", dst, dst, shift, None, OP.add)
                drain("dve")
            emit("dve", "scalar_tensor_tensor", dst, kk, -C2, dst, OP.mult, OP.add)
            drain("dve")
            emit("dve", "scalar_tensor_tensor", dst, kk, -C3, dst, OP.mult, OP.add)
            drain("dve")
            emit("dve", "tensor_scalar", dst, dst, 0.0, TWO_PI, OP.max, OP.min)
            drain("dve")
            dv = emit("dve", "nop", inc=s_dve)
            return emit("act", "activation", dst, dst, AFT.Sin, bias=NEGPI,
                        scale=1.0, waits=[(s_dve, dv)], inc=s_act)

        a1 = wrap(ri, 0.0, kq[:, 0:HD], kv1)
        a2 = wrap(rr, PI_2, kq[:, HD:2 * HD], kv2)
        emit("dve", "tensor_scalar", rr, rr, ST[:, 11:12], None, OP.mult,
             waits=[(s_act, a2)])
        emit("dve", "tensor_scalar", ri, ri, ST[:, 11:12], None, OP.mult,
             waits=[(s_act, a1)])
        drain("dve")
        kq_guard[0] = emit("dve", "nop", inc=s_dve)

    def rot_headln(par, yt, sq, tmp, out_bf, rr, ri):
        HL = HL2[:, par * 6 * H:(par + 1) * 6 * H]

        def hl(i):
            return HL[:, i * H:(i + 1) * H]
        y4 = yt.rearrange("p (h j two) -> p h j two", h=H, two=2)
        t4 = tmp.rearrange("p (h j two) -> p h j two", h=H, two=2)
        s4 = sq.rearrange("p (h j two) -> p h j two", h=H, two=2)
        rr4 = rr.unsqueeze(1).broadcast_to([P, H, HD]).rearrange(
            "p h (j two) -> p h j two", two=2)
        ri4 = ri.unsqueeze(1).broadcast_to([P, H, HD]).rearrange(
            "p h (j two) -> p h j two", two=2)
        ye, yo = y4[:, :, :, 0:1], y4[:, :, :, 1:2]
        emit("dve", "tensor_tensor", t4[:, :, :, 0:1], ye, rr4[:, :, :, 0:1],
             OP.mult, waits=[(s_act, tmp_guard[0])])
        emit("dve", "tensor_tensor", t4[:, :, :, 1:2], ye, ri4[:, :, :, 0:1],
             OP.mult)
        emit("dve", "tensor_tensor", s4[:, :, :, 0:1], yo, ri4[:, :, :, 1:2],
             OP.mult)
        emit("dve", "tensor_tensor", s4[:, :, :, 1:2], yo, rr4[:, :, :, 1:2],
             OP.mult)
        drain("dve")
        emit("dve", "tensor_tensor", t4[:, :, :, 0:1], t4[:, :, :, 0:1],
             s4[:, :, :, 0:1], OP.subtract)
        emit("dve", "tensor_tensor", t4[:, :, :, 1:2], t4[:, :, :, 1:2],
             s4[:, :, :, 1:2], OP.add)
        drain("dve")
        dv = emit("dve", "nop", inc=s_dve)
        av = emit("act", "activation", sq, tmp, AFT.Square, waits=[(s_dve, dv)],
                  inc=s_act)
        tmp_guard[0] = av
        t3 = tmp.rearrange("p (h d) -> p h d", h=H)
        s3 = sq.rearrange("p (h d) -> p h d", h=H)
        emit("dve", "reduce_sum", hl(0).unsqueeze(2), t3, axis=AX.X)
        emit("dve", "reduce_sum", hl(1).unsqueeze(2), s3, axis=AX.X,
             waits=[(s_act, av)])
        drain("dve")
        emit("dve", "tensor_scalar", hl(2), hl(0), 1.0 / HD, None, OP.mult)
        emit("dve", "tensor_scalar", hl(3), hl(1), 1.0 / HD, None, OP.mult)
        drain("dve")
        emit("dve", "tensor_tensor", hl(1), hl(2), hl(2), OP.mult)
        drain("dve")
        emit("dve", "tensor_tensor", hl(5), hl(3), hl(1), OP.subtract)
        drain("dve")
        emit("dve", "tensor_scalar", hl(5), hl(5), EPS, None, OP.add)
        drain("dve")
        dv = emit("dve", "nop", inc=s_dve)
        emit("act", "nop", waits=[(s_dve, dv)])
        rsqrt_ref(hl(3), hl(5), hl(0), hl(1))
        emit("dve", "tensor_tensor", hl(4), hl(2), hl(3), OP.mult)
        drain("dve")
        emit("dve", "tensor_scalar", hl(4), hl(4), -1.0, None, OP.mult)
        drain("dve")
        dv = emit("dve", "nop", inc=s_dve)
        emit("act", "nop", waits=[(s_dve, dv)])
        rbc = hl(3).unsqueeze(2).broadcast_to([P, H, HD])
        bbc = hl(4).unsqueeze(2).broadcast_to([P, H, HD])
        emit("dve", "tensor_tensor", s3, t3, rbc, OP.mult)
        drain("dve")
        emit("dve", "tensor_tensor", t3, s3, bbc, OP.add)
        drain("dve")
        ge = A.b("lnhg_e", HD).unsqueeze(1).broadcast_to([P, H, HD])
        be = A.b("lnhb_e", HD).unsqueeze(1).broadcast_to([P, H, HD])
        emit("dve", "tensor_tensor", s3, t3, ge, OP.mult)
        drain("dve")
        ob3 = out_bf.rearrange("p (h d) -> p h d", h=H)
        emit("dve", "tensor_tensor", ob3, s3, be, OP.add)
        drain("dve")
        return emit("dve", "nop", inc=s_dve)

    def _tm_waves(src_bf, nch, te_waits):
        for k0 in range(0, nch, 4):
            kn = min(4, nch - k0)
            pb = 4 + (k0 // 4) % 2
            for kk in range(kn):
                w = list(te_waits) if (k0 == 0 and kk == 0) else []
                if kk == 0:
                    w += bgw(pb)
                tv = emit("te", "transpose", bankb(pb, P, o=kk * P),
                          src_bf[:, (k0 + kk) * P:(k0 + kk + 1) * P], IDENT,
                          waits=w, inc=(s_te if kk == kn - 1 else None))
            yield k0, kn, pb, tv

    def tm_to_fm(src_bf, dst4_fn, nch, te_waits=()):
        last = None
        for k0, kn, pb, tv in _tm_waves(src_bf, nch, te_waits):
            emit("dve", "tensor_copy", dst4_fn(k0, kn),
                 bankb(pb, kn * P).rearrange("p (k c) -> p k c", c=P),
                 waits=[(s_te, tv)])
            drain("dve")
            last = emit("dve", "nop", inc=s_dve)
            bg[pb] = (s_dve, last)
        return last


    A.alloc("wres", DCH * D // 2)
    A.alloc("akv_fm", DCH * GT // 2)
    WQ = A.b("wres", DCH * D)
    akvb = A.b("akv_fm", DCH * GT)
    akvb3 = akvb.rearrange("p (ki g) -> p ki g", g=GT)
    srcw = wq_bf[:, :].rearrange("(ki p) n -> p ki n", p=P)
    dstw = WQ.rearrange("p (ki n) -> p ki n", ki=DCH)
    wq_loaded = 0
    for kk in range(0, DCH, 4):
        kn = min(4, DCH - kk)
        wq_loaded = dma("gp", swwq, dstw[:, kk:kk + kn], srcw[:, kk:kk + kn],
                        waits=[(sptw, sptw.n), (s_dve, prep_all)]
                        if kk == 0 else [])

    for nm, u in [("xpipe", 2 * D), ("sq", 2 * D), ("xnn", D),
                  ("tile_bf", D), ("fm_bf", DCH * P), ("ptm", 2 * D),
                  ("rrg", NTG * HD), ("rig", NTG * HD)]:
        A.alloc(nm, u)
    A.alloc_high("kq", 2 * HD)
    XPP = [A.f("xpipe", 2 * D)[:, i * D:(i + 1) * D] for i in range(2)]
    SQP = [A.f("sq", 2 * D)[:, i * D:(i + 1) * D] for i in range(2)]
    _tmp = A.f("xnn", D)
    TMPP = [_tmp, _tmp]
    TBP = [A.b("tile_bf", 2 * D)[:, i * D:(i + 1) * D] for i in range(2)]
    FMP = [A.b("fm_bf", 2 * DCH * P)[:, i * DCH * P:(i + 1) * DCH * P]
           for i in range(2)]
    FM3P = [FMP[i].rearrange("p (ki c) -> p ki c", ki=DCH) for i in range(2)]
    QTMP = [A.f("ptm", 2 * D)[:, i * D:(i + 1) * D] for i in range(2)]
    KTMP = QTMP
    RRG = A.f("rrg", NTG * HD)
    RIG = A.f("rig", NTG * HD)
    NMS = sm(NTG)

    chain_end = [prep_all, prep_all]
    PREP_SPTW = sptw.n
    pctr = [0]

    def chain_start():
        par = pctr[0] % 2
        pctr[0] += 1
        return par, chain_end[par]

    def pre_tile_chain(xsrc, kq1, kq2, tcol, t, with_xnspill, aq_scl_col):
        par, gate = chain_start()
        xp, sq_s, tile_b = XPP[par], SQP[par], TBP[par]
        fm_b, fm3 = FMP[par], FM3P[par]
        ST = ST2[:, par * 16:(par + 1) * 16]
        xw = [(s_dve, gate)]
        if pctr[0] <= 2:
            xw.append((sptw, PREP_SPTW))
        xv = dma("sp", sxp[par], xp, xsrc, waits=xw)
        emit("dve", "nop", waits=[(sxp[par], xv)])
        emit("act", "nop", waits=[(sxp[par], xv), (s_dve, gate)])
        ln_stats(par, xp, sq_s)
        if with_xnspill:
            emit("dve", "tensor_scalar", tile_b, xp, ST[:, 2:3], ST[:, 12:13],
                 OP.subtract, OP.mult)
            drain("dve")
            dv = emit("dve", "nop", inc=s_dve)
            fmk = tm_to_fm(tile_b,
                           lambda k0, kn: fm3[:, k0:k0 + kn], DCH,
                           te_waits=[(s_dve, dv)])
            dstx = bass.AP(xn_sp.tensor, t * P,
                           [[TLOC, P], [P * TLOC, DCH], [1, P]])
            wv = dma("sp", stw, dstx, fm3, waits=[(s_dve, fmk), (stw, stw.n)])
            emit("dve", "nop", waits=[(stw, wv)])
        emit("dve", "tensor_scalar", tile_b, xp, ST[:, 2:3],
             ST[:, aq_scl_col:aq_scl_col + 1], OP.subtract, OP.mult)
        drain("dve")
        emit("dve", "tensor_copy", NMS[:, t:t + 1], ST[:, 11:12])
        drain("dve")
        dv = emit("dve", "nop", inc=s_dve)
        last = tm_to_fm(tile_b,
                        lambda k0, kn, _t=t: akvb3[:, k0:k0 + kn,
                                                   _t * P:(_t + 1) * P],
                        DCH, te_waits=[(s_dve, dv)])
        rot_tables_pre(par, kq1, kq2, tcol, t)
        chain_end[par] = emit("dve", "nop", inc=s_dve)

    def rot_tables_pre(par, kqsrc1, kqsrc2, tcol, t):
        ST = ST2[:, par * 16:(par + 1) * 16]
        ANG = ANG2[:, par * HD:(par + 1) * HD]
        rr = RRG[:, t * HD:(t + 1) * HD]
        ri = RIG[:, t * HD:(t + 1) * HD]
        kq = A.f("kq", 2 * HD)
        kv1 = dma("sp", skq, kq[:, 0:HD], kqsrc1, waits=[(s_dve, kq_guard[0])])
        kv2 = dma("sp", skq, kq[:, HD:2 * HD], kqsrc2)
        emit("dve", "tensor_scalar", ANG, FDUP, tcol, None, OP.mult)
        drain("dve")

        def wrap(dst, shift, kk):
            emit("dve", "scalar_tensor_tensor", dst, kk, -C1, ANG, OP.mult,
                 OP.add, waits=[(skq, kv2)])
            drain("dve")
            if shift != 0.0:
                emit("dve", "tensor_scalar", dst, dst, shift, None, OP.add)
                drain("dve")
            emit("dve", "scalar_tensor_tensor", dst, kk, -C2, dst, OP.mult, OP.add)
            drain("dve")
            emit("dve", "scalar_tensor_tensor", dst, kk, -C3, dst, OP.mult, OP.add)
            drain("dve")
            emit("dve", "tensor_scalar", dst, dst, 0.0 - PI, PI, OP.max, OP.min,
                 waits=[]) if False else None
            emit("dve", "tensor_scalar", dst, dst, 0.0, TWO_PI, OP.max, OP.min)
            drain("dve")

        wrap(ri, 0.0, kq[:, 0:HD])
        wrap(rr, PI_2, kq[:, HD:2 * HD])
        kq_guard[0] = emit("dve", "nop", inc=s_dve)

    def group_tables(nt):
        dv = emit("dve", "nop", inc=s_dve)
        a1 = emit("act", "activation", RIG[:, 0:nt * HD], RIG[:, 0:nt * HD],
                  AFT.Sin, bias=NEGPI, scale=1.0, waits=[(s_dve, dv)],
                  inc=s_act)
        a2 = emit("act", "activation", RRG[:, 0:nt * HD], RRG[:, 0:nt * HD],
                  AFT.Sin, bias=NEGPI, scale=1.0, inc=s_act)
        for t in range(nt):
            emit("dve", "tensor_scalar", RRG[:, t * HD:(t + 1) * HD],
                 RRG[:, t * HD:(t + 1) * HD], NMS[:, t:t + 1], None, OP.mult,
                 waits=[(s_act, a2)] if t == 0 else [])
            emit("dve", "tensor_scalar", RIG[:, t * HD:(t + 1) * HD],
                 RIG[:, t * HD:(t + 1) * HD], NMS[:, t:t + 1], None, OP.mult)
        drain("dve")
        return emit("dve", "nop", inc=s_dve)

    def proj_tiles(t, wbuf, woff, cols_off, brow_off, banksets=None):
        emit("te", "nop", waits=[]) if False else None
        for ci, (co, cw) in enumerate(COGS):
            w = bgw(ci) if True else []
            emit("te", "matmul", bank(ci, cw), ONESBR,
                 BROWS[0:1, brow_off + co: brow_off + co + cw],
                 start=True, stop=False, waits=w)
        for ki in range(DCH):
            emit("te", "ldweights",
                 akvb[:, ki * GT + t * P: ki * GT + (t + 1) * P])
            for ci, (co, cw) in enumerate(COGS):
                tv = emit("te", "matmul", bank(ci, cw),
                          akvb[:, ki * GT + t * P: ki * GT + (t + 1) * P],
                          wbuf[:, ki * cols_off + woff + co:
                               ki * cols_off + woff + co + cw],
                          start=False, stop=(ki == DCH - 1),
                          inc=(s_te if (ki == DCH - 1 and ci == NCG - 1)
                               else None))
        return tv

    def evac_act(tv_proj, dst, gate=None):
        for ci, (co, cw) in enumerate(COGS):
            w = [(s_te, tv_proj)] if ci == 0 else []
            if gate is not None and ci == 0:
                w.append(gate)
            av = emit("act", "activation", dst[:, co:co + cw], bank(ci, cw),
                      AFT.Identity, waits=w,
                      inc=(s_act if ci == NCG - 1 else None))
        for ci in range(NCG):
            bg[ci] = (s_act, av)
        return av

    def q_main_chain(t):
        par, gate = chain_start()
        sq_s, tmp_s = SQP[par], TMPP[par]
        tile_b, fm_b, fm3, qtm = TBP[par], FMP[par], FM3P[par], QTMP[par]
        emit("te", "nop", waits=[(swwq, wq_loaded)] if t == 0 else [])
        tvp = proj_tiles(t, WQ, 0, D, 0)
        av = evac_act(tvp, qtm, gate=(s_dve, gate))
        emit("dve", "nop", waits=[(s_act, av)])
        rot_headln(par, qtm, sq_s, tmp_s, tile_b,
                   RRG[:, t * HD:(t + 1) * HD], RIG[:, t * HD:(t + 1) * HD])
        dv = emit("dve", "nop", inc=s_dve)
        fmk = tm_to_fm(tile_b, lambda k0, kn: fm3[:, k0:k0 + kn], DCH,
                       te_waits=[(s_dve, dv)])
        dstq = bass.AP(qn_sp.tensor, t * P, [[TLOC, P], [P * TLOC, DCH], [1, P]])
        qv = dma("sp", stw, dstq, fm3, waits=[(s_dve, fmk), (stw, stw.n)])
        chain_end[par] = emit("dve", "nop", waits=[(stw, qv)], inc=s_dve)

    for t in range(NTO):
        pre_tile_chain(xow_d[t * P:(t + 1) * P, :],
                       kqo_d[:, t * HD:(t + 1) * HD],
                       kqo2_d[:, t * HD:(t + 1) * HD],
                       TCO[:, t:t + 1], t, True, 14)
    group_tables(NTO)
    for t in range(NTO):
        q_main_chain(t)

    q1_te_done = s_te.n
    prep_store_n = 2 * DCH * 16
    WKVH = A.b("wres", DCH * D)
    dstwh = WKVH.rearrange("p (ki n) -> p ki n", ki=DCH)
    n_ptr = NGRP + 1
    ptr_slice(0, n_ptr)

    def load_wkv_half(half, wguards):
        srcl = wkv_bf[:, half * D:(half + 1) * D].rearrange(
            "(ki p) n -> p ki n", p=P)
        v = 0
        for kk in range(DCH):
            w = list(wguards) if kk == 0 else []
            v = dma("gp", swkv, dstwh[:, kk:kk + 1], srcl[:, kk:kk + 1], waits=w)
        return v

    def k_sweep_tile(g, t, tv_proj):
        par, gate = chain_start()
        sq_s, tmp_s = SQP[par], TMPP[par]
        tile_b, fm_b, fm3, ktm = TBP[par], FMP[par], FM3P[par], KTMP[par]
        av = evac_act(tv_proj, ktm, gate=(s_dve, gate))
        emit("dve", "nop", waits=[(s_act, av)])
        emit("act", "nop", waits=[(s_dve, gate)])
        rot_headln(par, ktm, sq_s, tmp_s, tile_b,
                   RRG[:, t * HD:(t + 1) * HD], RIG[:, t * HD:(t + 1) * HD])
        dv = emit("dve", "nop", inc=s_dve)
        fmk = tm_to_fm(tile_b, lambda k0, kn: fm3[:, k0:k0 + kn], DCH,
                       te_waits=[(s_dve, dv)])
        dstk = bass.AP(st_kn.tensor, g * D * GT + t * P,
                       [[GT, P], [P * GT, DCH], [1, P]])
        kv = dma("sp", stw, dstk, fm3, waits=[(s_dve, fmk), (stw, stw.n)])
        chain_end[par] = emit("dve", "nop", waits=[(stw, kv)], inc=s_dve)

    def v_sweep_tile(g, t, tv_proj):
        par, gate = chain_start()
        tile_b = TBP[par]
        gt = g * NTG + t
        for ci, (co, cw) in enumerate(COGS):
            w = [(s_te, tv_proj)] if ci == 0 else []
            if ci == 0:
                w.append((s_dve, gate))
            av = emit("act", "activation", tile_b[:, co:co + cw], bank(ci, cw),
                      AFT.Identity, waits=w,
                      inc=(s_act if ci == NCG - 1 else None))
        for ci in range(NCG):
            bg[ci] = (s_act, av)
        dstv = bass.AP(st_v.tensor, gt * P * HD,
                       [[HD, P], [NKC * P * HD, H], [1, HD]])
        t3v = tile_b.rearrange("p (h c) -> p h c", h=H)
        vv = dma("sp", stw, dstv, t3v, waits=[(s_act, av), (stw, stw.n)])
        chain_end[par] = emit("dve", "nop", waits=[(stw, vv)], inc=s_dve)

    for g in range(NGRP):
        for t in range(NTG):
            gt = g * NTG + t
            pre_tile_chain(x_d[gt * P:(gt + 1) * P, :],
                           kq_d[:, gt * HD:(gt + 1) * HD],
                           kq2_d[:, gt * HD:(gt + 1) * HD],
                           TCOLS[:, gt:gt + 1], t, False, 14)
        tables_done = group_tables(NTG)
        kguard = ([(s_te, s_te.n)] if g > 0
                  else [(s_te, q1_te_done), (sptw, prep_store_n)])
        wk = load_wkv_half(0, kguard)
        emit("te", "nop", waits=[(swkv, wk), (s_dve, tables_done)])
        for t in range(NTG):
            tvp = proj_tiles(t, WKVH, 0, D, D)
            k_sweep_tile(g, t, tvp)
        wk = load_wkv_half(1, [(s_te, s_te.n)])
        emit("te", "nop", waits=[(swkv, wk)])
        for t in range(NTG):
            tvp = proj_tiles(t, WKVH, 0, D, 2 * D)
            v_sweep_tile(g, t, tvp)
        ptr_slice(g + 1, n_ptr)

    A.release("wres", "akv_fm", "ptm", "xpipe", "sq", "xnn",
              "tile_bf", "fm_bf", "rrg", "rig", "kq",
              "lnhg_e", "lnhb_e", "brows")
    kv_all_stored = stw.n

    A.alloc("xn_fm", DCH * TLOC // 2)
    XNF = A.b("xn_fm", DCH * TLOC)
    srcx = bass.AP(xn_sp.tensor, 0, [[TLOC, P], [P * TLOC, DCH], [1, TLOC]])
    ph1_te_done = s_te.n
    xn_rl = dma("sp", srd, XNF.rearrange("p (ki n) -> p ki n", ki=DCH), srcx,
                waits=[(stw, stw.n), (s_te, ph1_te_done), (s_dve, s_dve.n),
                       (sptw, sptw.n)])
    A.alloc("m_fm", ICH * TLOC // 2)
    MFM = A.b("m_fm", ICH * TLOC)
    A.alloc("wgu_t", 2 * DCH * P)
    GBUF = A.b("wgu_t", 4 * DCH * P)
    A.alloc("silu_t", TLOC)
    SILU = A.f("silu_t", TLOC)

    gu_guard = [None, None]
    for j in range(ICH):
        lv = 0
        for s_i, blk in enumerate((j, ICH + j)):
            dbuf = GBUF[:, ((j % 2) * 2 + s_i) * DCH * P:
                        ((j % 2) * 2 + s_i + 1) * DCH * P]
            srcb = bass.AP(wguT.tensor, blk * D * P,
                           [[P, P], [P * P, DCH], [1, P]])
            w = []
            if s_i == 0:
                w = [(srd, srd.n)]
            if j < 2:
                w += [(sptw, sptw.n), (s_te, ph1_te_done)]
            elif s_i == 0 and gu_guard[j % 2] is not None:
                w += [gu_guard[j % 2]]
            lv = dma("sp", srd, dbuf.rearrange("p (ki c) -> p ki c", ki=DCH),
                     srcb, waits=w)
        emit("te", "nop", waits=[(srd, lv), (srd, xn_rl)])
        for s_i in range(2):
            dbuf = GBUF[:, ((j % 2) * 2 + s_i) * DCH * P:
                        ((j % 2) * 2 + s_i + 1) * DCH * P]
            for ki in range(DCH):
                emit("te", "ldweights", dbuf[:, ki * P:(ki + 1) * P])
                for bi, (to, tw) in enumerate(TLB):
                    bidx = s_i * 2 + bi
                    w = bgw(bidx) if ki == 0 else []
                    tv = emit("te", "matmul", bank(bidx, tw),
                              dbuf[:, ki * P:(ki + 1) * P],
                              XNF[:, ki * TLOC + to: ki * TLOC + to + tw],
                              start=(ki == 0), stop=(ki == DCH - 1), waits=w,
                              inc=(s_te if (s_i == 1 and ki == DCH - 1
                                            and bi == NTB - 1) else None))
        gu_guard[j % 2] = (s_te, tv)
        for bi, (to, tw) in enumerate(TLB):
            av = emit("act", "activation", SILU[:, to:to + tw], bank(bi, tw),
                      AFT.Silu, waits=[(s_te, tv)] if bi == 0 else [],
                      inc=(s_act if bi == NTB - 1 else None))
        for bi, (to, tw) in enumerate(TLB):
            emit("dve", "tensor_tensor",
                 MFM[:, j * TLOC + to: j * TLOC + to + tw],
                 SILU[:, to:to + tw], bank(2 + bi, tw), OP.mult,
                 waits=[(s_act, av)] if bi == 0 else [])
        drain("dve")
        mk = emit("dve", "nop", inc=s_dve)
        for bidx in range(2 * NTB + 2):
            if bidx < 8:
                bg[bidx] = (s_dve, mk)
    A.release("wgu_t", "silu_t", "xn_fm")
    up_te_done = s_te.n
    up_dve_done = s_dve.n

    A.alloc("wdn_t", 2 * D)
    WDN = A.b("wdn_t", 2 * D)
    A.alloc("ffn_t", D)
    FFT = A.f("ffn_t", D)
    dn_guard = [None, None]
    out_stash = 0
    for tp in range(0, NTO, 2):
        ntp = min(2, NTO - tp)
        for i in range(ICH):
            dbuf = WDN[:, (i % 2) * D:(i % 2) * D + D]
            w = [(srd, srd.n)]
            if dn_guard[i % 2] is not None:
                w.append(dn_guard[i % 2])
            if tp == 0 and i < 2:
                w += [(sptw, sptw.n), (s_te, up_te_done), (s_dve, up_dve_done)]
            lv = dma("sp", srd, dbuf, wdnT[i * P:(i + 1) * P, :], waits=w)
            emit("te", "nop", waits=[(srd, lv)])
            for tti in range(ntp):
                tt = tp + tti
                emit("te", "ldweights",
                     MFM[:, i * TLOC + tt * P: i * TLOC + (tt + 1) * P])
                for ci, (co, cw) in enumerate(COGS):
                    bidx = tti * NCG + ci
                    w2 = bgw(bidx) if i == 0 else []
                    tvd = emit("te", "matmul", bank(bidx, cw),
                               MFM[:, i * TLOC + tt * P: i * TLOC + (tt + 1) * P],
                               WDN[:, (i % 2) * D + co:(i % 2) * D + co + cw],
                               start=(i == 0), stop=(i == ICH - 1), waits=w2,
                               inc=(s_te if (tti == ntp - 1 and ci == NCG - 1)
                                    else None))
            dn_guard[i % 2] = (s_te, tvd)
        tv = dn_guard[(ICH - 1) % 2][1]
        for tti in range(ntp):
            tt = tp + tti
            for ci, (co, cw) in enumerate(COGS):
                bidx = tti * NCG + ci
                emit("dve", "scalar_tensor_tensor", FFT[:, co:co + cw],
                     bank(bidx, cw), 1.0, A.b("bo_b", D)[:, co:co + cw],
                     OP.mult, OP.add,
                     waits=([(s_te, tv), (stw, out_stash)]
                            if (tti == 0 and ci == 0) else []))
            drain("dve")
            dv = emit("dve", "nop", inc=s_dve)
            out_stash = dma("sp", stw, ffn_sp[tt * P:(tt + 1) * P, :], FFT,
                            waits=[(s_dve, dv), (stw, stw.n)])
            mk = emit("dve", "nop", waits=[(stw, out_stash)], inc=s_dve)
            for ci in range(NCG):
                bg[tti * NCG + ci] = (s_dve, mk)
    A.release("wdn_t", "ffn_t", "m_fm")

    A.alloc("qn_fm", DCH * TLOC // 2)
    QNF = A.b("qn_fm", DCH * TLOC)
    srcq = bass.AP(qn_sp.tensor, 0, [[TLOC, P], [P * TLOC, DCH], [1, TLOC]])
    ph2_te_done = s_te.n
    qn_rl = dma("sp", srd, QNF.rearrange("p (ki n) -> p ki n", ki=DCH), srcq,
                waits=[(s_te, ph2_te_done), (s_dve, s_dve.n)])
    A.alloc("kn_h", NKC * P)
    KNB = A.b("kn_h", 2 * NKC * P)
    A.alloc("v_h", NKC * HD)
    VB = A.b("v_h", 2 * NKC * HD)
    A.alloc("p_buf", NKC * QT // 2)
    PBUF = A.b("p_buf", NKC * QT)
    A.alloc("attn_fm", DCH * TLOC // 2)
    ATTN = A.b("attn_fm", DCH * TLOC)
    A.alloc("lrec", QT)
    LREC = A.f("lrec", QT)

    kn_loads = [0] * H
    head_done = [None] * (H + 2)

    def load_head(h):
        par = h % 2
        w = [(stw, kv_all_stored), (s_te, ph2_te_done), (srd, srd.n)]
        if h >= 2 and head_done[h - 2] is not None:
            w = [head_done[h - 2], (srd, srd.n)]
        last = 0
        for g in range(NGRP):
            last = dma("sp", srd,
                       KNB[:, par * NKC * P + g * GT:par * NKC * P + (g + 1) * GT],
                       st_kn[g * D + h * P: g * D + (h + 1) * P, :],
                       waits=w if g == 0 else [])
        srcv = bass.AP(st_v.tensor, h * NKC * P * HD,
                       [[HD, P], [P * HD, NKC], [1, HD]])
        last = dma("sp", srd,
                   VB[:, par * NKC * HD:(par + 1) * NKC * HD]
                   .rearrange("p (kc c) -> p kc c", kc=NKC), srcv)
        kn_loads[h] = last

    load_head(0)
    if H > 1:
        load_head(1)
    emit("act", "nop", waits=[(s_dve, negb_done)])
    for h in range(H):
        par = h % 2
        kn_h = KNB[:, par * NKC * P:(par + 1) * NKC * P]
        v_h = VB[:, par * NKC * HD:(par + 1) * NKC * HD]
        emit("te", "nop", waits=[(srd, kn_loads[h]), (srd, qn_rl),
                                 (s_dve, negb_done)])
        for qi in range(NQT):
            q0 = qi * QT
            rhs_q = QNF[:, h * TLOC + q0: h * TLOC + q0 + QT]
            exp_marks = []
            QKB = (0, 1, 4, 5)
            for c0 in range(0, NKC, 2):
                b_i = QKB[(c0 // 2) % 4]
                npair = min(2, NKC - c0)
                for cc in range(npair):
                    w = bgw(b_i) if cc == 0 else []
                    emit("te", "ldweights",
                         kn_h[:, (c0 + cc) * P:(c0 + cc + 1) * P])
                    tv = emit("te", "matmul", bank(b_i, QT, o=cc * QT),
                              kn_h[:, (c0 + cc) * P:(c0 + cc + 1) * P], rhs_q,
                              start=True, stop=True, waits=w,
                              inc=(s_te if cc == npair - 1 else None))
                av = emit("act", "activation",
                          PBUF[:, c0 * QT:(c0 + npair) * QT],
                          bank(b_i, npair * QT), AFT.Exp, bias=NEGB, scale=1.0,
                          waits=[(s_te, tv)], inc=s_act)
                exp_marks.append(av)
                bg[b_i] = (s_act, av)
            for c in range(NKC):
                w = [(s_act, exp_marks[c // 2])]
                if c == 0:
                    w += bgw(2)
                emit("te", "ldweights", v_h[:, c * HD:(c + 1) * HD], waits=w)
                av_tv = emit("te", "matmul", bank(2, QT),
                             v_h[:, c * HD:(c + 1) * HD],
                             PBUF[:, c * QT:(c + 1) * QT],
                             start=(c == 0), stop=(c == NKC - 1),
                             inc=(s_te if c == NKC - 1 else None))
            emit("te", "ldweights", ONES, waits=bgw(3))
            for c in range(NKC):
                l_tv = emit("te", "matmul", bank(3, QT)[0:1, :], ONES,
                            PBUF[:, c * QT:(c + 1) * QT],
                            start=(c == 0), stop=(c == NKC - 1),
                            inc=(s_te if c == NKC - 1 else None))
            emit("dve", "tensor_copy", LROW[0:1, :], bank(3, QT)[0:1, :],
                 waits=[(s_te, l_tv)])
            drain("dve")
            emit("dve", "reciprocal", LROW[0:1, :], LROW[0:1, :])
            drain("dve")
            dv = emit("dve", "nop", inc=s_dve)
            bg[3] = (s_dve, dv)
            tv = emit("te", "matmul", bank(6, QT), ONESF[0:1, :],
                      LROW[0:1, :], start=True, stop=True,
                      waits=[(s_dve, dv)] + bgw(6), inc=s_te)
            emit("dve", "tensor_copy", LREC, bank(6, QT), waits=[(s_te, tv)])
            drain("dve")
            lb = emit("dve", "nop", inc=s_dve)
            bg[6] = (s_dve, lb)
            emit("dve", "tensor_tensor",
                 ATTN[:, h * TLOC + q0: h * TLOC + q0 + QT], bank(2, QT), LREC,
                 OP.mult, waits=[(s_te, av_tv)])
            drain("dve")
            bg[2] = (s_dve, emit("dve", "nop", inc=s_dve))
        head_done[h] = (s_te, l_tv)
        if h + 2 < H:
            load_head(h + 2)
    attn_done = emit("dve", "nop", inc=s_dve)
    A.release("kn_h", "v_h", "p_buf", "qn_fm", "lrec")

    A.alloc("wo_r", DCH * D // 2)
    WOB = A.b("wo_r", DCH * D)
    srcw2 = wo_d[:, :].rearrange("(ki p) n -> p ki n", p=P)
    dstw2 = WOB.rearrange("p (ki n) -> p ki n", ki=DCH)
    wo_loaded = 0
    for kk in range(0, DCH, 4):
        kn = min(4, DCH - kk)
        wo_loaded = dma("gp", swwo, dstw2[:, kk:kk + kn], srcw2[:, kk:kk + kn],
                        waits=[(s_te, head_done[H - 1][1] if head_done[H - 1]
                                else 0), (s_dve, attn_done)] if kk == 0 else [])
    A.alloc("ffn_in", D)
    FIN = A.f("ffn_in", D)
    A.alloc("x_in", D)
    XIN = A.f("x_in", D)
    A.alloc("out_t", D)
    OUTT = A.f("out_t", D)
    emit("te", "nop", waits=[(swwo, wo_loaded), (s_dve, attn_done)])
    out_guard = 0
    for tt in range(NTO):
        fv = dma("sp", srd, FIN, ffn_sp[tt * P:(tt + 1) * P, :],
                 waits=[(s_dve, out_guard if tt > 0 else attn_done)])
        xv = dma("sp", srd, XIN, xow_d[tt * P:(tt + 1) * P, :])
        for ki in range(DCH):
            emit("te", "ldweights",
                 ATTN[:, ki * TLOC + tt * P: ki * TLOC + (tt + 1) * P])
            for ci, (co, cw) in enumerate(COGS):
                w = bgw(ci) if ki == 0 else []
                tv = emit("te", "matmul", bank(ci, cw),
                          ATTN[:, ki * TLOC + tt * P: ki * TLOC + (tt + 1) * P],
                          WOB[:, ki * D + co: ki * D + co + cw],
                          start=(ki == 0), stop=(ki == DCH - 1), waits=w,
                          inc=(s_te if (ki == DCH - 1 and ci == NCG - 1)
                               else None))
        for ci, (co, cw) in enumerate(COGS):
            w = [(s_te, tv), (srd, xv)] if ci == 0 else []
            emit("dve", "tensor_tensor", OUTT[:, co:co + cw], bank(ci, cw),
                 FIN[:, co:co + cw], OP.add, waits=w)
        drain("dve")
        emit("dve", "tensor_tensor", OUTT, OUTT, XIN, OP.add,
             waits=[(srd, xv)])
        drain("dve")
        dv = emit("dve", "nop", inc=s_dve)
        for ci in range(NCG):
            bg[ci] = (s_dve, dv)
        ov = dma("sp", stw, y_d[tt * P:(tt + 1) * P, :], OUTT,
                 waits=[(s_dve, dv), (stw, stw.n)])
        out_guard = emit("dve", "nop", waits=[(stw, ov)], inc=s_dve)

    if dbg:
        dbgs = [("d_xn", xn_sp[:, :]), ("d_qn", qn_sp[:, :]),
                ("d_kn", st_kn[0:D, :]), ("d_v", st_v[0:NKC * P, :]),
                ("d_ffn", ffn_sp[:, :]), ("d_wgu", wguT[0:D, :]),
                ("d_wdn", wdnT[0:P, :])]
        for nm, srcd in dbgs:
            od = nc.dram_tensor(nm, list(srcd.shape),
                                srcd.dtype, kind="ExternalOutput").ap()
            dma("gp", swgp, od[:, :], srcd,
                waits=[(stw, stw.n), (sptw, sptw.n), (srd, srd.n)])
        emit("gp", "nop", waits=[(swgp, swgp.n)])
    emit("sp", "nop", waits=[(stw, stw.n)])
    emit("te", "nop", waits=[(stw, stw.n)])
    emit("gp", "nop", waits=[(stw, stw.n)])
    emit("act", "nop", waits=[(stw, stw.n)])

    def render(q):
        def body(eng):
            for meth, args, kw, waits, incs in Q[q]:
                for s, v in waits:
                    eng.wait_ge(s, v)
                if meth == "nop":
                    if incs:
                        ins = eng.nop()
                        for s, k in incs:
                            ins.then_inc(s, k)
                    continue
                ins = getattr(eng, meth)(*args, **kw)
                if ins is not None:
                    for s, k in incs:
                        ins.then_inc(s, k)
        return body

    with nc.Block() as blk:
        blk.tensor(render("te"))
        blk.scalar(render("act"))
        blk.vector(render("dve"))
        blk.sync(render("sp"))
        blk.gpsimd(render("gp"))
    ctx.close()
    return nc


def _consts(cfg: Cfg):
    f = ref_freqs(cfg.HD)
    fdup = np.repeat(f, 2)[None, :].repeat(P, 0).astype(np.float32)
    tcols = (np.arange(cfg.T, dtype=np.float32).reshape(cfg.T // P, P).T)
    return (np.ascontiguousarray(fdup), np.ascontiguousarray(tcols),
            np.eye(P).astype(ml_dtypes.bfloat16),
            np.ones((P, 1), ml_dtypes.bfloat16))


def _kq_tables(cfg: Cfg, t0, t1):
    f = ref_freqs(cfg.HD)
    fdup = np.repeat(f, 2).astype(np.float32)
    t_idx = np.arange(t0, t1, dtype=np.float32)
    ang32 = (t_idx[:, None] * fdup[None, :]).astype(np.float32)
    k1 = np.floor(ang32.astype(np.float64) / (2 * np.pi))
    k2 = np.floor((ang32.astype(np.float64) + np.float64(np.float32(np.pi / 2)))
                  / (2 * np.pi))
    nt = (t1 - t0) // P

    def lay(k):
        k = k.astype(np.float32).reshape(nt, P, cfg.HD)
        return np.ascontiguousarray(k.transpose(1, 0, 2).reshape(P, nt * cfg.HD))

    return lay(k1), lay(k2)


def _bc(v):
    return np.ascontiguousarray(
        np.broadcast_to(np.asarray(v, np.float32).reshape(1, -1), (P, len(np.asarray(v).reshape(-1)))))


def run(cfg: Cfg, inputs: dict, runner=None, dbg=False):
    nc = build(cfg, dbg=dbg)
    fdup, tcols, ident, ones = _consts(cfg)
    x = np.ascontiguousarray(np.asarray(inputs["L0_attn_x"], np.float32)
                             .reshape(cfg.T, cfg.D))
    base = {
        "x_full": x,
        "Wq": np.ascontiguousarray(np.asarray(inputs["L0_attn_Wq"], np.float32)),
        "Wkv": np.ascontiguousarray(np.asarray(inputs["L0_attn_Wkv"], np.float32)),
        "Wo": np.ascontiguousarray(np.asarray(inputs["L0_attn_Wo"], np.float32)),
        "Wgu": np.ascontiguousarray(np.asarray(inputs["L0_ffn_w_gate_up"],
                                               np.float32)),
        "Wdn": np.ascontiguousarray(np.asarray(inputs["L0_ffn_w_down"],
                                               np.float32)),
        "ln_q_g": _bc(inputs["L0_attn_ln_q_g"]),
        "ln_q_b": _bc(inputs["L0_attn_ln_q_b"]),
        "ln_kv_g": _bc(inputs["L0_attn_ln_kv_g"]),
        "ln_kv_b": _bc(inputs["L0_attn_ln_kv_b"]),
        "ln_h_g": _bc(inputs["L0_attn_ln_h_g"]),
        "ln_h_b": _bc(inputs["L0_attn_ln_h_b"]),
        "bq": _bc(inputs["L0_attn_bq"]),
        "bkv": _bc(inputs["L0_attn_bkv"]),
        "bo": _bc(inputs["L0_attn_bo"]),
        "ident_c": ident, "ones_c": ones, "fdup_c": fdup, "tcols_c": tcols,
        "onesf_c": np.ones((1, P), np.float32),
        "onesbr_c": np.ones((1, P), ml_dtypes.bfloat16),
    }
    kq1, kq2 = _kq_tables(cfg, 0, cfg.T)
    base["kq_c"], base["kq2_c"] = kq1, kq2

    def _col(v):
        return np.ascontiguousarray(
            np.asarray(v, np.float32).reshape(cfg.DCH, P).T)
    base["gqc_c"] = _col(inputs["L0_attn_ln_q_g"])
    base["bqc_c"] = _col(inputs["L0_attn_ln_q_b"])
    base["gkc_c"] = _col(inputs["L0_attn_ln_kv_g"])
    base["bkc_c"] = _col(inputs["L0_attn_ln_kv_b"])
    base = {k: np.ascontiguousarray(v) for k, v in base.items()}
    in_maps = []
    tl = cfg.TLOC
    for c in range(NCORES):
        m = dict(base)
        m["x_own"] = np.ascontiguousarray(x[c * tl:(c + 1) * tl, :])
        m["tcown_c"] = np.ascontiguousarray(
            np.arange(c * tl, (c + 1) * tl, dtype=np.float32)
            .reshape(tl // P, P).T)
        m["kqo_c"], m["kqo2_c"] = _kq_tables(cfg, c * tl, (c + 1) * tl)
        in_maps.append(m)
    if runner is None:
        res = run_bass_kernel_spmd(nc, in_maps, core_ids=list(range(NCORES)))
        outs = [res.results[c]["y"] for c in range(NCORES)]
        if dbg:
            y = np.concatenate([np.asarray(o, np.float32) for o in outs], axis=0)
            return y.reshape(1, cfg.T, cfg.D), res.results
    else:
        outs = runner(nc, in_maps)
    y = np.concatenate([np.asarray(o, np.float32) for o in outs], axis=0)
    return y.reshape(1, cfg.T, cfg.D)


def kernel(**inputs):
    return run(FULL, inputs)
